# version 14
# baseline (speedup 1.0000x reference)
import contextlib
import numpy as np

import concourse.bass as bass
import concourse.mybir as mybir
import concourse.library_config as LC
from concourse.bass_isa import ReduceOp
from concourse.bass_utils import run_bass_kernel_spmd

F32 = mybir.dt.float32
BF16 = mybir.dt.bfloat16
AF = mybir.ActivationFunctionType
ALU = mybir.AluOpType
AXX = mybir.AxisListType.X

P = 128
T = 2048
D = 1024
NCH = 8
NTC = 16
TT = 512
NTT = 4
RND_C = 12582912.0
EPS = 1e-5

NH = 4
DK = 128
DV = 256
CH = 32
NCK = 64

HEADS = 16
DH = 64
SEQ = 4096

PHASES = 4
PHASE_BARR = []
P4STOP = None


class Sem:
    def __init__(self, h):
        self.h = h
        self.n = 0

    def bump(self, k=1):
        self.n += k
        return self.n


class Builder:
    ENGS = ("tensor", "vector", "scalar", "gpsimd", "sync")

    def __init__(self, nc, es):
        self.nc = nc
        self.es = es
        self.q = {e: [] for e in self.ENGS}
        self.last_wait = {}
        self.sems = {}

    def sem(self, name):
        if name not in self.sems:
            self.sems[name] = Sem(self.es.enter_context(self.nc.semaphore(name)))
        return self.sems[name]

    def wait(self, eng, sem, val):
        if val is None or val <= 0:
            return
        key = (eng, sem.h.name)
        if self.last_wait.get(key, -1) >= val:
            return
        self.last_wait[key] = val
        self.q[eng].append(lambda e, s=sem, v=val: e.wait_ge(s.h, v))

    def op(self, eng, fn, inc=None, k=1, serial=True):
        ser = getattr(self, "serial", {}).get(eng)
        if serial and ser is not None:
            self.wait(eng, ser, ser.n)
        if inc is None and ser is not None:
            inc = ser
        if inc is not None:
            v = inc.bump(k)
            def run(e, fn=fn, s=inc, kk=k):
                fn(e).then_inc(s.h, kk)
            self.q[eng].append(run)
            return v
        self.q[eng].append(lambda e, fn=fn: fn(e))
        return None

    def call(self, eng, meth, *args, inc=None, k=1, serial=True, **kwargs):
        def fn(e, meth=meth, args=args, kwargs=kwargs):
            return getattr(e, meth)(*args, **kwargs)
        return self.op(eng, fn, inc=inc, k=k, serial=serial)

    def dma(self, eng, dst, src, sem):
        return self.op(eng, lambda e, d=dst, s=src: e.dma_start(out=d, in_=s),
                       inc=sem, k=16, serial=False)

    def barrier(self):
        bs = self.sem("BARR")
        v = bs.n + len(self.ENGS)
        for e in self.ENGS:
            ser = getattr(self, "barrier_serial", {}).get(e)
            if ser is not None:
                self.wait(e, ser, ser.n)
            self.q[e].append(lambda eng, s=bs: eng.nop().then_inc(s.h, 1))
            self.q[e].append(lambda eng, s=bs, vv=v: eng.wait_ge(s.h, vv))
            self.last_wait[(e, bs.h.name)] = v
        bs.n = v

    def replay(self, block):
        for name in self.ENGS:
            ops = self.q[name]
            getattr(block, name)(lambda eng, ops=ops: [f(eng) for f in ops])


def _ct(a):
    return np.ascontiguousarray(np.asarray(a))


def _pe_const():
    pos = np.arange(SEQ, dtype=np.float64)[:, None]
    div = np.exp(np.arange(0, D, 2, dtype=np.float64) * (-np.log(10000.0) / D))
    pe = np.zeros((SEQ, D), np.float32)
    pe[:, 0::2] = np.sin(pos * div).astype(np.float32)
    pe[:, 1::2] = np.cos(pos * div).astype(np.float32)
    return pe


def prep_inputs(inp):
    x = np.asarray(inp["L0_attn_hidden_states"])

    def fm(mat):
        return _ct(mat.T)

    wx2 = np.asarray(inp["L0_attn_Wx2"]).reshape(D, 5, 32)
    xb = np.asarray(inp["L0_attn_x_bias"])
    wx2T = np.stack([np.concatenate([_ct(wx2[:, g, :].T), xb[g][None, :]], axis=0)
                     for g in range(5)], axis=0)

    common = {
        "wx0T": _ct(np.asarray(inp["L0_attn_Wx0"]).T),
        "wx2T": _ct(wx2T),
        "rWT": _ct(np.asarray(inp["L0_attn_rW"]).T),
        "wAT": _ct(np.asarray(inp["L0_attn_wA"]).T),
        "wBT": _ct(np.asarray(inp["L0_attn_wB"]).T),
        "kWT": _ct(np.asarray(inp["L0_attn_kW"]).T),
        "vWT": _ct(np.asarray(inp["L0_attn_vW"]).T),
        "gAT": _ct(np.asarray(inp["L0_attn_gA"]).T),
        "gBT": _ct(np.asarray(inp["L0_attn_gB"]).T),
        "oWT": _ct(np.asarray(inp["L0_attn_oW"]).T),
        "mu0c": _ct(np.asarray(inp["L0_attn_mu0"]).reshape(NCH, P).T),
        "wbc": _ct(np.asarray(inp["L0_attn_wb"]).reshape(NH, P).T),
        "gbr": _ct(np.asarray(inp["L0_attn_gb"])[None, :]),
        "uc": _ct(np.asarray(inp["L0_attn_bonus"]).T),
        "lngr": _ct(np.asarray(inp["L0_attn_ln_g"])[None, :]),
        "lnbr": _ct(np.asarray(inp["L0_attn_ln_b"])[None, :]),
        "wgT": _ct(np.asarray(inp["L0_ffn_w_gate"]).T),
        "wuT": _ct(np.asarray(inp["L0_ffn_w_up"]).T),
        "wdT": _ct(np.asarray(inp["L0_ffn_w_down"]).T),
        "wq1t": _ct(np.asarray(inp["L1_attn_Wq"]).reshape(8, 128, 8, 128)
                     .transpose(2, 0, 1, 3)),
        "wk1t": _ct(np.asarray(inp["L1_attn_Wk"]).reshape(8, 128, 8, 128)
                     .transpose(2, 0, 1, 3)),
        "wv1t": _ct(np.asarray(inp["L1_attn_Wv"]).reshape(8, 128, 8, 128)
                     .transpose(2, 0, 1, 3)),
        "wo1t": _ct(np.asarray(inp["L1_attn_Wo"]).reshape(8, 128, 8, 128)
                     .transpose(0, 2, 1, 3)),
        "bq1": _ct(np.asarray(inp["L1_attn_bq"]).reshape(NCH, P).T),
        "bk1": _ct(np.asarray(inp["L1_attn_bk"]).reshape(NCH, P).T),
        "bv1": _ct(np.asarray(inp["L1_attn_bv"]).reshape(NCH, P).T),
        "bo1": _ct(np.asarray(inp["L1_attn_bo"]).reshape(NCH, P).T),
        "gam1": _ct(np.asarray(inp["L1_attn_gamma"]).reshape(NCH, P).T),
        "bet1": _ct(np.asarray(inp["L1_attn_beta"]).reshape(NCH, P).T),
        "w1Tt": _ct(np.asarray(inp["L1_ffn_w1_w"]).reshape(32, 128, 8, 128)
                    .transpose(0, 2, 3, 1)),
        "w2Tt": _ct(np.asarray(inp["L1_ffn_w2_w"]).reshape(32, 128, 8, 128)
                    .transpose(0, 2, 3, 1)),
        "h1Tt": _ct(np.asarray(inp["L1_ffn_h1_w"]).reshape(32, 128, 32, 128)
                    .transpose(0, 2, 3, 1)),
        "h2Tt": _ct(np.asarray(inp["L1_ffn_h2_w"]).reshape(32, 128, 32, 128)
                    .transpose(0, 2, 3, 1)),
        "w3Tt": _ct(np.asarray(inp["L1_ffn_w3_w"]).reshape(8, 128, 32, 128)
                    .transpose(0, 2, 3, 1)),
        "s1c": _ct(np.asarray(inp["L1_ffn_w1_s"]).reshape(32, P).T),
        "s2c": _ct(np.asarray(inp["L1_ffn_w2_s"]).reshape(32, P).T),
        "sh1c": _ct(np.asarray(inp["L1_ffn_h1_s"]).reshape(32, P).T),
        "sh2c": _ct(np.asarray(inp["L1_ffn_h2_s"]).reshape(32, P).T),
        "s3c": _ct(np.asarray(inp["L1_ffn_w3_s"]).reshape(NCH, P).T),
    }

    pe = _pe_const()
    maps = []
    for c in range(8):
        b, half = divmod(c, 2)
        t0 = half * T
        m = dict(common)
        m["x_in"] = fm(x[b, t0:t0 + T, :])
        if half == 0:
            m["xprev_in"] = np.zeros((P, NCH), np.float32)
            m["flag"] = np.zeros((P, 1), np.float32)
        else:
            m["xprev_in"] = _ct(x[b, t0 - 1, :].reshape(NCH, P).T)
            m["flag"] = np.ones((P, 1), np.float32)
        m["pe_in"] = fm(pe[t0:t0 + T, :])
        maps.append(m)
    return maps


class Ctx:
    pass


def _declare_io(nc, c):
    dr = {}
    def din(name, shape, dt=F32):
        dr[name] = nc.dram_tensor(name, list(shape), dt, kind="ExternalInput")
    c.dr = dr
    din("x_in", (D, T)); din("xprev_in", (P, NCH)); din("flag", (P, 1))
    din("pe_in", (D, T))
    din("wx0T", (D, 160)); din("wx2T", (5, 33, D))
    din("rWT", (D, 512)); din("wAT", (D, 64)); din("wBT", (64, 512))
    din("kWT", (D, 512)); din("vWT", (D, D)); din("gAT", (D, 64))
    din("gBT", (64, D)); din("oWT", (D, D))
    din("mu0c", (P, NCH)); din("wbc", (P, NH)); din("gbr", (1, D))
    din("uc", (P, NH)); din("lngr", (1, D)); din("lnbr", (1, D))
    din("wgT", (D, 4096)); din("wuT", (D, 4096)); din("wdT", (4096, D))
    for n in ("wq1t", "wk1t", "wv1t", "wo1t"):
        din(n, (NCH, NCH, P, P))
    for n in ("bq1", "bk1", "bv1", "bo1", "gam1", "bet1"):
        din(n, (P, NCH))
    din("w1Tt", (32, 8, P, P)); din("w2Tt", (32, 8, P, P))
    din("h1Tt", (32, 32, P, P)); din("h2Tt", (32, 32, P, P))
    din("w3Tt", (8, 32, P, P))
    for n in ("s1c", "s2c", "sh1c", "sh2c"):
        din(n, (P, 32))
    din("s3c", (P, NCH))

    c.y_out = nc.dram_tensor("y", [D, T], F32, kind="ExternalOutput")
    c.x_hbm = [nc.dram_tensor(f"x_hbm{i}", [D, T], F32, kind="Internal")
               for i in range(2)]
    c.cc_s_in = nc.dram_tensor("cc_s_in", [DK, NH * DV], F32, kind="Internal")
    c.cc_s_out = nc.dram_tensor("cc_s_out", [2 * DK, NH * DV], F32, kind="Internal")
    c.cc_h_in = nc.dram_tensor("cc_h_in", [D, T], BF16, kind="Internal")
    c.h1bt = nc.dram_tensor("h1bt", [32, 32, P, P], BF16, kind="Internal")
    c.w1bt = nc.dram_tensor("w1bt", [32, NCH, P, P], BF16, kind="Internal")
    c.w2bt = nc.dram_tensor("w2bt", [32, NCH, P, P], BF16, kind="Internal")
    c.wqbt = nc.dram_tensor("wqbt", [NCH, NCH, P, P], BF16, kind="Internal")
    c.wkbt = nc.dram_tensor("wkbt", [NCH, NCH, P, P], BF16, kind="Internal")
    c.wvbt = nc.dram_tensor("wvbt", [NCH, NCH, P, P], BF16, kind="Internal")
    c.wobt = nc.dram_tensor("wobt", [NCH, NCH, P, P], BF16, kind="Internal")
    c.h2bt = nc.dram_tensor("h2bt", [32, 32, P, P], BF16, kind="Internal")
    c.w3bt = nc.dram_tensor("w3bt", [8, 32, P, P], BF16, kind="Internal")
    c.cc_h_out = nc.dram_tensor("cc_h_out", [2 * D, T], BF16, kind="Internal")

    jj, ii = np.meshgrid(np.arange(P), np.arange(P), indexing="ij")
    rmask = ((ii // CH == jj // CH) & (ii > jj)).astype(np.float32)
    c.rmask_t = nc.inline_tensor(rmask, name="rwkv_mask")
    c.ident_t = nc.inline_tensor(np.eye(P, dtype=np.float32), name="ident_f")
    c.ones_col_t = nc.inline_tensor(np.ones((P, 1), np.float32), name="ones_colc")
    c.ones_row_t = nc.inline_tensor(np.ones((1, P), np.float32), name="ones_rowc")


def build_nc():
    nc = bass.Bass("TRN2", target_bir_lowering=False)
    es = contextlib.ExitStack()
    c = Ctx()
    c.nc = nc
    b = Builder(nc, es)
    c.b = b
    _declare_io(nc, c)

    c.dS = b.sem("dS"); c.dC = b.sem("dC"); c.dG = b.sem("dG")
    c.dT = b.sem("dT")
    c.dS2 = b.sem("dS2")
    c.dG2 = b.sem("dG2")
    c.dGs = [c.dG, c.dG2, b.sem("dG3"), b.sem("dG4")]
    c.dCX = b.sem("dCX")
    c.dW = [b.sem(f"dW{i}") for i in range(4)]
    c.dPC = b.sem("dPC")
    c.dPL = [b.sem(f"dPL{i}") for i in range(4)]
    c.dPS = [b.sem(f"dPS{i}") for i in range(4)]
    c.TS = b.sem("TS"); c.VS = b.sem("VS"); c.SS = b.sem("SS"); c.GS = b.sem("GS")
    c.CC = b.sem("CC")
    b.serial = {"vector": c.VS, "scalar": c.SS, "gpsimd": c.GS}
    b.barrier_serial = {"vector": c.VS, "scalar": c.SS, "gpsimd": c.GS,
                        "tensor": c.TS}

    with es:
        def sb(name, shape, dt):
            return es.enter_context(nc.sbuf_tensor(name, list(shape), dt))
        c.sb = sb

        c.identsb = sb("identsb", (P, P), F32)
        c.identbf = sb("identbf", (P, P), BF16)
        c.rmask_sb = sb("rmask_sb", (P, P), BF16)
        c.ones_col = sb("ones_col", (P, 1), F32)
        c.ones_row = sb("ones_row", (1, P), F32)
        c.smalls = sb("smalls", (P, 64), F32)
        c.zrow = sb("zrow", (1, 256), BF16)
        c.ones_mat = sb("ones_mat", (P, 64), F32)
        c.C_MU0, c.C_WB, c.C_U, c.C_FLAG, c.C_NEG8 = 0, 8, 12, 16, 17
        c.C_WGS, c.C_WGR, c.C_WUS, c.C_WUR, c.C_WDS, c.C_WDR = 25, 26, 27, 28, 29, 30
        c.C_BQ, c.C_BK, c.C_BV, c.C_BO, c.C_GAM, c.C_BET, c.C_S3 = \
            24, 32, 40, 48, 56, 60, 52

        c.parena = es.enter_context(nc.psum_tensor("parena", [P, 7 * TT], F32))
        c.pb = [c.parena[:, i * TT:(i + 1) * TT] for i in range(7)]
        c.pbt = es.enter_context(nc.psum_tensor("pbt", [P, 2 * P], BF16))
        pc_init(c)

        PHASE_BARR.clear()
        emit_preload(c)
        PHASE_BARR.append(b.sem("BARR").n)
        emit_phase1(c)
        PHASE_BARR.append(b.sem("BARR").n)
        if PHASES >= 2:
            emit_phase2(c)
            PHASE_BARR.append(b.sem("BARR").n)
        if PHASES >= 3:
            emit_phase3(c)
            PHASE_BARR.append(b.sem("BARR").n)
        if PHASES >= 4:
            emit_phase4(c)
            PHASE_BARR.append(b.sem("BARR").n)
        emit_finish(c)

        block = es.enter_context(nc.Block())
        b.replay(block)
    return nc


def ln_stats_bcast(c, src_tiles, n_d, pb_m, pb_r, stat, sq):
    b = c.b
    W = src_tiles[0].shape[-1]
    TSm, VSm, SSm = c.TS, c.VS, c.SS
    pbs, pbq = c.pb[0], c.pb[1]
    for ck in range(NCH):
        b.call("tensor", "matmul", pbs[0:1, 0:W], c.ones_col[:, :], src_tiles[ck],
               start=(ck == 0), stop=(ck == NCH - 1), inc=TSm)
    for ck in range(NCH):
        vs = b.call("scalar", "activation", sq[:, 0:W], src_tiles[ck], AF.Square,
                    inc=SSm)
        b.wait("tensor", SSm, vs)
        vt = b.call("tensor", "matmul", pbq[0:1, 0:W], c.ones_col[:, :], sq[:, 0:W],
                    start=(ck == 0), stop=(ck == NCH - 1), inc=TSm)
        b.wait("scalar", TSm, vt)
    b.wait("vector", TSm, vt)
    b.call("vector", "tensor_scalar_mul", stat[0:1, 0:W], pbs[0:1, 0:W], 1.0 / n_d)
    b.call("vector", "tensor_scalar_mul", stat[0:1, W:2 * W], pbq[0:1, 0:W], 1.0 / n_d)
    b.call("vector", "tensor_mul", stat[0:1, 2 * W:3 * W], stat[0:1, 0:W],
           stat[0:1, 0:W])
    vv = b.call("vector", "scalar_tensor_tensor", stat[0:1, W:2 * W],
                stat[0:1, W:2 * W], EPS, stat[0:1, 2 * W:3 * W],
                ALU.add, ALU.subtract)
    b.wait("scalar", c.VS, vv)
    vs = b.call("scalar", "activation", stat[0:1, 2 * W:3 * W], stat[0:1, W:2 * W],
                AF.Sqrt, inc=SSm)
    b.wait("vector", SSm, vs)
    vv = b.call("vector", "reciprocal", stat[0:1, W:2 * W], stat[0:1, 2 * W:3 * W])
    b.wait("tensor", c.VS, vv)
    b.call("tensor", "matmul", pb_m[:, 0:W], c.ones_row[:, :], stat[0:1, 0:W],
           start=True, stop=True, inc=TSm)
    vt = b.call("tensor", "matmul", pb_r[:, 0:W], c.ones_row[:, :], stat[0:1, W:2 * W],
                start=True, stop=True, inc=TSm)
    return vt


def emit_preload(c):
    b, dr = c.b, c.dr
    b.dma("sync", c.identsb[:, :], c.ident_t.ap(), c.dS)
    b.dma("gpsimd", c.identbf[:, :], c.ident_t.ap(), c.dG)
    b.dma("gpsimd", c.rmask_sb[:, :], c.rmask_t.ap(), c.dG)
    b.dma("sync", c.ones_col[:, :], c.ones_col_t.ap(), c.dS)
    b.dma("sync", c.ones_row[:, :], c.ones_row_t.ap(), c.dS)
    b.dma("sync", c.smalls[:, c.C_MU0:c.C_MU0 + NCH], c.dr["mu0c"].ap(), c.dS)
    b.dma("sync", c.smalls[:, c.C_WB:c.C_WB + NH], dr["wbc"].ap(), c.dS)
    b.dma("sync", c.smalls[:, c.C_U:c.C_U + NH], dr["uc"].ap(), c.dS)
    b.dma("sync", c.smalls[:, c.C_FLAG:c.C_FLAG + 1], dr["flag"].ap(), c.dS)
    b.call("vector", "memset", c.zrow[:, :], 0.0)
    b.call("vector", "memset", c.ones_mat[:, :], 1.0)
    b.call("vector", "memset", c.smalls[:, c.C_NEG8:c.C_NEG8 + 1], -8.0)
    b.barrier()


def emit_ln_to(c, src_dram, xln, stat, sq, xt, shift_cols):
    b = c.b
    pb = c.pb
    Ts = T + shift_cols
    for tt in range(NTT):
        for ck in range(NCH):
            b.dma("sync", xt[:, ck * TT:(ck + 1) * TT],
                  src_dram[ck * P:(ck + 1) * P, tt * TT:(tt + 1) * TT], c.dS)
        b.wait("tensor", c.dS, c.dS.n)
        b.wait("scalar", c.dS, c.dS.n)
        srcs = [xt[:, ck * TT:(ck + 1) * TT] for ck in range(NCH)]
        vt = ln_stats_bcast(c, srcs, D, pb[2], pb[3], stat, sq)
        b.wait("vector", c.TS, vt)
        for ck in range(NCH):
            b.call("vector", "tensor_sub", xt[:, ck * TT:(ck + 1) * TT],
                   xt[:, ck * TT:(ck + 1) * TT], pb[2][:, 0:TT])
            b.call("vector", "tensor_mul",
                   xln[:, ck * Ts + shift_cols + tt * TT:
                       ck * Ts + shift_cols + (tt + 1) * TT],
                   xt[:, ck * TT:(ck + 1) * TT], pb[3][:, 0:TT])
        b.barrier()


def emit_phase1(c):
    nc, b, dr = c.nc, c.b, c.dr
    dS, dG, TS, VS, SS, CC = c.dS, c.dG, c.TS, c.VS, c.SS, c.CC
    pb = c.pb

    with contextlib.ExitStack() as ph:
        sbf = lambda n, sh, dt: ph.enter_context(nc.sbuf_tensor(n, list(sh), dt))
        r_ = sbf("r_", (P, NH * T), BF16)
        k_ = sbf("k_", (P, NH * T), BF16)
        vtm = sbf("vtm", (P, NTC * D), BF16)
        tw = sbf("tw", (P, T), BF16)
        tg = sbf("tg", (P, T), BF16)

        with contextlib.ExitStack() as s1:
            sbl = lambda n, sh, dt: s1.enter_context(nc.sbuf_tensor(n, list(sh), dt))
            xln = sbl("xln", (P, NCH * (T + 1)), BF16)
            stat = sbl("stat", (1, 3 * TT), F32)
            sq = sbl("sq", (P, TT), F32)
            xt = sbl("xt", (P, NCH * TT), F32)
            mscr = sbl("mscr", (P, TT), F32)
            mixb = sbl("mixb", (P, NCH * TT), BF16)
            xx = sbl("xx", (33, 5 * TT), BF16)
            wx0 = sbl("wx0", (P, NCH * 160), BF16)
            wx2 = sbl("wx2", (33, 5 * D), BF16)
            rwS = sbl("rwS", (P, NCH * 512), BF16)
            kwS = sbl("kwS", (P, NCH * 512), BF16)
            vwS = sbl("vwS", (P, NCH * D), BF16)
            waS = sbl("waS", (P, NCH * 64), BF16)
            gaS = sbl("gaS", (P, NCH * 64), BF16)
            xprev = sbl("xprev", (P, NCH), F32)

            for ck in range(NCH):
                b.dma("gpsimd", wx0[:, ck * 160:(ck + 1) * 160],
                      dr["wx0T"][ck * P:(ck + 1) * P, :], dG)
                b.dma("gpsimd", rwS[:, ck * 512:(ck + 1) * 512],
                      dr["rWT"][ck * P:(ck + 1) * P, :], dG)
                b.dma("gpsimd", kwS[:, ck * 512:(ck + 1) * 512],
                      dr["kWT"][ck * P:(ck + 1) * P, :], dG)
                b.dma("gpsimd", vwS[:, ck * D:(ck + 1) * D],
                      dr["vWT"][ck * P:(ck + 1) * P, :], dG)
                b.dma("gpsimd", waS[:, ck * 64:(ck + 1) * 64],
                      dr["wAT"][ck * P:(ck + 1) * P, :], dG)
                b.dma("gpsimd", gaS[:, ck * 64:(ck + 1) * 64],
                      dr["gAT"][ck * P:(ck + 1) * P, :], dG)
            for g in range(5):
                b.dma("gpsimd", wx2[:, g * D:(g + 1) * D], dr["wx2T"][g, :, :], dG)
            wload_dG = dG.n

            b.dma("sync", xprev[:, :], dr["xprev_in"].ap(), dS)
            b.wait("tensor", dS, dS.n)
            b.wait("scalar", dS, dS.n)
            b.call("tensor", "matmul", pb[0][0:1, 0:NCH], c.ones_col[:, :],
                   xprev[:, :], start=True, stop=True, inc=TS)
            vs = b.call("scalar", "activation", sq[:, 0:NCH], xprev[:, :], AF.Square,
                        inc=SS)
            b.wait("tensor", SS, vs)
            vt = b.call("tensor", "matmul", pb[1][0:1, 0:NCH], c.ones_col[:, :],
                        sq[:, 0:NCH], start=True, stop=True, inc=TS)
            b.wait("vector", TS, vt)
            b.call("vector", "reduce_sum", stat[0:1, 0:1], pb[0][0:1, 0:NCH], AXX)
            b.call("vector", "reduce_sum", stat[0:1, 1:2], pb[1][0:1, 0:NCH], AXX)
            b.call("vector", "tensor_scalar_mul", stat[0:1, 0:2], stat[0:1, 0:2],
                   1.0 / D)
            b.call("vector", "tensor_mul", stat[0:1, 2:3], stat[0:1, 0:1],
                   stat[0:1, 0:1])
            vv = b.call("vector", "scalar_tensor_tensor", stat[0:1, 1:2],
                        stat[0:1, 1:2], EPS, stat[0:1, 2:3], ALU.add, ALU.subtract)
            b.wait("scalar", VS, vv)
            vs = b.call("scalar", "activation", stat[0:1, 2:3], stat[0:1, 1:2],
                        AF.Sqrt, inc=SS)
            b.wait("vector", SS, vs)
            vv = b.call("vector", "reciprocal", stat[0:1, 1:2], stat[0:1, 2:3])
            b.wait("tensor", VS, vv)
            b.call("tensor", "matmul", pb[2][:, 0:1], c.ones_row[:, :],
                   stat[0:1, 0:1], start=True, stop=True, inc=TS)
            vt = b.call("tensor", "matmul", pb[3][:, 0:1], c.ones_row[:, :],
                        stat[0:1, 1:2], start=True, stop=True, inc=TS)
            b.wait("vector", TS, vt)
            for ck in range(NCH):
                b.call("vector", "tensor_scalar",
                       xln[:, ck * (T + 1):ck * (T + 1) + 1], xprev[:, ck:ck + 1],
                       pb[2][:, 0:1], pb[3][:, 0:1], ALU.subtract, ALU.mult)
            b.barrier()

            emit_ln_to(c, dr["x_in"], xln, stat, sq, xt, 1)

            b.call("vector", "memset", xx[32:33, :], 1.0)
            b.barrier()
            b.wait("tensor", dG, wload_dG)
            b.wait("vector", dG, wload_dG)

            for tt in range(NTT):
                base = tt * TT
                for ck in range(NCH):
                    xo = ck * (T + 1) + 1 + base
                    b.call("vector", "tensor_sub", xt[:, ck * TT:(ck + 1) * TT],
                           xln[:, xo - 1:xo - 1 + TT], xln[:, xo:xo + TT])
                for ck in range(NCH):
                    xo = ck * (T + 1) + 1 + base
                    b.call("vector", "scalar_tensor_tensor",
                           mixb[:, ck * TT:(ck + 1) * TT],
                           xt[:, ck * TT:(ck + 1) * TT],
                           c.smalls[:, c.C_MU0 + ck:c.C_MU0 + ck + 1],
                           xln[:, xo:xo + TT], ALU.mult, ALU.add)
                b.wait("tensor", VS, VS.n)
                for g in range(5):
                    for ck in range(NCH):
                        b.call("tensor", "matmul", pb[g][0:32, 0:TT],
                               wx0[:, ck * 160 + g * 32:ck * 160 + (g + 1) * 32],
                               mixb[:, ck * TT:(ck + 1) * TT],
                               start=(ck == 0), stop=(ck == NCH - 1), inc=TS)
                b.wait("scalar", TS, TS.n)
                for g in range(5):
                    b.call("scalar", "activation", xx[0:32, g * TT:(g + 1) * TT],
                           pb[g][0:32, 0:TT], AF.Tanh, inc=SS)
                xx_S = SS.n

                for gname, gidx in (("r", 0), ("w", 1), ("k", 2), ("v", 3), ("g", 4)):
                    b.wait("tensor", SS, xx_S)
                    for mck in range(NCH):
                        xo = mck * (T + 1) + 1 + base
                        vt = b.call("tensor", "matmul", pb[5][:, 0:TT],
                                    wx2[0:33, gidx * D + mck * P:gidx * D + (mck + 1) * P],
                                    xx[0:33, gidx * TT:(gidx + 1) * TT],
                                    start=True, stop=True, inc=TS)
                        b.wait("vector", TS, vt)
                        b.call("vector", "tensor_mul", mscr[:, 0:TT], pb[5][:, 0:TT],
                               xt[:, mck * TT:(mck + 1) * TT])
                        vv = b.call("vector", "tensor_add",
                                    mixb[:, mck * TT:(mck + 1) * TT], mscr[:, 0:TT],
                                    xln[:, xo:xo + TT])
                        b.wait("tensor", VS, vv)
                    if gname in ("r", "k"):
                        wsb = rwS if gname == "r" else kwS
                        dst = r_ if gname == "r" else k_
                        for m in range(4):
                            for ck in range(NCH):
                                b.call("tensor", "matmul", pb[m][:, 0:TT],
                                       wsb[:, ck * 512 + m * P:ck * 512 + (m + 1) * P],
                                       mixb[:, ck * TT:(ck + 1) * TT],
                                       start=(ck == 0), stop=(ck == NCH - 1), inc=TS)
                        b.wait("vector", TS, TS.n)
                        for m in range(4):
                            b.call("vector", "tensor_copy",
                                   dst[:, m * T + base:m * T + base + TT],
                                   pb[m][:, 0:TT])
                        b.wait("tensor", VS, VS.n)
                    elif gname == "v":
                        for tc4 in range(4):
                            tc = tt * 4 + tc4
                            for fh in range(2):
                                bk = (tc4 * 2 + fh) % 2
                                for ck in range(NCH):
                                    b.call("tensor", "matmul", pb[bk][:, 0:TT],
                                           mixb[:, ck * TT + tc4 * P:
                                                ck * TT + (tc4 + 1) * P],
                                           vwS[:, ck * D + fh * 512:
                                               ck * D + (fh + 1) * 512],
                                           start=(ck == 0), stop=(ck == NCH - 1),
                                           inc=TS)
                                b.wait("vector", TS, TS.n)
                                vv = b.call("vector", "tensor_copy",
                                            vtm[:, tc * D + fh * 512:
                                                tc * D + (fh + 1) * 512],
                                            pb[bk][:, 0:TT])
                                b.wait("tensor", VS, vv)
                    else:
                        wsb = waS if gname == "w" else gaS
                        dst = tw if gname == "w" else tg
                        for ck in range(NCH):
                            b.call("tensor", "matmul", pb[6][0:64, 0:TT],
                                   wsb[:, ck * 64:(ck + 1) * 64],
                                   mixb[:, ck * TT:(ck + 1) * TT],
                                   start=(ck == 0), stop=(ck == NCH - 1), inc=TS)
                        b.wait("scalar", TS, TS.n)
                        vs = b.call("scalar", "activation",
                                    dst[0:64, base:base + TT], pb[6][0:64, 0:TT],
                                    AF.Tanh, inc=SS)
                        b.wait("tensor", SS, vs)
                b.barrier()

        with contextlib.ExitStack() as s2:
            sbl = lambda n, sh, dt: s2.enter_context(nc.sbuf_tensor(n, list(sh), dt))
            gbT = sbl("gbT", (64, D), BF16)
            oWb = sbl("oWb", (P, NCH * D), BF16)
            o_tm = sbl("o_tm", (P, NTC * D), BF16)

            with contextlib.ExitStack() as s3:
                sbm = lambda n, sh, dt: s3.enter_context(nc.sbuf_tensor(n, list(sh), dt))
                wbT = sbm("wbT", (64, 512), BF16)
                qe = sbm("qe", (P, NH * T), BF16)
                ke = sbm("ke", (P, T), BF16)
                kwtm = sbm("kwtm", (P, NTC * P), BF16)
                wtil = sbm("wtil", (P, T), F32)
                wcc = sbm("wcc", (P, T), F32)
                e1 = sbm("e1", (P, T), F32)
                amask = sbm("amask", (P, P), BF16)
                dexps = sbm("dexps", (P, 5 * NCK), F32)
                bnd = sbm("bnd", (P, NCK), F32)
                s_row = sbm("s_row", (1, T), F32)
                s_tm = sbm("s_tm", (P, NH * NTC), F32)
                S_all = sbm("S_all", (P, NH * DV), BF16)
                S_in = sbm("S_in", (P, NH * DV), BF16)

                b.dma("gpsimd", wbT[:, :], dr["wBT"].ap(), dG)
                b.dma("gpsimd", gbT[:, :], dr["gBT"].ap(), dG)
                for ck in range(NCH):
                    b.dma("gpsimd", oWb[:, ck * D:(ck + 1) * D],
                          dr["oWT"][ck * P:(ck + 1) * P, :], dG)
                b.wait("tensor", dG, dG.n)

                for h in range(NH):
                    vv = b.call("vector", "tensor_mul", e1[:, :],
                                r_[:, h * T:(h + 1) * T], k_[:, h * T:(h + 1) * T])
                    b.wait("tensor", VS, vv)
                    cp = {0: 0, 1: 0}
                    for q4 in range(NTT):
                        bk5 = 5 + q4 % 2
                        b.wait("tensor", VS, cp[q4 % 2])
                        vt = b.call("tensor", "matmul", pb[bk5][0:1, 0:TT],
                                    c.smalls[:, c.C_U + h:c.C_U + h + 1],
                                    e1[:, q4 * TT:(q4 + 1) * TT],
                                    start=True, stop=True, inc=TS)
                        b.wait("vector", TS, vt)
                        cp[q4 % 2] = b.call("vector", "tensor_copy",
                                            s_row[0:1, q4 * TT:(q4 + 1) * TT],
                                            pb[bk5][0:1, 0:TT])
                    for tc in range(NTC):
                        bk5 = 5 + tc % 2
                        b.wait("tensor", VS, cp[tc % 2])
                        vt = b.call("tensor", "transpose", pb[bk5][:, 0:1],
                                    s_row[0:1, tc * P:(tc + 1) * P],
                                    c.identsb[0:1, 0:1], inc=TS)
                        b.wait("vector", TS, vt)
                        cp[tc % 2] = b.call("vector", "tensor_copy",
                                            s_tm[:, h * NTC + tc:h * NTC + tc + 1],
                                            pb[bk5][:, 0:1])
                    b.wait("tensor", VS, max(cp.values()))
                b.call("vector", "memset", S_all[:, :], 0.0)
                b.barrier()

                for h in range(NH):
                    hT = h * T
                    ep = {0: 0, 1: 0}
                    for q4 in range(NTT):
                        bk5 = 5 + q4 % 2
                        b.wait("tensor", SS, ep[q4 % 2])
                        vt = b.call("tensor", "matmul", pb[bk5][:, 0:TT],
                                    wbT[:, h * P:(h + 1) * P],
                                    tw[0:64, q4 * TT:(q4 + 1) * TT],
                                    start=True, stop=True, inc=TS)
                        b.wait("scalar", TS, vt)
                        ep[q4 % 2] = b.call("scalar", "activation",
                                            wtil[:, q4 * TT:(q4 + 1) * TT],
                                            pb[bk5][:, 0:TT], AF.Exp,
                                            bias=c.smalls[:, c.C_WB + h:
                                                          c.C_WB + h + 1],
                                            inc=SS)
                    b.wait("vector", SS, SS.n)
                    b.call("vector", "tensor_tensor_scan", wcc[:, :],
                           wtil[:, :], wtil[:, :], 0.0, ALU.add, ALU.bypass)
                    b.call("vector", "memset", bnd[:, 0:1], 0.0)
                    b.call("vector", "tensor_copy", bnd[:, 1:NCK],
                           wcc[:, CH - 1:T - 1:CH])
                    b.call("vector", "tensor_tensor",
                           wcc.ap().rearrange("p (a b) -> p a b", a=NCK),
                           wcc.ap().rearrange("p (a b) -> p a b", a=NCK),
                           bnd[:, :, None].broadcast_to([P, NCK, CH]),
                           ALU.subtract)
                    b.wait("scalar", VS, VS.n)
                    b.call("scalar", "activation", dexps[:, 0:NCK],
                           wcc[:, CH - 1:T:CH], AF.Exp, scale=-1.0, inc=SS)
                    vs = b.call("scalar", "activation", dexps[:, NCK:2 * NCK],
                                wcc[:, CH - 1:T:CH], AF.Copy, scale=-1.0, inc=SS)
                    b.wait("vector", SS, vs)
                    b.call("vector", "tensor_tensor_scan", dexps[:, 2 * NCK:3 * NCK],
                           dexps[:, NCK:2 * NCK], dexps[:, NCK:2 * NCK], 0.0,
                           ALU.add, ALU.bypass)
                    vv = b.call("vector", "tensor_sub", dexps[:, 3 * NCK:4 * NCK],
                                dexps[:, 2 * NCK:3 * NCK], dexps[:, NCK:2 * NCK])
                    b.wait("scalar", VS, vv)
                    dexc_s = b.call("scalar", "activation", dexps[:, 4 * NCK:5 * NCK],
                                    dexps[:, 3 * NCK:4 * NCK], AF.Exp, inc=SS)

                    vv = b.call("vector", "tensor_sub", e1[:, :], wtil[:, :],
                                wcc[:, :])
                    b.wait("scalar", VS, vv)
                    vs = b.call("scalar", "activation", e1[:, :], e1[:, :], AF.Exp,
                                inc=SS)
                    b.wait("vector", SS, vs)
                    vv = b.call("vector", "tensor_mul", qe[:, hT:hT + T],
                                r_[:, hT:hT + T], e1[:, :])
                    b.wait("scalar", VS, vv)
                    vs = b.call("scalar", "activation", e1[:, :], wcc[:, :], AF.Exp,
                                inc=SS)
                    b.wait("vector", SS, vs)
                    vv = b.call("vector", "tensor_mul", ke[:, :], k_[:, hT:hT + T],
                                e1[:, :])
                    b.wait("scalar", VS, vv)
                    b.call("vector", "tensor_copy", bnd[:, 0:NCK],
                           wcc[:, CH - 1:T:CH])
                    b.call("vector", "tensor_tensor",
                           e1.ap().rearrange("p (a b) -> p a b", a=NCK),
                           wcc.ap().rearrange("p (a b) -> p a b", a=NCK),
                           bnd[:, :, None].broadcast_to([P, NCK, CH]),
                           ALU.subtract)
                    b.wait("scalar", VS, VS.n)
                    vs = b.call("scalar", "activation", e1[:, :], e1[:, :], AF.Exp,
                                inc=SS)
                    b.wait("vector", SS, vs)
                    vv = b.call("vector", "tensor_mul", e1[:, :], k_[:, hT:hT + T],
                                e1[:, :])
                    b.wait("tensor", VS, vv)
                    cp = {0: 0, 1: 0}
                    for tc in range(NTC):
                        bk5 = 5 + tc % 2
                        b.wait("tensor", VS, cp[tc % 2])
                        vt = b.call("tensor", "transpose", pb[bk5][:, 0:P],
                                    e1[:, tc * P:(tc + 1) * P], c.identsb[:, :],
                                    inc=TS)
                        b.wait("vector", TS, vt)
                        cp[tc % 2] = b.call("vector", "tensor_copy",
                                            kwtm[:, tc * P:(tc + 1) * P],
                                            pb[bk5][:, 0:P])
                    b.wait("tensor", VS, max(cp.values()))

                    Sh = S_all[:, h * DV:(h + 1) * DV]
                    b.wait("vector", SS, dexc_s)
                    for tc in range(NTC):
                        vt = b.call("tensor", "matmul", pb[0][:, 0:P],
                                    ke[:, tc * P:(tc + 1) * P],
                                    qe[:, hT + tc * P:hT + (tc + 1) * P],
                                    start=True, stop=True, inc=TS)
                        b.wait("vector", TS, vt)
                        am_v = b.call("vector", "tensor_mul", amask[:, :],
                                      pb[0][:, 0:P], c.rmask_sb[:, :])
                        for j4 in range(4):
                            b.call("tensor", "matmul", pb[2 + j4][:, 0:DV],
                                   kwtm[j4 * CH:(j4 + 1) * CH, tc * P:(tc + 1) * P],
                                   vtm[j4 * CH:(j4 + 1) * CH,
                                       tc * D + h * DV:tc * D + (h + 1) * DV],
                                   start=True, stop=True,
                                   tile_position=(j4 * CH, 0), inc=TS)
                        b.wait("tensor", VS, am_v)
                        b.call("tensor", "matmul", pb[1][:, 0:DV], amask[:, :],
                               vtm[:, tc * D + h * DV:tc * D + (h + 1) * DV],
                               start=True, stop=False, inc=TS)
                        for j4 in range(4):
                            vt = b.call("tensor", "matmul",
                                        pb[1][j4 * CH:(j4 + 1) * CH, 0:DV],
                                        qe[:, hT + tc * P + j4 * CH:
                                           hT + tc * P + (j4 + 1) * CH],
                                        Sh, start=False, stop=False,
                                        tile_position=(0, j4 * CH), inc=TS)
                            b.wait("vector", TS, vt)
                            vv = b.call("vector", "scalar_tensor_tensor", Sh, Sh,
                                        dexps[:, tc * 4 + j4:tc * 4 + j4 + 1],
                                        pb[2 + j4][:, 0:DV], ALU.mult, ALU.add)
                            b.wait("tensor", VS, vv)
                        b.call("tensor", "matmul", pb[1][:, 0:DV],
                               c.zrow[0:1, 0:P], c.zrow[0:1, 0:DV],
                               start=False, stop=True, inc=TS)
                        b.wait("vector", TS, TS.n)
                        vv = b.call("vector", "scalar_tensor_tensor",
                                    o_tm[:, tc * D + h * DV:tc * D + (h + 1) * DV],
                                    vtm[:, tc * D + h * DV:tc * D + (h + 1) * DV],
                                    s_tm[:, h * NTC + tc:h * NTC + tc + 1],
                                    pb[1][:, 0:DV], ALU.mult, ALU.add)
                        b.wait("tensor", VS, vv)
                    for tc in range(NTC):
                        qs = qe[:, hT + tc * P:hT + (tc + 1) * P]
                        b.call("vector", "tensor_tensor",
                               qs.rearrange("p (a b) -> p a b", a=4),
                               qs.rearrange("p (a b) -> p a b", a=4),
                               dexps[:, 4 * NCK + tc * 4:4 * NCK + (tc + 1) * 4]
                               [:, :, None].broadcast_to([P, 4, CH]),
                               ALU.mult)
                    b.barrier()

                b.dma("gpsimd", c.cc_s_in.ap(), S_all[:, :], dG)
                b.wait("gpsimd", dG, dG.n)
                vg = b.op("gpsimd", lambda e: e.collective_compute(
                    "AllGather", ALU.bypass,
                    replica_groups=[[0, 1], [2, 3], [4, 5], [6, 7]],
                    ins=[c.cc_s_in.ap()], outs=[c.cc_s_out.ap()]), inc=CC)
                b.wait("gpsimd", CC, vg)
                b.dma("gpsimd", S_in[:, :], c.cc_s_out[0:DK, :], dG)
                b.wait("vector", dG, dG.n)
                vv = b.call("vector", "tensor_scalar_mul", S_in[:, :], S_in[:, :],
                            c.smalls[:, c.C_FLAG:c.C_FLAG + 1])
                b.wait("tensor", VS, vv)
                for h in range(NH):
                    for tc in range(NTC):
                        vt = b.call("tensor", "matmul", pb[0][:, 0:DV],
                                    qe[:, h * T + tc * P:h * T + (tc + 1) * P],
                                    S_in[:, h * DV:(h + 1) * DV],
                                    start=True, stop=True, inc=TS)
                        b.wait("vector", TS, vt)
                        vv = b.call("vector", "tensor_add",
                                    o_tm[:, tc * D + h * DV:tc * D + (h + 1) * DV],
                                    o_tm[:, tc * D + h * DV:tc * D + (h + 1) * DV],
                                    pb[0][:, 0:DV])
                        b.wait("tensor", VS, vv)
                b.barrier()

            with contextlib.ExitStack() as s4:
                sbl = lambda n, sh, dt: s4.enter_context(nc.sbuf_tensor(n, list(sh), dt))
                gbb = sbl("gbb", (P, 3 * D), F32)
                ogfm = sbl("ogfm", (P, NCH * T), BF16)
                lnv = sbl("lnv", (1, 3 * D), F32)
                colstat = sbl("colstat", (P, 8), F32)
                yst = sbl("yst", (P, TT), F32)
                xrel = sbl("xrel", (P, TT), F32)
                sc = sbl("sc", (P, 2 * D), F32)

                b.dma("sync", lnv[0:1, 0:D], dr["lngr"].ap(), dS)
                b.dma("sync", lnv[0:1, D:2 * D], dr["lnbr"].ap(), dS)
                b.dma("sync", lnv[0:1, 2 * D:3 * D], dr["gbr"].ap(), dS)
                b.wait("tensor", dS, dS.n)
                for i in range(3):
                    for fh in range(2):
                        vt = b.call("tensor", "matmul", pb[6][:, 0:TT],
                                    c.ones_row[:, :],
                                    lnv[0:1, i * D + fh * 512:i * D + (fh + 1) * 512],
                                    start=True, stop=True, inc=TS)
                        b.wait("vector", TS, vt)
                        vv = b.call("vector", "tensor_copy",
                                    gbb[:, i * D + fh * 512:i * D + (fh + 1) * 512],
                                    pb[6][:, 0:TT])
                        b.wait("tensor", VS, vv)
                b.barrier()

                for tc in range(NTC):
                    b.call("scalar", "activation", sc[:, 0:D],
                           o_tm[:, tc * D:(tc + 1) * D], AF.Copy,
                           accum_out=colstat[:, 0:1], inc=SS)
                    vs = b.call("scalar", "activation", sc[:, 0:D],
                                o_tm[:, tc * D:(tc + 1) * D], AF.Square,
                                accum_out=colstat[:, 1:2], inc=SS)
                    b.wait("vector", SS, vs)
                    b.call("vector", "tensor_scalar_mul", colstat[:, 0:2],
                           colstat[:, 0:2], 1.0 / D)
                    b.call("vector", "tensor_mul", colstat[:, 2:3], colstat[:, 0:1],
                           colstat[:, 0:1])
                    vv = b.call("vector", "scalar_tensor_tensor", colstat[:, 1:2],
                                colstat[:, 1:2], EPS, colstat[:, 2:3],
                                ALU.add, ALU.subtract)
                    b.wait("scalar", VS, vv)
                    vs = b.call("scalar", "activation", colstat[:, 2:3],
                                colstat[:, 1:2], AF.Sqrt, inc=SS)
                    b.wait("vector", SS, vs)
                    b.call("vector", "reciprocal", colstat[:, 1:2], colstat[:, 2:3])
                    for fh in range(2):
                        f0 = fh * 512
                        vt = b.call("tensor", "matmul", pb[0][:, 0:TT],
                                    tg[0:64, tc * P:(tc + 1) * P],
                                    gbT[:, f0:f0 + 512],
                                    start=True, stop=True, inc=TS)
                        b.wait("vector", TS, vt)
                        vv = b.call("vector", "tensor_add", sc[:, 0:TT],
                                    pb[0][:, 0:TT], gbb[:, 2 * D + f0:2 * D + f0 + 512])
                        b.wait("scalar", VS, vv)
                        b.wait("tensor", VS, vv)
                        gs = b.call("scalar", "activation", sc[:, 0:TT], sc[:, 0:TT],
                                    AF.Silu, inc=SS)
                        b.call("vector", "tensor_scalar", sc[:, D:D + TT],
                               o_tm[:, tc * D + f0:tc * D + f0 + 512],
                               colstat[:, 0:1], colstat[:, 1:2],
                               ALU.subtract, ALU.mult)
                        b.call("vector", "tensor_mul", sc[:, D:D + TT],
                               sc[:, D:D + TT], gbb[:, f0:f0 + 512])
                        b.call("vector", "tensor_add", sc[:, D:D + TT],
                               sc[:, D:D + TT], gbb[:, D + f0:D + f0 + 512])
                        b.wait("vector", SS, gs)
                        vv = b.call("vector", "tensor_mul",
                                    o_tm[:, tc * D + f0:tc * D + f0 + 512],
                                    sc[:, D:D + TT], sc[:, 0:TT])
                        b.wait("scalar", VS, vv)
                b.barrier()

                for tc in range(NTC):
                    for ck in range(NCH):
                        vt = b.call("tensor", "transpose", c.pbt[:, 0:P],
                                    o_tm[:, tc * D + ck * P:tc * D + (ck + 1) * P],
                                    c.identbf[:, :], inc=TS)
                        b.wait("vector", TS, vt)
                        vv = b.call("vector", "tensor_copy",
                                    ogfm[:, ck * T + tc * P:ck * T + (tc + 1) * P],
                                    c.pbt[:, 0:P])
                        b.wait("tensor", VS, vv)
                b.barrier()

                prev_read = {0: 0, 1: 0}
                for oc in range(NCH):
                    for q4 in range(NTT):
                        bk = (oc * NTT + q4) % 2
                        b.wait("tensor", VS, prev_read[bk])
                        for ck in range(NCH):
                            b.call("tensor", "matmul", pb[bk][:, 0:TT],
                                   oWb[:, ck * D + oc * P:ck * D + (oc + 1) * P],
                                   ogfm[:, ck * T + q4 * TT:ck * T + (q4 + 1) * TT],
                                   start=(ck == 0), stop=(ck == NCH - 1), inc=TS)
                        mm_T = TS.n
                        b.dma("sync", xrel[:, :],
                              dr["x_in"][oc * P:(oc + 1) * P, q4 * TT:(q4 + 1) * TT],
                              dS)
                        b.wait("vector", dS, dS.n)
                        b.wait("vector", TS, mm_T)
                        vv = b.call("vector", "tensor_add", yst[:, :],
                                    pb[bk][:, 0:TT], xrel[:, :])
                        prev_read[bk] = vv
                        b.wait("sync", VS, vv)
                        b.dma("sync",
                              c.x_hbm[0][oc * P:(oc + 1) * P, q4 * TT:(q4 + 1) * TT],
                              yst[:, :], c.dT)
                        b.wait("tensor", c.dT, c.dT.n)
                        b.wait("vector", c.dT, c.dT.n)
                b.barrier()

    b.barrier()
    c.x_cur = 0


def emit_token_max_bcast(c, macc, colbuf, rowbuf, mfull):
    b = c.b
    pb = c.pb
    TS, VS = c.TS, c.VS
    b.wait("tensor", VS, VS.n)
    for tc in range(NTC):
        vt = b.call("tensor", "transpose", pb[4][:, 0:P],
                    macc[:, tc * P:(tc + 1) * P], c.identsb[:, :], inc=TS)
        b.wait("vector", TS, vt)
        vv = b.call("vector", "reduce_max", colbuf[:, :], pb[4][:, 0:P], AXX)
        b.wait("tensor", VS, vv)
        vt = b.call("tensor", "transpose", pb[5][0:1, 0:P], colbuf[:, :],
                    c.identsb[:, :], inc=TS)
        b.wait("vector", TS, vt)
        vv = b.call("vector", "tensor_copy", rowbuf[0:1, tc * P:(tc + 1) * P],
                    pb[5][0:1, 0:P])
        b.wait("tensor", VS, vv)
    for q4 in range(NTT):
        vt = b.call("tensor", "matmul", pb[4][:, 0:TT], c.ones_row[:, :],
                    rowbuf[0:1, q4 * TT:(q4 + 1) * TT], start=True, stop=True,
                    inc=TS)
        b.wait("vector", TS, vt)
        vv = b.call("vector", "tensor_copy", mfull[:, q4 * TT:(q4 + 1) * TT],
                    pb[4][:, 0:TT])
        b.wait("tensor", VS, vv)
    b.wait("vector", c.TS, c.TS.n)



def pc_init(c):
    c.pc_jobs = []
    for srcn, dsn, noc, na in (("wq1t", "wqbt", 8, 8), ("wk1t", "wkbt", 8, 8),
                               ("wv1t", "wvbt", 8, 8), ("wo1t", "wobt", 8, 8),
                               ("w1Tt", "w1bt", 32, 8), ("w2Tt", "w2bt", 32, 8),
                               ("h1Tt", "h1bt", 32, 32), ("h2Tt", "h2bt", 32, 32),
                               ("w3Tt", "w3bt", 8, 32)):
        dstt = getattr(c, dsn)
        for oc in range(noc):
            for a0 in range(0, na, 8):
                c.pc_jobs.append((srcn, dstt, oc, a0, a0 + 8))
    c.pc_state = {"li": 0, "si": 0, "loadv": {}, "storev": {}}
    c.pc_done = {}


def pc_step(c):
    b, dr = c.b, c.dr
    st = c.pc_state
    nb = len(c.pcst)
    i = st["li"]
    if i < len(c.pc_jobs):
        bi = i % nb
        srcn, dstt, oc, a0, a1 = c.pc_jobs[i]
        if i >= nb:
            b.wait("gpsimd", c.dPS[bi], st["storev"][i - nb])
        b.dma("gpsimd", c.pcst[bi].ap().rearrange("q (a m) -> q a m", a=8),
              dr[srcn][oc, a0:a1, :, :].rearrange("a p m -> p a m"), c.dPL[bi])
        st["loadv"][i] = c.dPL[bi].n
        st["li"] = i + 1
    j = st["si"]
    if j < len(c.pc_jobs) and j <= st["li"] - 2 or (st["li"] == len(c.pc_jobs)
                                                    and j < st["li"]):
        bj = j % nb
        srcn, dstt, oc, a0, a1 = c.pc_jobs[j]
        b.wait("gpsimd", c.dPL[bj], st["loadv"][j])
        b.dma("gpsimd", dstt[oc, a0:a1, :, :].rearrange("a p m -> p a m"),
              c.pcst[bj].ap().rearrange("q (a m) -> q a m", a=8), c.dPS[bj])
        st["storev"][j] = c.dPS[bj].n
        st["si"] = j + 1
        if j + 1 == len(c.pc_jobs) or c.pc_jobs[j + 1][0] != srcn:
            c.pc_done[srcn] = [c.dPS[k].n for k in range(4)]


def pc_drain(c):
    while c.pc_state["si"] < len(c.pc_jobs):
        pc_step(c)


def emit_phase2(c):
    nc, b, dr = c.nc, c.b, c.dr
    dS, dG, dT, TS, VS, SS, GS = c.dS, c.dG, c.dT, c.TS, c.VS, c.SS, c.GS
    pb = c.pb
    x_in = c.x_hbm[0]
    x_out = c.x_hbm[1]

    with contextlib.ExitStack() as ph:
        sbf = lambda n, sh, dt: ph.enter_context(nc.sbuf_tensor(n, list(sh), dt))
        xq = sbf("p2xq", (P, NCH * T), BF16)
        sm2 = sbf("p2sm", (1, 16), F32)

        with contextlib.ExitStack() as sA:
            sbl = lambda n, sh, dt: sA.enter_context(nc.sbuf_tensor(n, list(sh), dt))
            wfA = [sbl(f"p2wfA{i}", (P, 4096), F32) for i in range(2)]
            acol = sbl("p2acol", (P, 4), F32)
            rcol = sbl("p2rcol", (P, 1), F32)
            b.call("vector", "memset", acol[:, :], 0.0)
            b.barrier()
            specs = [("wgT", 8, 4096, 0), ("wuT", 8, 4096, 1), ("wdT", 32, 1024, 2)]
            prev_red = {0: 0, 1: 0}
            dsems = {0: dS, 1: c.dS2}
            for name, nsl, wid, idx in specs:
                for s in range(nsl):
                    bi = s % 2
                    b.wait("sync", VS, prev_red[bi])
                    b.dma("sync", wfA[bi][:, 0:wid],
                          dr[name][s * P:(s + 1) * P, :], dsems[bi])
                    b.wait("vector", dsems[bi], dsems[bi].n)
                    b.call("vector", "reduce_sum", rcol[:, :], wfA[bi][:, 0:wid],
                           AXX, apply_absolute_value=True)
                    vv = b.call("vector", "tensor_add", acol[:, idx:idx + 1],
                                acol[:, idx:idx + 1], rcol[:, :])
                    prev_red[bi] = vv
            b.wait("tensor", VS, VS.n)
            for idx, (scol, rcol_i, nelem) in enumerate(
                    ((c.C_WGS, c.C_WGR, 4096 * D), (c.C_WUS, c.C_WUR, 4096 * D),
                     (c.C_WDS, c.C_WDR, 4096 * D))):
                vt = b.call("tensor", "matmul", pb[0][0:1, 0:1],
                            acol[:, idx:idx + 1], c.ones_col[:, :],
                            start=True, stop=True, inc=TS)
                b.wait("vector", TS, vt)
                b.call("vector", "tensor_scalar", sm2[0:1, 0:1], pb[0][0:1, 0:1],
                       1.0 / nelem, 1e-5, ALU.mult, ALU.max)
                vv = b.call("vector", "reciprocal", sm2[0:1, 1:2], sm2[0:1, 0:1])
                b.wait("tensor", VS, vv)
                b.call("tensor", "matmul", pb[1][:, 0:1], c.ones_row[:, :],
                       sm2[0:1, 0:1], start=True, stop=True, inc=TS)
                vt = b.call("tensor", "matmul", pb[2][:, 0:1], c.ones_row[:, :],
                            sm2[0:1, 1:2], start=True, stop=True, inc=TS)
                b.wait("vector", TS, vt)
                b.call("vector", "tensor_copy", c.smalls[:, rcol_i:rcol_i + 1],
                       pb[1][:, 0:1])
                vv = b.call("vector", "tensor_copy", c.smalls[:, scol:scol + 1],
                            pb[2][:, 0:1])
                b.wait("tensor", VS, vv)
            b.barrier()

        with contextlib.ExitStack() as sL:
            sbl = lambda n, sh, dt: sL.enter_context(nc.sbuf_tensor(n, list(sh), dt))
            xln = sbl("p2xln", (P, NCH * T), F32)
            stat = sbl("p2stat", (1, 3 * TT), F32)
            sq = sbl("p2sq", (P, TT), F32)
            xt = sbl("p2xt", (P, NCH * TT), F32)
            mfull = sbl("p2mfull", (P, T), F32)
            tfull = sbl("p2tfull", (P, T), F32)
            rfull = sbl("p2rfull", (P, T), F32)
            q1col = sbl("p2q1col", (P, 1), F32)
            q1row = sbl("p2q1row", (1, T), F32)
            emit_ln_to(c, x_in, xln, stat, sq, xt, 0)
            for ck in range(NCH):
                vs = b.call("scalar", "activation", tfull[:, :],
                            xln[:, ck * T:(ck + 1) * T], AF.Abs, inc=SS)
                b.wait("vector", SS, vs)
                if ck == 0:
                    vv = b.call("vector", "tensor_copy", mfull[:, :], tfull[:, :])
                else:
                    vv = b.call("vector", "tensor_max", mfull[:, :], mfull[:, :],
                                tfull[:, :])
                b.wait("scalar", VS, vv)
            emit_token_max_bcast(c, mfull, q1col, q1row, tfull)
            b.call("vector", "tensor_scalar_max", mfull[:, :], tfull[:, :], 1e-5)
            b.call("vector", "reciprocal", rfull[:, :], mfull[:, :])
            b.call("vector", "tensor_scalar_mul", mfull[:, :], mfull[:, :],
                   1.0 / 127.0)
            for ck in range(NCH):
                b.call("vector", "tensor_mul", tfull[:, :],
                       xln[:, ck * T:(ck + 1) * T], rfull[:, :])
                b.call("vector", "tensor_scalar", tfull[:, :], tfull[:, :],
                       127.0, RND_C, ALU.mult, ALU.add)
                b.call("vector", "tensor_scalar", tfull[:, :], tfull[:, :],
                       RND_C, None, ALU.subtract, ALU.bypass)
                b.call("vector", "tensor_mul", xq[:, ck * T:(ck + 1) * T],
                       tfull[:, :], mfull[:, :])
            b.barrier()

        gu = ph.enter_context(nc.sbuf_tensor("p2gu", [P, 32 * T], BF16))
        c.pcst = [ph.enter_context(nc.sbuf_tensor(f"pcst{i}", [P, 1024], BF16))
                  for i in range(2)]

        with contextlib.ExitStack() as sG:
            sbl = lambda n, sh, dt: sG.enter_context(nc.sbuf_tensor(n, list(sh), dt))
            wf = sbl("p2wf", (P, NCH * 512), F32)
            wgq = sbl("p2wgq", (P, NCH * 512), BF16)
            wuq = sbl("p2wuq", (P, NCH * 512), BF16)
            gsb = sbl("p2gsb", (P, 2 * TT), BF16)
            g_s = {0: 0, 1: 0}
            g_v = {0: 0, 1: 0}
            for og in range(8):
                co = og * 512
                for wname, wq_sb, scol in (("wgT", wgq, c.C_WGS),
                                           ("wuT", wuq, c.C_WUS)):
                    for ck in range(NCH):
                        b.dma("sync", wf[:, ck * 512:(ck + 1) * 512],
                              dr[wname][ck * P:(ck + 1) * P, co:co + 512], dS)
                    b.wait("vector", dS, dS.n)
                    b.call("vector", "tensor_scalar", wf[:, :], wf[:, :],
                           c.smalls[:, scol:scol + 1], RND_C, ALU.mult, ALU.add)
                    b.call("vector", "tensor_scalar", wf[:, :], wf[:, :],
                           RND_C, 1.0, ALU.subtract, ALU.min)
                    vv = b.call("vector", "tensor_scalar", wq_sb[:, :], wf[:, :],
                                -1.0, None, ALU.max, ALU.bypass)
                    b.wait("sync", VS, vv)
                    b.wait("tensor", VS, vv)
                for oc4 in range(4):
                    ic = og * 4 + oc4
                    for tt in range(NTT):
                        gi = ic * NTT + tt
                        bp = 2 * (gi % 2)
                        b.wait("tensor", SS, g_s[gi % 2])
                        b.wait("tensor", VS, g_v[gi % 2])
                        for ck in range(NCH):
                            b.call("tensor", "matmul", pb[bp][:, 0:TT],
                                   wgq[:, ck * 512 + oc4 * P:ck * 512 + (oc4 + 1) * P],
                                   xq[:, ck * T + tt * TT:ck * T + (tt + 1) * TT],
                                   start=(ck == 0), stop=(ck == NCH - 1), inc=TS)
                        mmA = TS.n
                        for ck in range(NCH):
                            b.call("tensor", "matmul", pb[bp + 1][:, 0:TT],
                                   wuq[:, ck * 512 + oc4 * P:ck * 512 + (oc4 + 1) * P],
                                   xq[:, ck * T + tt * TT:ck * T + (tt + 1) * TT],
                                   start=(ck == 0), stop=(ck == NCH - 1), inc=TS)
                        mmB = TS.n
                        b.wait("scalar", TS, mmA)
                        b.wait("scalar", VS, g_v[gi % 2])
                        vs = b.call("scalar", "activation",
                                    gsb[:, (gi % 2) * TT:(gi % 2 + 1) * TT],
                                    pb[bp][:, 0:TT], AF.Silu,
                                    scale=c.smalls[:, c.C_WGR:c.C_WGR + 1],
                                    inc=SS)
                        g_s[gi % 2] = vs
                        b.wait("vector", SS, vs)
                        b.wait("vector", TS, mmB)
                        vv = b.call("vector", "scalar_tensor_tensor",
                                    gu[:, ic * T + tt * TT:ic * T + (tt + 1) * TT],
                                    pb[bp + 1][:, 0:TT],
                                    c.smalls[:, c.C_WUR:c.C_WUR + 1],
                                    gsb[:, (gi % 2) * TT:(gi % 2 + 1) * TT],
                                    ALU.mult, ALU.mult)
                        g_v[gi % 2] = vv
                for _ in range(36):
                    pc_step(c)
            b.barrier()

        with contextlib.ExitStack() as sQ:
            sbl = lambda n, sh, dt: sQ.enter_context(nc.sbuf_tensor(n, list(sh), dt))
            mfull = sbl("p2mful2", (P, T), F32)
            tfull = sbl("p2tful2", (P, T), F32)
            rfull = sbl("p2rful2", (P, T), F32)
            q1col = sbl("p2q2col", (P, 1), F32)
            q1row = sbl("p2q2row", (1, T), F32)
            for ic in range(32):
                vs = b.call("scalar", "activation", tfull[:, :],
                            gu[:, ic * T:(ic + 1) * T], AF.Abs, inc=SS)
                b.wait("vector", SS, vs)
                if ic == 0:
                    vv = b.call("vector", "tensor_copy", mfull[:, :], tfull[:, :])
                else:
                    vv = b.call("vector", "tensor_max", mfull[:, :], mfull[:, :],
                                tfull[:, :])
                b.wait("scalar", VS, vv)
            emit_token_max_bcast(c, mfull, q1col, q1row, tfull)
            b.call("vector", "tensor_scalar_max", mfull[:, :], tfull[:, :], 1e-5)
            b.call("vector", "reciprocal", rfull[:, :], mfull[:, :])
            b.call("vector", "tensor_scalar_mul", rfull[:, :], rfull[:, :], 127.0)
            b.call("vector", "tensor_scalar_mul", mfull[:, :], mfull[:, :],
                   1.0 / 127.0)
            for ic in range(32):
                b.call("vector", "tensor_mul", tfull[:, :],
                       gu[:, ic * T:(ic + 1) * T], rfull[:, :])
                b.call("vector", "tensor_scalar", tfull[:, :], tfull[:, :],
                       RND_C, RND_C, ALU.add, ALU.subtract)
                b.call("vector", "tensor_mul", gu[:, ic * T:(ic + 1) * T],
                       tfull[:, :], mfull[:, :])
            b.barrier()

        with contextlib.ExitStack() as sD:
            sbl = lambda n, sh, dt: sD.enter_context(nc.sbuf_tensor(n, list(sh), dt))
            wdf = sbl("p2wdf", (P, 16 * P), F32)
            wdq = [sbl(f"p2wdq{i}", (P, 32 * P), BF16) for i in range(2)]
            xrel = sbl("p2xrel", (P, TT), F32)
            yst = sbl("p2yst", (P, TT), F32)
            wq_done = {0: 0, 1: 0}
            mm_done = {0: 0, 1: 0}

            def prefD2(oc2):
                bw = oc2 % 2
                b.wait("sync", TS, mm_done[bw])
                for hf in range(2):
                    b.wait("sync", VS, wq_done[1 - bw] if hf == 0 else VS.n)
                    for ck2 in range(16):
                        cka = hf * 16 + ck2
                        b.dma("sync", wdf[:, ck2 * P:(ck2 + 1) * P],
                              dr["wdT"][cka * P:(cka + 1) * P,
                                        oc2 * P:(oc2 + 1) * P], dS)
                    b.wait("vector", dS, dS.n)
                    b.call("vector", "tensor_scalar", wdf[:, :], wdf[:, :],
                           c.smalls[:, c.C_WDS:c.C_WDS + 1], RND_C,
                           ALU.mult, ALU.add)
                    b.call("vector", "tensor_scalar", wdf[:, :], wdf[:, :],
                           RND_C, 1.0, ALU.subtract, ALU.min)
                    wq_done[bw] = b.call("vector", "tensor_scalar",
                                         wdq[bw][:, hf * 16 * P:
                                                 (hf + 1) * 16 * P],
                                         wdf[:, :],
                                         -1.0, None, ALU.max, ALU.bypass)

            prefD2(0)
            prev_ev = {0: 0, 1: 0}
            for oc2 in range(NCH):
                bw = oc2 % 2
                b.wait("tensor", VS, wq_done[bw])
                for tt in range(NTT):
                    bk = tt % 2
                    b.wait("tensor", VS, prev_ev[bk])
                    for ck2 in range(32):
                        b.call("tensor", "matmul", pb[bk][:, 0:TT],
                               wdq[bw][:, ck2 * P:(ck2 + 1) * P],
                               gu[:, ck2 * T + tt * TT:ck2 * T + (tt + 1) * TT],
                               start=(ck2 == 0), stop=(ck2 == 31), inc=TS)
                    mmT = TS.n
                    if tt == 0 and oc2 + 1 < NCH:
                        prefD2(oc2 + 1)
                        for _ in range(12):
                            pc_step(c)
                    b.wait("sync", VS, prev_ev[bk])
                    b.dma("sync", xrel[:, :],
                          x_in[oc2 * P:(oc2 + 1) * P, tt * TT:(tt + 1) * TT],
                          c.dS2)
                    b.wait("vector", c.dS2, c.dS2.n)
                    b.wait("vector", TS, mmT)
                    vv = b.call("vector", "scalar_tensor_tensor", yst[:, :],
                                pb[bk][:, 0:TT],
                                c.smalls[:, c.C_WDR:c.C_WDR + 1], xrel[:, :],
                                ALU.mult, ALU.add)
                    prev_ev[bk] = vv
                    b.wait("sync", VS, vv)
                    b.dma("sync",
                          x_out[oc2 * P:(oc2 + 1) * P, tt * TT:(tt + 1) * TT],
                          yst[:, :], c.dT)
                    b.wait("vector", c.dT, c.dT.n)
                mm_done[bw] = TS.n
            b.barrier()

        pc_drain(c)
        for k in range(4):
            b.wait("vector", c.dPS[k], c.pc_done["w3Tt"][k])

    b.barrier()
    c.x_cur = 1


def emit_phase3(c):
    nc, b, dr = c.nc, c.b, c.dr
    dS, dG, dT, TS, VS, SS, CC = c.dS, c.dG, c.dT, c.TS, c.VS, c.SS, c.CC
    pb = c.pb
    x_in = c.x_hbm[1]
    x_out = c.x_hbm[0]

    with contextlib.ExitStack() as ph:
        sbf = lambda n, sh, dt: ph.enter_context(nc.sbuf_tensor(n, list(sh), dt))
        h_loc = sbf("p3hloc", (P, NCH * T), BF16)
        ctx = sbf("p3ctx", (P, NCH * T), BF16)

        with contextlib.ExitStack() as sL:
            sbl = lambda n, sh, dt: sL.enter_context(nc.sbuf_tensor(n, list(sh), dt))
            stat = sbl("p3stat", (1, 3 * TT), F32)
            sq = sbl("p3sq", (P, TT), F32)
            xt = sbl("p3xt", (P, NCH * TT), F32)
            pet = sbl("p3pet", (P, TT), F32)
            hst = sbl("p3hst", (P, TT), BF16)
            for tt in range(NTT):
                for ck in range(NCH):
                    b.dma("sync", xt[:, ck * TT:(ck + 1) * TT],
                          x_in[ck * P:(ck + 1) * P, tt * TT:(tt + 1) * TT], dS)
                b.wait("tensor", dS, dS.n)
                b.wait("scalar", dS, dS.n)
                srcs = [xt[:, ck * TT:(ck + 1) * TT] for ck in range(NCH)]
                vt = ln_stats_bcast(c, srcs, D, pb[2], pb[3], stat, sq)
                b.wait("vector", TS, vt)
                for ck in range(NCH):
                    b.dma("sync", pet[:, :],
                          dr["pe_in"][ck * P:(ck + 1) * P, tt * TT:(tt + 1) * TT],
                          c.dS2)
                    b.wait("vector", c.dS2, c.dS2.n)
                    b.call("vector", "tensor_sub", xt[:, ck * TT:(ck + 1) * TT],
                           xt[:, ck * TT:(ck + 1) * TT], pb[2][:, 0:TT])
                    b.call("vector", "tensor_mul", xt[:, ck * TT:(ck + 1) * TT],
                           xt[:, ck * TT:(ck + 1) * TT], pb[3][:, 0:TT])
                    vv = b.call("vector", "tensor_add", hst[:, :],
                                xt[:, ck * TT:(ck + 1) * TT], pet[:, :])
                    b.call("vector", "tensor_copy",
                           h_loc[:, ck * T + tt * TT:ck * T + (tt + 1) * TT],
                           hst[:, :])
                    b.wait("sync", VS, VS.n)
                    b.dma("sync",
                          c.cc_h_in[ck * P:(ck + 1) * P, tt * TT:(tt + 1) * TT],
                          hst[:, :], c.dT)
                    b.wait("vector", c.dT, c.dT.n)
                b.barrier()

        vg = b.op("gpsimd", lambda e: e.collective_compute(
            "AllGather", ALU.bypass,
            replica_groups=[[0, 1], [2, 3], [4, 5], [6, 7]],
            ins=[c.cc_h_in.ap()], outs=[c.cc_h_out.ap()]), inc=CC)
        b.wait("gpsimd", CC, vg)

        with contextlib.ExitStack() as s2:
            sbl = lambda n, sh, dt: s2.enter_context(nc.sbuf_tensor(n, list(sh), dt))
            h_full = sbl("p3hfull", (P, NCH * 2 * T), BF16)
            for ck in range(NCH):
                for hf in range(2):
                    b.dma("gpsimd",
                          h_full[:, ck * 2 * T + hf * T:ck * 2 * T + (hf + 1) * T],
                          c.cc_h_out[hf * D + ck * P:hf * D + (ck + 1) * P, :], dG)
            hload_dG = dG.n

            with contextlib.ExitStack() as s3:
                sbm = lambda n, sh, dt: s3.enter_context(nc.sbuf_tensor(n, list(sh), dt))
                wq2 = sbm("p3wq", (P, 2 * NCH * P), BF16)
                wk = sbm("p3wk", (P, NCH * P), BF16)
                wv = sbm("p3wv", (P, NCH * P), BF16)
                biases = sbm("p3bias", (P, 4 * NCH), F32)
                qp_all = sbm("p3qp", (P, NCH * T), BF16)
                kp = sbm("p3kp", (P, 2 * T), BF16)
                vp = sbm("p3vp", (P, 2 * T), BF16)
                vtm = sbm("p3vtm", (P, 32 * 130), BF16)
                pt = sbm("p3pt", (P, 4 * TT), BF16)
                rdrow = sbm("p3rd", (P, TT), F32)
                ctmp0 = sbm("p3ctmp0", (64, TT), BF16)
                ctmp1 = sbm("p3ctmp1", (64, TT), BF16)
                ctmp2 = sbm("p3ctmp2", (64, TT), BF16)

                b.dma("sync", biases[:, 0:NCH], dr["bq1"].ap(), dS)
                b.dma("sync", biases[:, NCH:2 * NCH], dr["bk1"].ap(), dS)
                b.dma("sync", biases[:, 2 * NCH:3 * NCH], dr["bv1"].ap(), dS)
                b.dma("sync", biases[:, 3 * NCH:4 * NCH], dr["bo1"].ap(), dS)
                b.wait("tensor", dS, dS.n)
                b.wait("scalar", dS, dS.n)
                b.wait("vector", dS, dS.n)

                b.call("vector", "memset",
                       vtm.ap().rearrange("p (a c) -> p a c", c=65)[:, :, 64:65], 1.0)

                for k in range(4):
                    b.wait("sync", c.dPS[k], c.pc_done["wq1t"][k])
                q_guard = {0: 0, 1: 0}
                dwv = {0: 0, 1: 0}
                prev_q = {0: 0, 1: 0}
                for p in range(NCH):
                    b2 = p % 2
                    b.wait("sync", TS, q_guard[b2])
                    b.dma("sync",
                          wq2[:, b2 * NCH * P:(b2 + 1) * NCH * P]
                          .rearrange("q (a m) -> q a m", a=NCH),
                          c.wqbt[p, :, :, :].rearrange("a p m -> p a m"),
                          c.dW[b2])
                    dwv[b2] = c.dW[b2].n
                    b.wait("tensor", c.dW[b2], dwv[b2])
                    for tt in range(NTT):
                        bk = tt % 2
                        b.wait("tensor", VS, prev_q[bk])
                        for ck in range(NCH):
                            b.call("tensor", "matmul", pb[bk][:, 0:TT],
                                   wq2[:, (b2 * NCH + ck) * P:
                                       (b2 * NCH + ck + 1) * P],
                                   h_loc[:, ck * T + tt * TT:ck * T + (tt + 1) * TT],
                                   start=(ck == 0), stop=(ck == NCH - 1), inc=TS)
                        b.wait("vector", TS, TS.n)
                        vs = b.call("vector", "tensor_scalar",
                                    qp_all[:, p * T + tt * TT:p * T + (tt + 1) * TT],
                                    pb[bk][:, 0:TT],
                                    biases[:, p:p + 1], None, ALU.add, ALU.bypass)
                        prev_q[bk] = vs
                    q_guard[b2] = TS.n

                b.wait("tensor", CC, vg)
                b.wait("tensor", dG, hload_dG)
                for k in range(4):
                    b.wait("sync", c.dPS[k], c.pc_done["wv1t"][k])
                b.barrier()

                prev_kv = {0: 0, 1: 0}
                kv_guard = 0
                c.pv_guard = 0
                c.dcx_prev = 0
                for p in range(NCH):
                    b.wait("sync", TS, kv_guard)
                    b.dma("sync", wk.ap().rearrange("q (a m) -> q a m", a=NCH),
                          c.wkbt[p, :, :, :].rearrange("a p m -> p a m"), c.dW[2])
                    b.dma("sync", wv.ap().rearrange("q (a m) -> q a m", a=NCH),
                          c.wvbt[p, :, :, :].rearrange("a p m -> p a m"), c.dW[3])
                    b.wait("tensor", c.dW[2], c.dW[2].n)
                    b.wait("tensor", c.dW[3], c.dW[3].n)
                    for tg in range(2 * NTT):
                        for j, (dst, wmat, bcol) in enumerate(
                                ((kp, wk, NCH + p), (vp, wv, 2 * NCH + p))):
                            bi = 2 + (tg * 2 + j) % 2
                            b.wait("tensor", VS, prev_kv[bi - 2])
                            for ck in range(NCH):
                                b.call("tensor", "matmul", pb[bi][:, 0:TT],
                                       wmat[:, ck * P:(ck + 1) * P],
                                       h_full[:, ck * 2 * T + tg * TT:
                                              ck * 2 * T + (tg + 1) * TT],
                                       start=(ck == 0), stop=(ck == NCH - 1), inc=TS)
                            b.wait("vector", TS, TS.n)
                            vs = b.call("vector", "tensor_scalar",
                                        dst[:, tg * TT:(tg + 1) * TT],
                                        pb[bi][:, 0:TT],
                                        biases[:, bcol:bcol + 1], None,
                                        ALU.add, ALU.bypass)
                            prev_kv[bi - 2] = vs
                    kv_guard = TS.n
                    b.wait("tensor", VS, VS.n)
                    for kc in range(32):
                        vt = b.call("tensor", "transpose", c.pbt[:, 0:P],
                                    vp[:, kc * P:(kc + 1) * P], c.identbf[:, :],
                                    inc=TS)
                        b.wait("vector", TS, vt)
                        b.call("vector", "tensor_copy",
                               vtm[:, kc * 130:kc * 130 + 64], c.pbt[:, 0:64])
                        vv = b.call("vector", "tensor_copy",
                                    vtm[:, kc * 130 + 65:kc * 130 + 129],
                                    c.pbt[:, 64:128])
                        b.wait("tensor", VS, vv)

                    for qt in range(NTT):
                        qk_T = {}
                        pv_T = {}
                        exp_v = {}
                        for w in range(33):
                            bi = w % 2
                            if w < 32:
                                b.wait("tensor", SS, exp_v.get(w - 2, 0))
                                if w == 0:
                                    b.wait("tensor", VS, c.pv_guard)
                                b.call("tensor", "matmul", pb[2 * bi][:, 0:TT],
                                       kp[0:64, w * P:(w + 1) * P],
                                       qp_all[0:64, p * T + qt * TT:
                                              p * T + (qt + 1) * TT],
                                       start=True, stop=True,
                                       tile_position=(0, 0), inc=TS)
                                b.call("tensor", "matmul", pb[2 * bi + 1][:, 0:TT],
                                       kp[64:128, w * P:(w + 1) * P],
                                       qp_all[64:128, p * T + qt * TT:
                                              p * T + (qt + 1) * TT],
                                       start=True, stop=True,
                                       tile_position=(64, 0), inc=TS)
                                qk_T[w] = TS.n
                            if w > 0:
                                wp = w - 1
                                bip = wp % 2
                                b.wait("tensor", SS, exp_v[wp])
                                b.call("tensor", "matmul", pb[4][0:65, 0:TT],
                                       vtm[:, wp * 130:wp * 130 + 65],
                                       pt[:, 2 * bip * TT:(2 * bip + 1) * TT],
                                       start=(wp == 0), stop=(wp == 31), inc=TS)
                                b.call("tensor", "matmul", pb[5][0:65, 0:TT],
                                       vtm[:, wp * 130 + 65:wp * 130 + 130],
                                       pt[:, (2 * bip + 1) * TT:(2 * bip + 2) * TT],
                                       start=(wp == 0), stop=(wp == 31), inc=TS)
                                pv_T[wp] = TS.n
                            if w < 32:
                                b.wait("scalar", TS, qk_T[w])
                                b.wait("scalar", TS, pv_T.get(w - 2, 0))
                                vs = b.call("scalar", "activation",
                                            pt[:, 2 * bi * TT:(2 * bi + 2) * TT],
                                            c.parena[:, 2 * bi * TT:
                                                     (2 * bi + 2) * TT],
                                            AF.Exp, scale=0.125,
                                            bias=c.smalls[:, c.C_NEG8:
                                                          c.C_NEG8 + 1],
                                            inc=SS)
                                exp_v[w] = vs
                        b.wait("vector", TS, pv_T[31])
                        r0 = b.call("vector", "reciprocal",
                                    rdrow[64:65, 0:TT], pb[4][64:65, 0:TT])
                        b.call("vector", "tensor_copy", ctmp0[0:64, :],
                               pb[4][0:64, 0:TT])
                        cpv = b.call("vector", "tensor_copy", ctmp1[0:64, :],
                                     pb[5][0:64, 0:TT])
                        c.pv_guard = cpv
                        b.wait("tensor", VS, r0)
                        bc0 = b.call("tensor", "matmul", pb[6][0:64, 0:TT],
                                     c.ones_mat[64:65, 0:64],
                                     rdrow[64:65, 0:TT],
                                     start=True, stop=True, inc=TS)
                        b.wait("vector", TS, bc0)
                        m0 = b.call("vector", "tensor_mul",
                                    ctx[0:64, p * T + qt * TT:
                                        p * T + (qt + 1) * TT],
                                    ctmp0[0:64, :], pb[6][0:64, 0:TT])
                        r1 = b.call("vector", "reciprocal",
                                    rdrow[64:65, 0:TT], pb[5][64:65, 0:TT])
                        b.wait("tensor", VS, r1)
                        bc1 = b.call("tensor", "matmul", pb[6][0:64, 0:TT],
                                     c.ones_mat[64:65, 0:64],
                                     rdrow[64:65, 0:TT],
                                     start=True, stop=True, inc=TS)
                        b.wait("vector", TS, bc1)
                        b.wait("vector", c.dCX, c.dcx_prev)
                        m1 = b.call("vector", "tensor_mul", ctmp2[0:64, :],
                                    ctmp1[0:64, :], pb[6][0:64, 0:TT])
                        b.wait("sync", VS, m1)
                        b.dma("sync",
                              ctx[64:128, p * T + qt * TT:p * T + (qt + 1) * TT],
                              ctmp2[0:64, :], c.dCX)
                        c.dcx_prev = c.dCX.n
                    b.barrier()

        with contextlib.ExitStack() as s4:
            sbl = lambda n, sh, dt: s4.enter_context(nc.sbuf_tensor(n, list(sh), dt))
            wo = sbl("p3wo", (P, NCH * D), BF16)
            yf = sbl("p3y", (P, NCH * T), F32)
            bias4 = sbl("p3bo", (P, 3 * NCH), F32)
            stat = sbl("p3stat2", (1, 3 * TT), F32)
            sq = sbl("p3sq2", (P, TT), F32)
            xrel = sbl("p3xrel", (P, TT), F32)
            yst = sbl("p3yst", (P, TT), F32)
            for k in range(4):
                b.wait("sync", c.dPS[k], c.pc_done["wo1t"][k])
            for pp in range(NCH):
                b.dma("sync", wo[:, pp * D:(pp + 1) * D]
                      .rearrange("q (a m) -> q a m", a=NCH),
                      c.wobt[pp, :, :, :].rearrange("a p m -> p a m"), dS)
            b.wait("tensor", c.dCX, c.dcx_prev)
            b.dma("sync", bias4[:, 0:NCH], dr["bo1"].ap(), dS)
            b.dma("sync", bias4[:, NCH:2 * NCH], dr["gam1"].ap(), dS)
            b.dma("sync", bias4[:, 2 * NCH:3 * NCH], dr["bet1"].ap(), dS)
            b.wait("tensor", dS, dS.n)
            b.wait("scalar", dS, dS.n)
            b.wait("vector", dS, dS.n)
            prev_read = {0: 0, 1: 0}
            for oc in range(NCH):
                for tt in range(NTT):
                    bk = (oc * NTT + tt) % 2
                    b.wait("tensor", VS, prev_read[bk])
                    for pp in range(NCH):
                        b.call("tensor", "matmul", pb[bk][:, 0:TT],
                               wo[:, pp * D + oc * P:pp * D + (oc + 1) * P],
                               ctx[:, pp * T + tt * TT:pp * T + (tt + 1) * TT],
                               start=(pp == 0), stop=(pp == NCH - 1), inc=TS)
                    mmT = TS.n
                    b.wait("vector", TS, mmT)
                    vv = b.call("vector", "scalar_tensor_tensor",
                                yf[:, oc * T + tt * TT:oc * T + (tt + 1) * TT],
                                pb[bk][:, 0:TT], bias4[:, oc:oc + 1],
                                h_loc[:, oc * T + tt * TT:oc * T + (tt + 1) * TT],
                                ALU.add, ALU.add)
                    prev_read[bk] = vv
            b.barrier()
            for tt in range(NTT):
                srcs = [yf[:, ck * T + tt * TT:ck * T + (tt + 1) * TT]
                        for ck in range(NCH)]
                vt = ln_stats_bcast(c, srcs, D, pb[2], pb[3], stat, sq)
                b.wait("vector", TS, vt)
                for ck in range(NCH):
                    b.dma("sync", xrel[:, :],
                          x_in[ck * P:(ck + 1) * P, tt * TT:(tt + 1) * TT], c.dS2)
                    b.call("vector", "tensor_sub", sq[:, :],
                           yf[:, ck * T + tt * TT:ck * T + (tt + 1) * TT],
                           pb[2][:, 0:TT])
                    b.call("vector", "tensor_mul", sq[:, :], sq[:, :],
                           pb[3][:, 0:TT])
                    b.call("vector", "tensor_scalar", sq[:, :], sq[:, :],
                           bias4[:, NCH + ck:NCH + ck + 1],
                           bias4[:, 2 * NCH + ck:2 * NCH + ck + 1],
                           ALU.mult, ALU.add)
                    b.wait("vector", c.dS2, c.dS2.n)
                    vv = b.call("vector", "tensor_add", yst[:, :], sq[:, :],
                                xrel[:, :])
                    b.wait("sync", VS, vv)
                    b.dma("sync",
                          x_out[ck * P:(ck + 1) * P, tt * TT:(tt + 1) * TT],
                          yst[:, :], c.dT)
                    b.wait("vector", c.dT, c.dT.n)
                b.barrier()

    b.barrier()
    c.x_cur = 0


def emit_phase4(c):
    nc, b, dr = c.nc, c.b, c.dr
    dS, dT, TS, VS, SS = c.dS, c.dT, c.TS, c.VS, c.SS
    pb = c.pb
    x_in = c.x_hbm[0]
    RW = 32 * TT

    with contextlib.ExitStack() as ph:
        sbf = lambda n, sh, dt: ph.enter_context(nc.sbuf_tensor(n, list(sh), dt))
        xln = sbf("p4xln", (P, NCH * T), BF16)
        ha = sbf("p4ha", (P, 4 * RW), BF16)
        scal = sbf("p4scal", (P, 3 * 32 + NCH), F32)

        b.dma("sync", scal[:, 0:32], dr["s1c"].ap(), dS)
        b.dma("sync", scal[:, 32:64], dr["s2c"].ap(), dS)
        b.dma("sync", scal[:, 64:96], dr["sh1c"].ap(), dS)
        b.dma("sync", scal[:, 96:96 + NCH], dr["s3c"].ap(), dS)
        b.wait("sync", dS, dS.n)
        b.wait("scalar", dS, dS.n)
        b.wait("vector", dS, dS.n)

        with contextlib.ExitStack() as sLn:
            sbn = lambda n, sh, dt: sLn.enter_context(nc.sbuf_tensor(n, list(sh), dt))
            stat = sbn("p4stat", (1, 3 * TT), F32)
            sq = sbn("p4sq", (P, TT), F32)
            xt = sbn("p4xt", (P, NCH * TT), F32)
            emit_ln_to(c, x_in, xln, stat, sq, xt, 0)

        hsl = [sbf(f"p4hsl{i}", (P, 32 * P), BF16) for i in range(4)]

        regions = [(xln, 0), (ha, 0), (ha, RW), (ha, 2 * RW), (ha, 3 * RW)]

        assert c.pc_state["si"] == len(c.pc_jobs), "pre-cast incomplete"
        pc_done = c.pc_done

        g_mm = {i: 0 for i in range(4)}
        g_ps_s = {0: 0, 1: 0}
        g_ps_v = {0: 0, 1: 0}
        dg_val = {i: 0 for i in range(4)}

        def prefA(ic):
            bi = ic % 4
            sem = c.dW[bi]
            b.wait("sync", TS, g_mm[bi])
            b.dma("sync", hsl[bi][:, 0:NCH * P].rearrange("q (a m) -> q a m", a=NCH),
                  c.w1bt[ic, :, :, :].rearrange("a p m -> p a m"), sem)
            b.dma("sync", hsl[bi][:, NCH * P:2 * NCH * P]
                  .rearrange("q (a m) -> q a m", a=NCH),
                  c.w2bt[ic, :, :, :].rearrange("a p m -> p a m"), sem)
            dg_val[bi] = sem.n

        with contextlib.ExitStack() as sA:
            t1 = sA.enter_context(nc.sbuf_tensor("p4t1", [P, 2 * TT], F32))
            for k in range(4):
                b.wait("sync", c.dPS[k], pc_done["w2Tt"][k])
            for ic in range(4):
                prefA(ic)
            for ic in range(32):
                bi = ic % 4
                b.wait("tensor", c.dW[bi], dg_val[bi])
                for tt in range(NTT):
                    gi = ic * NTT + tt
                    b2 = gi % 2
                    pa, pbk = pb[2 * b2], pb[2 * b2 + 1]
                    rten, rbase = regions[1 + tt]
                    b.wait("tensor", SS, g_ps_s[b2])
                    b.wait("tensor", VS, g_ps_v[b2])
                    for ck in range(NCH):
                        b.call("tensor", "matmul", pa[:, 0:TT],
                               hsl[bi][:, ck * P:(ck + 1) * P],
                               xln[:, ck * T + tt * TT:ck * T + (tt + 1) * TT],
                               start=(ck == 0), stop=(ck == NCH - 1), inc=TS)
                    mmA = TS.n
                    for ck in range(NCH):
                        b.call("tensor", "matmul", pbk[:, 0:TT],
                               hsl[bi][:, (NCH + ck) * P:(NCH + ck + 1) * P],
                               xln[:, ck * T + tt * TT:ck * T + (tt + 1) * TT],
                               start=(ck == 0), stop=(ck == NCH - 1), inc=TS)
                    mmB = TS.n
                    b.wait("scalar", TS, mmA)
                    b.wait("scalar", VS, g_ps_v[b2])
                    vs = b.call("scalar", "activation",
                                t1[:, b2 * TT:(b2 + 1) * TT], pa[:, 0:TT],
                                AF.Silu, scale=scal[:, ic:ic + 1], inc=SS)
                    g_ps_s[b2] = vs
                    b.wait("vector", TS, mmB)
                    b.wait("vector", SS, vs)
                    vv = b.call("vector", "scalar_tensor_tensor",
                                rten[:, rbase + ic * TT:rbase + (ic + 1) * TT],
                                pbk[:, 0:TT], scal[:, 32 + ic:32 + ic + 1],
                                t1[:, b2 * TT:(b2 + 1) * TT], ALU.mult, ALU.mult)
                    g_ps_v[b2] = vv
                g_mm[bi] = TS.n
                if ic + 4 < 32:
                    prefA(ic + 4)
        b.barrier()

        g_mm = {i: 0 for i in range(4)}
        g_ps_s = {i: 0 for i in range(4)}
        g_ps_v = {i: 0 for i in range(4)}
        dw_val = {i: 0 for i in range(4)}

        def mk_pref(wsrc, n_oc):
            def pref(i):
                oc = i % n_oc
                bi = i % 4
                b.wait("sync", TS, g_mm[bi])
                b.dma("sync", hsl[bi].ap().rearrange("q (a m) -> q a m", a=32),
                      wsrc[oc, :, :, :].rearrange("a p m -> p a m"), c.dW[bi])
                dw_val[bi] = c.dW[bi].n
            return pref

        prev = regions[1:5]
        for stage, wsrc, pcn, scol in (("B", c.h1bt, "h1Tt", None),
                                       ("C", c.h2bt, "h2Tt", "sh2c")):
            if scol is not None:
                b.wait("sync", SS, SS.n)
                b.dma("sync", scal[:, 64:96], dr[scol].ap(), dS)
                b.wait("sync", dS, dS.n)
                b.wait("scalar", dS, dS.n)
            cur = ([regions[0], regions[1], regions[2], regions[3]] if stage == "B"
                   else [regions[4], regions[0], regions[1], regions[2]])
            for k in range(4):
                b.wait("sync", c.dPS[k], pc_done[pcn][k])
            pref = mk_pref(wsrc, 32)
            for i in range(4):
                pref(i)
            for tt in range(NTT):
                sten, sbase = prev[tt]
                rten, rbase = cur[tt]
                for oc in range(32):
                    i = tt * 32 + oc
                    bi = i % 4
                    b.wait("tensor", c.dW[bi], dw_val[bi])
                    b.wait("tensor", SS, g_ps_s[bi])
                    b.wait("tensor", VS, g_ps_v[bi])
                    for ick in range(32):
                        b.call("tensor", "matmul", pb[bi][:, 0:TT],
                               hsl[bi][:, ick * P:(ick + 1) * P],
                               sten[:, sbase + ick * TT:sbase + (ick + 1) * TT],
                               start=(ick == 0), stop=(ick == 31), inc=TS)
                    g_mm[bi] = TS.n
                    if i + 4 < 128:
                        pref(i + 4)
                    b.wait("scalar", TS, g_mm[bi])
                    vs = b.call("scalar", "activation",
                                rten[:, rbase + oc * TT:rbase + (oc + 1) * TT],
                                pb[bi][:, 0:TT], AF.Gelu,
                                scale=scal[:, 64 + oc:64 + oc + 1], inc=SS)
                    g_ps_s[bi] = vs
            prev = cur

        with contextlib.ExitStack() as sD:
            xrel = sD.enter_context(nc.sbuf_tensor("p4xrel", [P, TT], F32))
            yst = sD.enter_context(nc.sbuf_tensor("p4yst", [P, TT], F32))
            for k in range(4):
                b.wait("sync", c.dPS[k], pc_done["w3Tt"][k])
            pref = mk_pref(c.w3bt, NCH)
            prev_stt = 0
            for i in range(4):
                pref(i)
            for tt in range(NTT):
                sten, sbase = prev[tt]
                for oc2 in range(NCH):
                    i = tt * NCH + oc2
                    bi = i % 4
                    b.wait("tensor", c.dW[bi], dw_val[bi])
                    b.wait("tensor", SS, g_ps_s[bi])
                    b.wait("tensor", VS, g_ps_v[bi])
                    for ick in range(32):
                        b.call("tensor", "matmul", pb[bi][:, 0:TT],
                               hsl[bi][:, ick * P:(ick + 1) * P],
                               sten[:, sbase + ick * TT:sbase + (ick + 1) * TT],
                               start=(ick == 0), stop=(ick == 31), inc=TS)
                    g_mm[bi] = TS.n
                    if i + 4 < 32:
                        pref(i + 4)
                    b.wait("sync", VS, prev_stt)
                    b.dma("sync", xrel[:, :],
                          x_in[oc2 * P:(oc2 + 1) * P, tt * TT:(tt + 1) * TT],
                          c.dS2)
                    b.wait("vector", c.dS2, c.dS2.n)
                    b.wait("vector", TS, g_mm[bi])
                    vv = b.call("vector", "scalar_tensor_tensor", yst[:, :],
                                pb[bi][:, 0:TT],
                                scal[:, 96 + oc2:96 + oc2 + 1], xrel[:, :],
                                ALU.mult, ALU.add)
                    g_ps_v[bi] = vv
                    prev_stt = vv
                    b.wait("sync", VS, vv)
                    b.dma("sync",
                          c.y_out[oc2 * P:(oc2 + 1) * P, tt * TT:(tt + 1) * TT],
                          yst[:, :], dT)
                    b.wait("vector", dT, dT.n)
            b.barrier()

    b.barrier()
    c.x_cur = -1


def emit_finish(c):
    if PHASES >= 4:
        return
    b = c.b
    src = c.x_hbm[c.x_cur] if hasattr(c, "x_cur") else c.dr["x_in"]
    with contextlib.ExitStack() as ph:
        t = ph.enter_context(c.nc.sbuf_tensor("ycopy", [P, TT], F32))
        for ck in range(NCH):
            for q4 in range(NTT):
                b.dma("sync", t[:, :],
                      src[ck * P:(ck + 1) * P, q4 * TT:(q4 + 1) * TT], c.dS)
                b.wait("sync", c.dS, c.dS.n)
                b.dma("sync", c.y_out[ck * P:(ck + 1) * P, q4 * TT:(q4 + 1) * TT],
                      t[:, :], c.dT)
                b.wait("sync", c.dT, c.dT.n)
        b.barrier()


_NC_CACHE = [None]


def kernel(**inputs):
    maps = prep_inputs(inputs)
    if _NC_CACHE[0] is None:
        _NC_CACHE[0] = build_nc()
    nc = _NC_CACHE[0]
    res = run_bass_kernel_spmd(nc, maps, core_ids=list(range(8)))
    out = np.empty((4, SEQ, D), np.float32)
    for cc in range(8):
        bi, half = divmod(cc, 2)
        y = np.asarray(res.results[cc]["y"])
        out[bi, half * T:(half + 1) * T, :] = y.T
    return out


# version 16
# speedup vs baseline: 1.0664x; 1.0664x over previous
import contextlib
import numpy as np

import concourse.bass as bass
import concourse.mybir as mybir
import concourse.library_config as LC
from concourse.bass_isa import ReduceOp
from concourse.bass_utils import run_bass_kernel_spmd

F32 = mybir.dt.float32
BF16 = mybir.dt.bfloat16
AF = mybir.ActivationFunctionType
ALU = mybir.AluOpType
AXX = mybir.AxisListType.X

P = 128
T = 2048
D = 1024
NCH = 8
NTC = 16
TT = 512
NTT = 4
RND_C = 12582912.0
EPS = 1e-5

NH = 4
DK = 128
DV = 256
CH = 32
NCK = 64

HEADS = 16
DH = 64
SEQ = 4096

PHASES = 4
PHASE_BARR = []
P4STOP = None


class Sem:
    def __init__(self, h):
        self.h = h
        self.n = 0

    def bump(self, k=1):
        self.n += k
        return self.n


class Builder:
    ENGS = ("tensor", "vector", "scalar", "gpsimd", "sync")

    def __init__(self, nc, es):
        self.nc = nc
        self.es = es
        self.q = {e: [] for e in self.ENGS}
        self.last_wait = {}
        self.sems = {}

    def sem(self, name):
        if name not in self.sems:
            self.sems[name] = Sem(self.es.enter_context(self.nc.semaphore(name)))
        return self.sems[name]

    def wait(self, eng, sem, val):
        if val is None or val <= 0:
            return
        key = (eng, sem.h.name)
        if self.last_wait.get(key, -1) >= val:
            return
        self.last_wait[key] = val
        self.q[eng].append(lambda e, s=sem, v=val: e.wait_ge(s.h, v))

    def op(self, eng, fn, inc=None, k=1, serial=True):
        ser = getattr(self, "serial", {}).get(eng)
        if serial and ser is not None:
            self.wait(eng, ser, ser.n)
        if inc is None and ser is not None:
            inc = ser
        if inc is not None:
            v = inc.bump(k)
            def run(e, fn=fn, s=inc, kk=k):
                fn(e).then_inc(s.h, kk)
            self.q[eng].append(run)
            return v
        self.q[eng].append(lambda e, fn=fn: fn(e))
        return None

    def call(self, eng, meth, *args, inc=None, k=1, serial=True, **kwargs):
        def fn(e, meth=meth, args=args, kwargs=kwargs):
            return getattr(e, meth)(*args, **kwargs)
        return self.op(eng, fn, inc=inc, k=k, serial=serial)

    def dma(self, eng, dst, src, sem):
        return self.op(eng, lambda e, d=dst, s=src: e.dma_start(out=d, in_=s),
                       inc=sem, k=16, serial=False)

    def barrier(self):
        bs = self.sem("BARR")
        v = bs.n + len(self.ENGS)
        for e in self.ENGS:
            ser = getattr(self, "barrier_serial", {}).get(e)
            if ser is not None:
                self.wait(e, ser, ser.n)
            self.q[e].append(lambda eng, s=bs: eng.nop().then_inc(s.h, 1))
            self.q[e].append(lambda eng, s=bs, vv=v: eng.wait_ge(s.h, vv))
            self.last_wait[(e, bs.h.name)] = v
        bs.n = v

    def replay(self, block):
        for name in self.ENGS:
            ops = self.q[name]
            getattr(block, name)(lambda eng, ops=ops: [f(eng) for f in ops])


def _ct(a):
    return np.ascontiguousarray(np.asarray(a))


def _pe_const():
    pos = np.arange(SEQ, dtype=np.float64)[:, None]
    div = np.exp(np.arange(0, D, 2, dtype=np.float64) * (-np.log(10000.0) / D))
    pe = np.zeros((SEQ, D), np.float32)
    pe[:, 0::2] = np.sin(pos * div).astype(np.float32)
    pe[:, 1::2] = np.cos(pos * div).astype(np.float32)
    return pe


def prep_inputs(inp):
    x = np.asarray(inp["L0_attn_hidden_states"])

    def fm(mat):
        return _ct(mat.T)

    wx2 = np.asarray(inp["L0_attn_Wx2"]).reshape(D, 5, 32)
    xb = np.asarray(inp["L0_attn_x_bias"])
    wx2T = np.stack([np.concatenate([_ct(wx2[:, g, :].T), xb[g][None, :]], axis=0)
                     for g in range(5)], axis=0)

    common = {
        "wx0T": _ct(np.asarray(inp["L0_attn_Wx0"]).T),
        "wx2T": _ct(wx2T),
        "rWT": _ct(np.asarray(inp["L0_attn_rW"]).T),
        "wAT": _ct(np.asarray(inp["L0_attn_wA"]).T),
        "wBT": _ct(np.asarray(inp["L0_attn_wB"]).T),
        "kWT": _ct(np.asarray(inp["L0_attn_kW"]).T),
        "vWT": _ct(np.asarray(inp["L0_attn_vW"]).T),
        "gAT": _ct(np.asarray(inp["L0_attn_gA"]).T),
        "gBT": _ct(np.asarray(inp["L0_attn_gB"]).T),
        "oWT": _ct(np.asarray(inp["L0_attn_oW"]).T),
        "mu0c": _ct(np.asarray(inp["L0_attn_mu0"]).reshape(NCH, P).T),
        "wbc": _ct(np.asarray(inp["L0_attn_wb"]).reshape(NH, P).T),
        "gbr": _ct(np.asarray(inp["L0_attn_gb"])[None, :]),
        "uc": _ct(np.asarray(inp["L0_attn_bonus"]).T),
        "lngr": _ct(np.asarray(inp["L0_attn_ln_g"])[None, :]),
        "lnbr": _ct(np.asarray(inp["L0_attn_ln_b"])[None, :]),
        "wgT": _ct(np.asarray(inp["L0_ffn_w_gate"]).T),
        "wuT": _ct(np.asarray(inp["L0_ffn_w_up"]).T),
        "wdT": _ct(np.asarray(inp["L0_ffn_w_down"]).T),
        "wq1t": _ct(np.asarray(inp["L1_attn_Wq"]).reshape(8, 128, 8, 128)
                     .transpose(2, 0, 1, 3)),
        "wk1t": _ct(np.asarray(inp["L1_attn_Wk"]).reshape(8, 128, 8, 128)
                     .transpose(2, 0, 1, 3)),
        "wv1t": _ct(np.asarray(inp["L1_attn_Wv"]).reshape(8, 128, 8, 128)
                     .transpose(2, 0, 1, 3)),
        "wo1t": _ct(np.asarray(inp["L1_attn_Wo"]).reshape(8, 128, 8, 128)
                     .transpose(0, 2, 1, 3)),
        "bq1": _ct(np.asarray(inp["L1_attn_bq"]).reshape(NCH, P).T),
        "bk1": _ct(np.asarray(inp["L1_attn_bk"]).reshape(NCH, P).T),
        "bv1": _ct(np.asarray(inp["L1_attn_bv"]).reshape(NCH, P).T),
        "bo1": _ct(np.asarray(inp["L1_attn_bo"]).reshape(NCH, P).T),
        "gam1": _ct(np.asarray(inp["L1_attn_gamma"]).reshape(NCH, P).T),
        "bet1": _ct(np.asarray(inp["L1_attn_beta"]).reshape(NCH, P).T),
        "w1Tt": _ct(np.asarray(inp["L1_ffn_w1_w"]).reshape(32, 128, 8, 128)
                    .transpose(0, 2, 3, 1)),
        "w2Tt": _ct(np.asarray(inp["L1_ffn_w2_w"]).reshape(32, 128, 8, 128)
                    .transpose(0, 2, 3, 1)),
        "h1Tt": _ct(np.asarray(inp["L1_ffn_h1_w"]).reshape(32, 128, 32, 128)
                    .transpose(0, 2, 3, 1)),
        "h2Tt": _ct(np.asarray(inp["L1_ffn_h2_w"]).reshape(32, 128, 32, 128)
                    .transpose(0, 2, 3, 1)),
        "w3Tt": _ct(np.asarray(inp["L1_ffn_w3_w"]).reshape(8, 128, 32, 128)
                    .transpose(0, 2, 3, 1)),
        "s1c": _ct(np.asarray(inp["L1_ffn_w1_s"]).reshape(32, P).T),
        "s2c": _ct(np.asarray(inp["L1_ffn_w2_s"]).reshape(32, P).T),
        "sh1c": _ct(np.asarray(inp["L1_ffn_h1_s"]).reshape(32, P).T),
        "sh2c": _ct(np.asarray(inp["L1_ffn_h2_s"]).reshape(32, P).T),
        "s3c": _ct(np.asarray(inp["L1_ffn_w3_s"]).reshape(NCH, P).T),
    }

    pe = _pe_const()
    maps = []
    for c in range(8):
        b, half = divmod(c, 2)
        t0 = half * T
        m = dict(common)
        m["x_in"] = fm(x[b, t0:t0 + T, :])
        if half == 0:
            m["xprev_in"] = np.zeros((P, NCH), np.float32)
            m["flag"] = np.zeros((P, 1), np.float32)
        else:
            m["xprev_in"] = _ct(x[b, t0 - 1, :].reshape(NCH, P).T)
            m["flag"] = np.ones((P, 1), np.float32)
        m["pe_in"] = fm(pe[t0:t0 + T, :])
        maps.append(m)
    return maps


class Ctx:
    pass


def _declare_io(nc, c):
    dr = {}
    def din(name, shape, dt=F32):
        dr[name] = nc.dram_tensor(name, list(shape), dt, kind="ExternalInput")
    c.dr = dr
    din("x_in", (D, T)); din("xprev_in", (P, NCH)); din("flag", (P, 1))
    din("pe_in", (D, T))
    din("wx0T", (D, 160)); din("wx2T", (5, 33, D))
    din("rWT", (D, 512)); din("wAT", (D, 64)); din("wBT", (64, 512))
    din("kWT", (D, 512)); din("vWT", (D, D)); din("gAT", (D, 64))
    din("gBT", (64, D)); din("oWT", (D, D))
    din("mu0c", (P, NCH)); din("wbc", (P, NH)); din("gbr", (1, D))
    din("uc", (P, NH)); din("lngr", (1, D)); din("lnbr", (1, D))
    din("wgT", (D, 4096)); din("wuT", (D, 4096)); din("wdT", (4096, D))
    for n in ("wq1t", "wk1t", "wv1t", "wo1t"):
        din(n, (NCH, NCH, P, P))
    for n in ("bq1", "bk1", "bv1", "bo1", "gam1", "bet1"):
        din(n, (P, NCH))
    din("w1Tt", (32, 8, P, P)); din("w2Tt", (32, 8, P, P))
    din("h1Tt", (32, 32, P, P)); din("h2Tt", (32, 32, P, P))
    din("w3Tt", (8, 32, P, P))
    for n in ("s1c", "s2c", "sh1c", "sh2c"):
        din(n, (P, 32))
    din("s3c", (P, NCH))

    c.y_out = nc.dram_tensor("y", [D, T], F32, kind="ExternalOutput")
    c.x_hbm = [nc.dram_tensor(f"x_hbm{i}", [D, T], F32, kind="Internal")
               for i in range(2)]
    c.cc_s_in = nc.dram_tensor("cc_s_in", [DK, NH * DV], F32, kind="Internal")
    c.cc_s_out = nc.dram_tensor("cc_s_out", [2 * DK, NH * DV], F32, kind="Internal")
    c.cc_h_in = nc.dram_tensor("cc_h_in", [D, T], BF16, kind="Internal")
    c.h1bt = nc.dram_tensor("h1bt", [32, 32, P, P], BF16, kind="Internal")
    c.w1bt = nc.dram_tensor("w1bt", [32, NCH, P, P], BF16, kind="Internal")
    c.w2bt = nc.dram_tensor("w2bt", [32, NCH, P, P], BF16, kind="Internal")
    c.wqbt = nc.dram_tensor("wqbt", [NCH, NCH, P, P], BF16, kind="Internal")
    c.wkbt = nc.dram_tensor("wkbt", [NCH, NCH, P, P], BF16, kind="Internal")
    c.wvbt = nc.dram_tensor("wvbt", [NCH, NCH, P, P], BF16, kind="Internal")
    c.wobt = nc.dram_tensor("wobt", [NCH, NCH, P, P], BF16, kind="Internal")
    c.h2bt = nc.dram_tensor("h2bt", [32, 32, P, P], BF16, kind="Internal")
    c.w3bt = nc.dram_tensor("w3bt", [8, 32, P, P], BF16, kind="Internal")
    c.cc_h_out = nc.dram_tensor("cc_h_out", [2 * D, T], BF16, kind="Internal")

    jj, ii = np.meshgrid(np.arange(P), np.arange(P), indexing="ij")
    rmask = ((ii // CH == jj // CH) & (ii > jj)).astype(np.float32)
    c.rmask_t = nc.inline_tensor(rmask, name="rwkv_mask")
    c.ident_t = nc.inline_tensor(np.eye(P, dtype=np.float32), name="ident_f")
    c.ones_col_t = nc.inline_tensor(np.ones((P, 1), np.float32), name="ones_colc")
    c.ones_row_t = nc.inline_tensor(np.ones((1, P), np.float32), name="ones_rowc")


def build_nc():
    nc = bass.Bass("TRN2", target_bir_lowering=False)
    es = contextlib.ExitStack()
    c = Ctx()
    c.nc = nc
    b = Builder(nc, es)
    c.b = b
    _declare_io(nc, c)

    c.dS = b.sem("dS"); c.dC = b.sem("dC"); c.dG = b.sem("dG")
    c.dT = b.sem("dT")
    c.dS2 = b.sem("dS2")
    c.dG2 = b.sem("dG2")
    c.dGs = [c.dG, c.dG2, b.sem("dG3"), b.sem("dG4")]
    c.dCX = b.sem("dCX")
    c.dW = [b.sem(f"dW{i}") for i in range(4)]
    c.dPC = b.sem("dPC")
    c.dPL = [b.sem(f"dPL{i}") for i in range(4)]
    c.dPS = [b.sem(f"dPS{i}") for i in range(4)]
    c.TS = b.sem("TS"); c.VS = b.sem("VS"); c.SS = b.sem("SS"); c.GS = b.sem("GS")
    c.CC = b.sem("CC")
    b.serial = {"vector": c.VS, "scalar": c.SS, "gpsimd": c.GS}
    b.barrier_serial = {"vector": c.VS, "scalar": c.SS, "gpsimd": c.GS,
                        "tensor": c.TS}

    with es:
        def sb(name, shape, dt):
            return es.enter_context(nc.sbuf_tensor(name, list(shape), dt))
        c.sb = sb

        c.identsb = sb("identsb", (P, P), F32)
        c.identbf = sb("identbf", (P, P), BF16)
        c.rmask_sb = sb("rmask_sb", (P, P), BF16)
        c.ones_col = sb("ones_col", (P, 1), F32)
        c.ones_row = sb("ones_row", (1, P), F32)
        c.smalls = sb("smalls", (P, 64), F32)
        c.zrow = sb("zrow", (1, 256), BF16)
        c.ones_mat = sb("ones_mat", (P, 64), F32)
        c.C_MU0, c.C_WB, c.C_U, c.C_FLAG, c.C_NEG8 = 0, 8, 12, 16, 17
        c.C_WGS, c.C_WGR, c.C_WUS, c.C_WUR, c.C_WDS, c.C_WDR = 25, 26, 27, 28, 29, 30
        c.C_BQ, c.C_BK, c.C_BV, c.C_BO, c.C_GAM, c.C_BET, c.C_S3 = \
            24, 32, 40, 48, 56, 60, 52

        c.parena = es.enter_context(nc.psum_tensor("parena", [P, 7 * TT], F32))
        c.pb = [c.parena[:, i * TT:(i + 1) * TT] for i in range(7)]
        c.pbt = es.enter_context(nc.psum_tensor("pbt", [P, 2 * P], BF16))
        pc_init(c)

        PHASE_BARR.clear()
        emit_preload(c)
        PHASE_BARR.append(b.sem("BARR").n)
        emit_phase1(c)
        PHASE_BARR.append(b.sem("BARR").n)
        if PHASES >= 2:
            emit_phase2(c)
            PHASE_BARR.append(b.sem("BARR").n)
        if PHASES >= 3:
            emit_phase3(c)
            PHASE_BARR.append(b.sem("BARR").n)
        if PHASES >= 4:
            emit_phase4(c)
            PHASE_BARR.append(b.sem("BARR").n)
        emit_finish(c)

        block = es.enter_context(nc.Block())
        b.replay(block)
    return nc


def ln_stats_bcast(c, src_tiles, n_d, pb_m, pb_r, stat, sq):
    b = c.b
    W = src_tiles[0].shape[-1]
    TSm, VSm, SSm = c.TS, c.VS, c.SS
    pbs, pbq = c.pb[0], c.pb[1]
    for ck in range(NCH):
        b.call("tensor", "matmul", pbs[0:1, 0:W], c.ones_col[:, :], src_tiles[ck],
               start=(ck == 0), stop=(ck == NCH - 1), inc=TSm)
    for ck in range(NCH):
        vs = b.call("scalar", "activation", sq[:, 0:W], src_tiles[ck], AF.Square,
                    inc=SSm)
        b.wait("tensor", SSm, vs)
        vt = b.call("tensor", "matmul", pbq[0:1, 0:W], c.ones_col[:, :], sq[:, 0:W],
                    start=(ck == 0), stop=(ck == NCH - 1), inc=TSm)
        b.wait("scalar", TSm, vt)
    b.wait("vector", TSm, vt)
    b.call("vector", "tensor_scalar_mul", stat[0:1, 0:W], pbs[0:1, 0:W], 1.0 / n_d)
    b.call("vector", "tensor_scalar_mul", stat[0:1, W:2 * W], pbq[0:1, 0:W], 1.0 / n_d)
    b.call("vector", "tensor_mul", stat[0:1, 2 * W:3 * W], stat[0:1, 0:W],
           stat[0:1, 0:W])
    vv = b.call("vector", "scalar_tensor_tensor", stat[0:1, W:2 * W],
                stat[0:1, W:2 * W], EPS, stat[0:1, 2 * W:3 * W],
                ALU.add, ALU.subtract)
    b.wait("scalar", c.VS, vv)
    vs = b.call("scalar", "activation", stat[0:1, 2 * W:3 * W], stat[0:1, W:2 * W],
                AF.Sqrt, inc=SSm)
    b.wait("vector", SSm, vs)
    vv = b.call("vector", "reciprocal", stat[0:1, W:2 * W], stat[0:1, 2 * W:3 * W])
    b.wait("tensor", c.VS, vv)
    b.call("tensor", "matmul", pb_m[:, 0:W], c.ones_row[:, :], stat[0:1, 0:W],
           start=True, stop=True, inc=TSm)
    vt = b.call("tensor", "matmul", pb_r[:, 0:W], c.ones_row[:, :], stat[0:1, W:2 * W],
                start=True, stop=True, inc=TSm)
    return vt


def emit_preload(c):
    b, dr = c.b, c.dr
    b.dma("sync", c.identsb[:, :], c.ident_t.ap(), c.dS)
    b.dma("gpsimd", c.identbf[:, :], c.ident_t.ap(), c.dG)
    b.dma("gpsimd", c.rmask_sb[:, :], c.rmask_t.ap(), c.dG)
    b.dma("sync", c.ones_col[:, :], c.ones_col_t.ap(), c.dS)
    b.dma("sync", c.ones_row[:, :], c.ones_row_t.ap(), c.dS)
    b.dma("sync", c.smalls[:, c.C_MU0:c.C_MU0 + NCH], c.dr["mu0c"].ap(), c.dS)
    b.dma("sync", c.smalls[:, c.C_WB:c.C_WB + NH], dr["wbc"].ap(), c.dS)
    b.dma("sync", c.smalls[:, c.C_U:c.C_U + NH], dr["uc"].ap(), c.dS)
    b.dma("sync", c.smalls[:, c.C_FLAG:c.C_FLAG + 1], dr["flag"].ap(), c.dS)
    b.call("vector", "memset", c.zrow[:, :], 0.0)
    b.call("vector", "memset", c.ones_mat[:, :], 1.0)
    b.call("vector", "memset", c.smalls[:, c.C_NEG8:c.C_NEG8 + 1], -8.0)
    b.barrier()


def emit_ln_to(c, src_dram, xln, stat, sq, xt, shift_cols):
    b = c.b
    pb = c.pb
    Ts = T + shift_cols
    for tt in range(NTT):
        for ck in range(NCH):
            b.dma("sync", xt[:, ck * TT:(ck + 1) * TT],
                  src_dram[ck * P:(ck + 1) * P, tt * TT:(tt + 1) * TT], c.dS)
        b.wait("tensor", c.dS, c.dS.n)
        b.wait("scalar", c.dS, c.dS.n)
        srcs = [xt[:, ck * TT:(ck + 1) * TT] for ck in range(NCH)]
        vt = ln_stats_bcast(c, srcs, D, pb[2], pb[3], stat, sq)
        b.wait("vector", c.TS, vt)
        for ck in range(NCH):
            b.call("vector", "tensor_sub", xt[:, ck * TT:(ck + 1) * TT],
                   xt[:, ck * TT:(ck + 1) * TT], pb[2][:, 0:TT])
            b.call("vector", "tensor_mul",
                   xln[:, ck * Ts + shift_cols + tt * TT:
                       ck * Ts + shift_cols + (tt + 1) * TT],
                   xt[:, ck * TT:(ck + 1) * TT], pb[3][:, 0:TT])
        b.barrier()


def emit_phase1(c):
    nc, b, dr = c.nc, c.b, c.dr
    dS, dG, TS, VS, SS, CC = c.dS, c.dG, c.TS, c.VS, c.SS, c.CC
    pb = c.pb

    with contextlib.ExitStack() as ph:
        sbf = lambda n, sh, dt: ph.enter_context(nc.sbuf_tensor(n, list(sh), dt))
        r_ = sbf("r_", (P, NH * T), BF16)
        k_ = sbf("k_", (P, NH * T), BF16)
        vtm = sbf("vtm", (P, NTC * D), BF16)
        tw = sbf("tw", (P, T), BF16)
        tg = sbf("tg", (P, T), BF16)

        with contextlib.ExitStack() as s1:
            sbl = lambda n, sh, dt: s1.enter_context(nc.sbuf_tensor(n, list(sh), dt))
            xln = sbl("xln", (P, NCH * (T + 1)), BF16)
            stat = sbl("stat", (1, 3 * TT), F32)
            sq = sbl("sq", (P, TT), F32)
            xt = sbl("xt", (P, NCH * TT), F32)
            mscr = sbl("mscr", (P, TT), F32)
            mixb = sbl("mixb", (P, NCH * TT), BF16)
            xx = sbl("xx", (33, 5 * TT), BF16)
            wx0 = sbl("wx0", (P, NCH * 160), BF16)
            wx2 = sbl("wx2", (33, 5 * D), BF16)
            rwS = sbl("rwS", (P, NCH * 512), BF16)
            kwS = sbl("kwS", (P, NCH * 512), BF16)
            vwS = sbl("vwS", (P, NCH * D), BF16)
            waS = sbl("waS", (P, NCH * 64), BF16)
            gaS = sbl("gaS", (P, NCH * 64), BF16)
            xprev = sbl("xprev", (P, NCH), F32)

            for ck in range(NCH):
                b.dma("gpsimd", wx0[:, ck * 160:(ck + 1) * 160],
                      dr["wx0T"][ck * P:(ck + 1) * P, :], dG)
                b.dma("gpsimd", rwS[:, ck * 512:(ck + 1) * 512],
                      dr["rWT"][ck * P:(ck + 1) * P, :], dG)
                b.dma("gpsimd", kwS[:, ck * 512:(ck + 1) * 512],
                      dr["kWT"][ck * P:(ck + 1) * P, :], dG)
                b.dma("gpsimd", vwS[:, ck * D:(ck + 1) * D],
                      dr["vWT"][ck * P:(ck + 1) * P, :], dG)
                b.dma("gpsimd", waS[:, ck * 64:(ck + 1) * 64],
                      dr["wAT"][ck * P:(ck + 1) * P, :], dG)
                b.dma("gpsimd", gaS[:, ck * 64:(ck + 1) * 64],
                      dr["gAT"][ck * P:(ck + 1) * P, :], dG)
            for g in range(5):
                b.dma("gpsimd", wx2[:, g * D:(g + 1) * D], dr["wx2T"][g, :, :], dG)
            wload_dG = dG.n

            b.dma("sync", xprev[:, :], dr["xprev_in"].ap(), dS)
            b.wait("tensor", dS, dS.n)
            b.wait("scalar", dS, dS.n)
            b.call("tensor", "matmul", pb[0][0:1, 0:NCH], c.ones_col[:, :],
                   xprev[:, :], start=True, stop=True, inc=TS)
            vs = b.call("scalar", "activation", sq[:, 0:NCH], xprev[:, :], AF.Square,
                        inc=SS)
            b.wait("tensor", SS, vs)
            vt = b.call("tensor", "matmul", pb[1][0:1, 0:NCH], c.ones_col[:, :],
                        sq[:, 0:NCH], start=True, stop=True, inc=TS)
            b.wait("vector", TS, vt)
            b.call("vector", "reduce_sum", stat[0:1, 0:1], pb[0][0:1, 0:NCH], AXX)
            b.call("vector", "reduce_sum", stat[0:1, 1:2], pb[1][0:1, 0:NCH], AXX)
            b.call("vector", "tensor_scalar_mul", stat[0:1, 0:2], stat[0:1, 0:2],
                   1.0 / D)
            b.call("vector", "tensor_mul", stat[0:1, 2:3], stat[0:1, 0:1],
                   stat[0:1, 0:1])
            vv = b.call("vector", "scalar_tensor_tensor", stat[0:1, 1:2],
                        stat[0:1, 1:2], EPS, stat[0:1, 2:3], ALU.add, ALU.subtract)
            b.wait("scalar", VS, vv)
            vs = b.call("scalar", "activation", stat[0:1, 2:3], stat[0:1, 1:2],
                        AF.Sqrt, inc=SS)
            b.wait("vector", SS, vs)
            vv = b.call("vector", "reciprocal", stat[0:1, 1:2], stat[0:1, 2:3])
            b.wait("tensor", VS, vv)
            b.call("tensor", "matmul", pb[2][:, 0:1], c.ones_row[:, :],
                   stat[0:1, 0:1], start=True, stop=True, inc=TS)
            vt = b.call("tensor", "matmul", pb[3][:, 0:1], c.ones_row[:, :],
                        stat[0:1, 1:2], start=True, stop=True, inc=TS)
            b.wait("vector", TS, vt)
            for ck in range(NCH):
                b.call("vector", "tensor_scalar",
                       xln[:, ck * (T + 1):ck * (T + 1) + 1], xprev[:, ck:ck + 1],
                       pb[2][:, 0:1], pb[3][:, 0:1], ALU.subtract, ALU.mult)
            b.barrier()

            emit_ln_to(c, dr["x_in"], xln, stat, sq, xt, 1)

            b.call("vector", "memset", xx[32:33, :], 1.0)
            b.barrier()
            b.wait("tensor", dG, wload_dG)
            b.wait("vector", dG, wload_dG)

            for tt in range(NTT):
                base = tt * TT
                for ck in range(NCH):
                    xo = ck * (T + 1) + 1 + base
                    b.call("vector", "tensor_sub", xt[:, ck * TT:(ck + 1) * TT],
                           xln[:, xo - 1:xo - 1 + TT], xln[:, xo:xo + TT])
                for ck in range(NCH):
                    xo = ck * (T + 1) + 1 + base
                    b.call("vector", "scalar_tensor_tensor",
                           mixb[:, ck * TT:(ck + 1) * TT],
                           xt[:, ck * TT:(ck + 1) * TT],
                           c.smalls[:, c.C_MU0 + ck:c.C_MU0 + ck + 1],
                           xln[:, xo:xo + TT], ALU.mult, ALU.add)
                b.wait("tensor", VS, VS.n)
                for g in range(5):
                    for ck in range(NCH):
                        b.call("tensor", "matmul", pb[g][0:32, 0:TT],
                               wx0[:, ck * 160 + g * 32:ck * 160 + (g + 1) * 32],
                               mixb[:, ck * TT:(ck + 1) * TT],
                               start=(ck == 0), stop=(ck == NCH - 1), inc=TS)
                b.wait("scalar", TS, TS.n)
                for g in range(5):
                    b.call("scalar", "activation", xx[0:32, g * TT:(g + 1) * TT],
                           pb[g][0:32, 0:TT], AF.Tanh, inc=SS)
                xx_S = SS.n

                for gname, gidx in (("r", 0), ("w", 1), ("k", 2), ("v", 3), ("g", 4)):
                    b.wait("tensor", SS, xx_S)
                    for mck in range(NCH):
                        xo = mck * (T + 1) + 1 + base
                        vt = b.call("tensor", "matmul", pb[5][:, 0:TT],
                                    wx2[0:33, gidx * D + mck * P:gidx * D + (mck + 1) * P],
                                    xx[0:33, gidx * TT:(gidx + 1) * TT],
                                    start=True, stop=True, inc=TS)
                        b.wait("vector", TS, vt)
                        b.call("vector", "tensor_mul", mscr[:, 0:TT], pb[5][:, 0:TT],
                               xt[:, mck * TT:(mck + 1) * TT])
                        vv = b.call("vector", "tensor_add",
                                    mixb[:, mck * TT:(mck + 1) * TT], mscr[:, 0:TT],
                                    xln[:, xo:xo + TT])
                        b.wait("tensor", VS, vv)
                    if gname in ("r", "k"):
                        wsb = rwS if gname == "r" else kwS
                        dst = r_ if gname == "r" else k_
                        for m in range(4):
                            for ck in range(NCH):
                                b.call("tensor", "matmul", pb[m][:, 0:TT],
                                       wsb[:, ck * 512 + m * P:ck * 512 + (m + 1) * P],
                                       mixb[:, ck * TT:(ck + 1) * TT],
                                       start=(ck == 0), stop=(ck == NCH - 1), inc=TS)
                        b.wait("vector", TS, TS.n)
                        for m in range(4):
                            b.call("vector", "tensor_copy",
                                   dst[:, m * T + base:m * T + base + TT],
                                   pb[m][:, 0:TT])
                        b.wait("tensor", VS, VS.n)
                    elif gname == "v":
                        for tc4 in range(4):
                            tc = tt * 4 + tc4
                            for fh in range(2):
                                bk = (tc4 * 2 + fh) % 2
                                for ck in range(NCH):
                                    b.call("tensor", "matmul", pb[bk][:, 0:TT],
                                           mixb[:, ck * TT + tc4 * P:
                                                ck * TT + (tc4 + 1) * P],
                                           vwS[:, ck * D + fh * 512:
                                               ck * D + (fh + 1) * 512],
                                           start=(ck == 0), stop=(ck == NCH - 1),
                                           inc=TS)
                                b.wait("vector", TS, TS.n)
                                vv = b.call("vector", "tensor_copy",
                                            vtm[:, tc * D + fh * 512:
                                                tc * D + (fh + 1) * 512],
                                            pb[bk][:, 0:TT])
                                b.wait("tensor", VS, vv)
                    else:
                        wsb = waS if gname == "w" else gaS
                        dst = tw if gname == "w" else tg
                        for ck in range(NCH):
                            b.call("tensor", "matmul", pb[6][0:64, 0:TT],
                                   wsb[:, ck * 64:(ck + 1) * 64],
                                   mixb[:, ck * TT:(ck + 1) * TT],
                                   start=(ck == 0), stop=(ck == NCH - 1), inc=TS)
                        b.wait("scalar", TS, TS.n)
                        vs = b.call("scalar", "activation",
                                    dst[0:64, base:base + TT], pb[6][0:64, 0:TT],
                                    AF.Tanh, inc=SS)
                        b.wait("tensor", SS, vs)
                b.barrier()

        with contextlib.ExitStack() as s2:
            sbl = lambda n, sh, dt: s2.enter_context(nc.sbuf_tensor(n, list(sh), dt))
            gbT = sbl("gbT", (64, D), BF16)
            oWb = sbl("oWb", (P, NCH * D), BF16)
            o_tm = sbl("o_tm", (P, NTC * D), BF16)

            with contextlib.ExitStack() as s3:
                sbm = lambda n, sh, dt: s3.enter_context(nc.sbuf_tensor(n, list(sh), dt))
                wbT = sbm("wbT", (64, 512), BF16)
                qe = sbm("qe", (P, NH * T), BF16)
                ke = sbm("ke", (P, T), BF16)
                kwtm = sbm("kwtm", (P, NTC * P), BF16)
                wtil = sbm("wtil", (P, T), F32)
                wcc = sbm("wcc", (P, T), F32)
                e1 = sbm("e1", (P, T), F32)
                amask = sbm("amask", (P, P), BF16)
                dexps = sbm("dexps", (P, 5 * NCK), F32)
                bnd = sbm("bnd", (P, NCK), F32)
                s_row = sbm("s_row", (1, T), F32)
                s_tm = sbm("s_tm", (P, NH * NTC), F32)
                S_all = sbm("S_all", (P, NH * DV), BF16)
                S_in = sbm("S_in", (P, NH * DV), BF16)

                b.dma("gpsimd", wbT[:, :], dr["wBT"].ap(), dG)
                b.dma("gpsimd", gbT[:, :], dr["gBT"].ap(), dG)
                for ck in range(NCH):
                    b.dma("gpsimd", oWb[:, ck * D:(ck + 1) * D],
                          dr["oWT"][ck * P:(ck + 1) * P, :], dG)
                b.wait("tensor", dG, dG.n)

                for h in range(NH):
                    vv = b.call("vector", "tensor_mul", e1[:, :],
                                r_[:, h * T:(h + 1) * T], k_[:, h * T:(h + 1) * T])
                    b.wait("tensor", VS, vv)
                    cp = {0: 0, 1: 0}
                    for q4 in range(NTT):
                        bk5 = 5 + q4 % 2
                        b.wait("tensor", VS, cp[q4 % 2])
                        vt = b.call("tensor", "matmul", pb[bk5][0:1, 0:TT],
                                    c.smalls[:, c.C_U + h:c.C_U + h + 1],
                                    e1[:, q4 * TT:(q4 + 1) * TT],
                                    start=True, stop=True, inc=TS)
                        b.wait("vector", TS, vt)
                        cp[q4 % 2] = b.call("vector", "tensor_copy",
                                            s_row[0:1, q4 * TT:(q4 + 1) * TT],
                                            pb[bk5][0:1, 0:TT])
                    for tc in range(NTC):
                        bk5 = 5 + tc % 2
                        b.wait("tensor", VS, cp[tc % 2])
                        vt = b.call("tensor", "transpose", pb[bk5][:, 0:1],
                                    s_row[0:1, tc * P:(tc + 1) * P],
                                    c.identsb[0:1, 0:1], inc=TS)
                        b.wait("vector", TS, vt)
                        cp[tc % 2] = b.call("vector", "tensor_copy",
                                            s_tm[:, h * NTC + tc:h * NTC + tc + 1],
                                            pb[bk5][:, 0:1])
                    b.wait("tensor", VS, max(cp.values()))
                b.call("vector", "memset", S_all[:, :], 0.0)
                b.barrier()

                for h in range(NH):
                    hT = h * T
                    ep = {0: 0, 1: 0}
                    for q4 in range(NTT):
                        bk5 = 5 + q4 % 2
                        b.wait("tensor", SS, ep[q4 % 2])
                        vt = b.call("tensor", "matmul", pb[bk5][:, 0:TT],
                                    wbT[:, h * P:(h + 1) * P],
                                    tw[0:64, q4 * TT:(q4 + 1) * TT],
                                    start=True, stop=True, inc=TS)
                        b.wait("scalar", TS, vt)
                        ep[q4 % 2] = b.call("scalar", "activation",
                                            wtil[:, q4 * TT:(q4 + 1) * TT],
                                            pb[bk5][:, 0:TT], AF.Exp,
                                            bias=c.smalls[:, c.C_WB + h:
                                                          c.C_WB + h + 1],
                                            inc=SS)
                    b.wait("vector", SS, SS.n)
                    b.call("vector", "tensor_tensor_scan", wcc[:, :],
                           wtil[:, :], wtil[:, :], 0.0, ALU.add, ALU.bypass)
                    b.call("vector", "memset", bnd[:, 0:1], 0.0)
                    b.call("vector", "tensor_copy", bnd[:, 1:NCK],
                           wcc[:, CH - 1:T - 1:CH])
                    b.call("vector", "tensor_tensor",
                           wcc.ap().rearrange("p (a b) -> p a b", a=NCK),
                           wcc.ap().rearrange("p (a b) -> p a b", a=NCK),
                           bnd[:, :, None].broadcast_to([P, NCK, CH]),
                           ALU.subtract)
                    b.wait("scalar", VS, VS.n)
                    b.call("scalar", "activation", dexps[:, 0:NCK],
                           wcc[:, CH - 1:T:CH], AF.Exp, scale=-1.0, inc=SS)
                    vs = b.call("scalar", "activation", dexps[:, NCK:2 * NCK],
                                wcc[:, CH - 1:T:CH], AF.Copy, scale=-1.0, inc=SS)
                    b.wait("vector", SS, vs)
                    b.call("vector", "tensor_tensor_scan", dexps[:, 2 * NCK:3 * NCK],
                           dexps[:, NCK:2 * NCK], dexps[:, NCK:2 * NCK], 0.0,
                           ALU.add, ALU.bypass)
                    vv = b.call("vector", "tensor_sub", dexps[:, 3 * NCK:4 * NCK],
                                dexps[:, 2 * NCK:3 * NCK], dexps[:, NCK:2 * NCK])
                    b.wait("scalar", VS, vv)
                    dexc_s = b.call("scalar", "activation", dexps[:, 4 * NCK:5 * NCK],
                                    dexps[:, 3 * NCK:4 * NCK], AF.Exp, inc=SS)

                    vv = b.call("vector", "tensor_sub", e1[:, :], wtil[:, :],
                                wcc[:, :])
                    b.wait("scalar", VS, vv)
                    vs = b.call("scalar", "activation", e1[:, :], e1[:, :], AF.Exp,
                                inc=SS)
                    b.wait("vector", SS, vs)
                    vv = b.call("vector", "tensor_mul", qe[:, hT:hT + T],
                                r_[:, hT:hT + T], e1[:, :])
                    b.wait("scalar", VS, vv)
                    vs = b.call("scalar", "activation", e1[:, :], wcc[:, :], AF.Exp,
                                inc=SS)
                    b.wait("vector", SS, vs)
                    vv = b.call("vector", "tensor_mul", ke[:, :], k_[:, hT:hT + T],
                                e1[:, :])
                    b.wait("scalar", VS, vv)
                    b.call("vector", "tensor_copy", bnd[:, 0:NCK],
                           wcc[:, CH - 1:T:CH])
                    b.call("vector", "tensor_tensor",
                           e1.ap().rearrange("p (a b) -> p a b", a=NCK),
                           wcc.ap().rearrange("p (a b) -> p a b", a=NCK),
                           bnd[:, :, None].broadcast_to([P, NCK, CH]),
                           ALU.subtract)
                    b.wait("scalar", VS, VS.n)
                    vs = b.call("scalar", "activation", e1[:, :], e1[:, :], AF.Exp,
                                inc=SS)
                    b.wait("vector", SS, vs)
                    vv = b.call("vector", "tensor_mul", e1[:, :], k_[:, hT:hT + T],
                                e1[:, :])
                    b.wait("tensor", VS, vv)
                    cp = {0: 0, 1: 0}
                    for tc in range(NTC):
                        bk5 = 5 + tc % 2
                        b.wait("tensor", VS, cp[tc % 2])
                        vt = b.call("tensor", "transpose", pb[bk5][:, 0:P],
                                    e1[:, tc * P:(tc + 1) * P], c.identsb[:, :],
                                    inc=TS)
                        b.wait("vector", TS, vt)
                        cp[tc % 2] = b.call("vector", "tensor_copy",
                                            kwtm[:, tc * P:(tc + 1) * P],
                                            pb[bk5][:, 0:P])
                    b.wait("tensor", VS, max(cp.values()))

                    Sh = S_all[:, h * DV:(h + 1) * DV]
                    b.wait("vector", SS, dexc_s)
                    for tc in range(NTC):
                        vt = b.call("tensor", "matmul", pb[0][:, 0:P],
                                    ke[:, tc * P:(tc + 1) * P],
                                    qe[:, hT + tc * P:hT + (tc + 1) * P],
                                    start=True, stop=True, inc=TS)
                        b.wait("vector", TS, vt)
                        am_v = b.call("vector", "tensor_mul", amask[:, :],
                                      pb[0][:, 0:P], c.rmask_sb[:, :])
                        for j4 in range(4):
                            b.call("tensor", "matmul", pb[2 + j4][:, 0:DV],
                                   kwtm[j4 * CH:(j4 + 1) * CH, tc * P:(tc + 1) * P],
                                   vtm[j4 * CH:(j4 + 1) * CH,
                                       tc * D + h * DV:tc * D + (h + 1) * DV],
                                   start=True, stop=True,
                                   tile_position=(j4 * CH, 0), inc=TS)
                        b.wait("tensor", VS, am_v)
                        b.call("tensor", "matmul", pb[1][:, 0:DV], amask[:, :],
                               vtm[:, tc * D + h * DV:tc * D + (h + 1) * DV],
                               start=True, stop=False, inc=TS)
                        for j4 in range(4):
                            vt = b.call("tensor", "matmul",
                                        pb[1][j4 * CH:(j4 + 1) * CH, 0:DV],
                                        qe[:, hT + tc * P + j4 * CH:
                                           hT + tc * P + (j4 + 1) * CH],
                                        Sh, start=False, stop=False,
                                        tile_position=(0, j4 * CH), inc=TS)
                            b.wait("vector", TS, vt)
                            vv = b.call("vector", "scalar_tensor_tensor", Sh, Sh,
                                        dexps[:, tc * 4 + j4:tc * 4 + j4 + 1],
                                        pb[2 + j4][:, 0:DV], ALU.mult, ALU.add)
                            b.wait("tensor", VS, vv)
                        b.call("tensor", "matmul", pb[1][:, 0:DV],
                               c.zrow[0:1, 0:P], c.zrow[0:1, 0:DV],
                               start=False, stop=True, inc=TS)
                        b.wait("vector", TS, TS.n)
                        vv = b.call("vector", "scalar_tensor_tensor",
                                    o_tm[:, tc * D + h * DV:tc * D + (h + 1) * DV],
                                    vtm[:, tc * D + h * DV:tc * D + (h + 1) * DV],
                                    s_tm[:, h * NTC + tc:h * NTC + tc + 1],
                                    pb[1][:, 0:DV], ALU.mult, ALU.add)
                        b.wait("tensor", VS, vv)
                    for tc in range(NTC):
                        qs = qe[:, hT + tc * P:hT + (tc + 1) * P]
                        b.call("vector", "tensor_tensor",
                               qs.rearrange("p (a b) -> p a b", a=4),
                               qs.rearrange("p (a b) -> p a b", a=4),
                               dexps[:, 4 * NCK + tc * 4:4 * NCK + (tc + 1) * 4]
                               [:, :, None].broadcast_to([P, 4, CH]),
                               ALU.mult)
                    b.barrier()

                b.dma("gpsimd", c.cc_s_in.ap(), S_all[:, :], dG)
                b.wait("gpsimd", dG, dG.n)
                vg = b.op("gpsimd", lambda e: e.collective_compute(
                    "AllGather", ALU.bypass,
                    replica_groups=[[0, 1], [2, 3], [4, 5], [6, 7]],
                    ins=[c.cc_s_in.ap()], outs=[c.cc_s_out.ap()]), inc=CC)
                b.wait("gpsimd", CC, vg)
                b.dma("gpsimd", S_in[:, :], c.cc_s_out[0:DK, :], dG)
                b.wait("vector", dG, dG.n)
                vv = b.call("vector", "tensor_scalar_mul", S_in[:, :], S_in[:, :],
                            c.smalls[:, c.C_FLAG:c.C_FLAG + 1])
                b.wait("tensor", VS, vv)
                for h in range(NH):
                    for tc in range(NTC):
                        vt = b.call("tensor", "matmul", pb[0][:, 0:DV],
                                    qe[:, h * T + tc * P:h * T + (tc + 1) * P],
                                    S_in[:, h * DV:(h + 1) * DV],
                                    start=True, stop=True, inc=TS)
                        b.wait("vector", TS, vt)
                        vv = b.call("vector", "tensor_add",
                                    o_tm[:, tc * D + h * DV:tc * D + (h + 1) * DV],
                                    o_tm[:, tc * D + h * DV:tc * D + (h + 1) * DV],
                                    pb[0][:, 0:DV])
                        b.wait("tensor", VS, vv)
                b.barrier()

            with contextlib.ExitStack() as s4:
                sbl = lambda n, sh, dt: s4.enter_context(nc.sbuf_tensor(n, list(sh), dt))
                gbb = sbl("gbb", (P, 3 * D), F32)
                ogfm = sbl("ogfm", (P, NCH * T), BF16)
                lnv = sbl("lnv", (1, 3 * D), F32)
                colstat = sbl("colstat", (P, 8), F32)
                yst = sbl("yst", (P, TT), F32)
                xrel = sbl("xrel", (P, TT), F32)
                sc = sbl("sc", (P, 2 * D), F32)

                b.dma("sync", lnv[0:1, 0:D], dr["lngr"].ap(), dS)
                b.dma("sync", lnv[0:1, D:2 * D], dr["lnbr"].ap(), dS)
                b.dma("sync", lnv[0:1, 2 * D:3 * D], dr["gbr"].ap(), dS)
                b.wait("tensor", dS, dS.n)
                for i in range(3):
                    for fh in range(2):
                        vt = b.call("tensor", "matmul", pb[6][:, 0:TT],
                                    c.ones_row[:, :],
                                    lnv[0:1, i * D + fh * 512:i * D + (fh + 1) * 512],
                                    start=True, stop=True, inc=TS)
                        b.wait("vector", TS, vt)
                        vv = b.call("vector", "tensor_copy",
                                    gbb[:, i * D + fh * 512:i * D + (fh + 1) * 512],
                                    pb[6][:, 0:TT])
                        b.wait("tensor", VS, vv)
                b.barrier()

                for tc in range(NTC):
                    b.call("scalar", "activation", sc[:, 0:D],
                           o_tm[:, tc * D:(tc + 1) * D], AF.Copy,
                           accum_out=colstat[:, 0:1], inc=SS)
                    vs = b.call("scalar", "activation", sc[:, 0:D],
                                o_tm[:, tc * D:(tc + 1) * D], AF.Square,
                                accum_out=colstat[:, 1:2], inc=SS)
                    b.wait("vector", SS, vs)
                    b.call("vector", "tensor_scalar_mul", colstat[:, 0:2],
                           colstat[:, 0:2], 1.0 / D)
                    b.call("vector", "tensor_mul", colstat[:, 2:3], colstat[:, 0:1],
                           colstat[:, 0:1])
                    vv = b.call("vector", "scalar_tensor_tensor", colstat[:, 1:2],
                                colstat[:, 1:2], EPS, colstat[:, 2:3],
                                ALU.add, ALU.subtract)
                    b.wait("scalar", VS, vv)
                    vs = b.call("scalar", "activation", colstat[:, 2:3],
                                colstat[:, 1:2], AF.Sqrt, inc=SS)
                    b.wait("vector", SS, vs)
                    b.call("vector", "reciprocal", colstat[:, 1:2], colstat[:, 2:3])
                    for fh in range(2):
                        f0 = fh * 512
                        vt = b.call("tensor", "matmul", pb[0][:, 0:TT],
                                    tg[0:64, tc * P:(tc + 1) * P],
                                    gbT[:, f0:f0 + 512],
                                    start=True, stop=True, inc=TS)
                        b.wait("vector", TS, vt)
                        vv = b.call("vector", "tensor_add", sc[:, 0:TT],
                                    pb[0][:, 0:TT], gbb[:, 2 * D + f0:2 * D + f0 + 512])
                        b.wait("scalar", VS, vv)
                        b.wait("tensor", VS, vv)
                        gs = b.call("scalar", "activation", sc[:, 0:TT], sc[:, 0:TT],
                                    AF.Silu, inc=SS)
                        b.call("vector", "tensor_scalar", sc[:, D:D + TT],
                               o_tm[:, tc * D + f0:tc * D + f0 + 512],
                               colstat[:, 0:1], colstat[:, 1:2],
                               ALU.subtract, ALU.mult)
                        b.call("vector", "tensor_mul", sc[:, D:D + TT],
                               sc[:, D:D + TT], gbb[:, f0:f0 + 512])
                        b.call("vector", "tensor_add", sc[:, D:D + TT],
                               sc[:, D:D + TT], gbb[:, D + f0:D + f0 + 512])
                        b.wait("vector", SS, gs)
                        vv = b.call("vector", "tensor_mul",
                                    o_tm[:, tc * D + f0:tc * D + f0 + 512],
                                    sc[:, D:D + TT], sc[:, 0:TT])
                        b.wait("scalar", VS, vv)
                b.barrier()

                for tc in range(NTC):
                    for ck in range(NCH):
                        vt = b.call("tensor", "transpose", c.pbt[:, 0:P],
                                    o_tm[:, tc * D + ck * P:tc * D + (ck + 1) * P],
                                    c.identbf[:, :], inc=TS)
                        b.wait("vector", TS, vt)
                        vv = b.call("vector", "tensor_copy",
                                    ogfm[:, ck * T + tc * P:ck * T + (tc + 1) * P],
                                    c.pbt[:, 0:P])
                        b.wait("tensor", VS, vv)
                b.barrier()

                prev_read = {0: 0, 1: 0}
                for oc in range(NCH):
                    for q4 in range(NTT):
                        bk = (oc * NTT + q4) % 2
                        b.wait("tensor", VS, prev_read[bk])
                        for ck in range(NCH):
                            b.call("tensor", "matmul", pb[bk][:, 0:TT],
                                   oWb[:, ck * D + oc * P:ck * D + (oc + 1) * P],
                                   ogfm[:, ck * T + q4 * TT:ck * T + (q4 + 1) * TT],
                                   start=(ck == 0), stop=(ck == NCH - 1), inc=TS)
                        mm_T = TS.n
                        b.dma("sync", xrel[:, :],
                              dr["x_in"][oc * P:(oc + 1) * P, q4 * TT:(q4 + 1) * TT],
                              dS)
                        b.wait("vector", dS, dS.n)
                        b.wait("vector", TS, mm_T)
                        vv = b.call("vector", "tensor_add", yst[:, :],
                                    pb[bk][:, 0:TT], xrel[:, :])
                        prev_read[bk] = vv
                        b.wait("sync", VS, vv)
                        b.dma("sync",
                              c.x_hbm[0][oc * P:(oc + 1) * P, q4 * TT:(q4 + 1) * TT],
                              yst[:, :], c.dT)
                        b.wait("tensor", c.dT, c.dT.n)
                        b.wait("vector", c.dT, c.dT.n)
                b.barrier()

    b.barrier()
    c.x_cur = 0


def emit_token_max_bcast(c, macc, colbuf, rowbuf, mfull):
    b = c.b
    pb = c.pb
    TS, VS = c.TS, c.VS
    b.wait("tensor", VS, VS.n)
    for tc in range(NTC):
        vt = b.call("tensor", "transpose", pb[4][:, 0:P],
                    macc[:, tc * P:(tc + 1) * P], c.identsb[:, :], inc=TS)
        b.wait("vector", TS, vt)
        vv = b.call("vector", "reduce_max", colbuf[:, :], pb[4][:, 0:P], AXX)
        b.wait("tensor", VS, vv)
        vt = b.call("tensor", "transpose", pb[5][0:1, 0:P], colbuf[:, :],
                    c.identsb[:, :], inc=TS)
        b.wait("vector", TS, vt)
        vv = b.call("vector", "tensor_copy", rowbuf[0:1, tc * P:(tc + 1) * P],
                    pb[5][0:1, 0:P])
        b.wait("tensor", VS, vv)
    for q4 in range(NTT):
        vt = b.call("tensor", "matmul", pb[4][:, 0:TT], c.ones_row[:, :],
                    rowbuf[0:1, q4 * TT:(q4 + 1) * TT], start=True, stop=True,
                    inc=TS)
        b.wait("vector", TS, vt)
        vv = b.call("vector", "tensor_copy", mfull[:, q4 * TT:(q4 + 1) * TT],
                    pb[4][:, 0:TT])
        b.wait("tensor", VS, vv)
    b.wait("vector", c.TS, c.TS.n)



def pc_init(c):
    c.pc_jobs = []
    for srcn, dsn, noc, na in (("wq1t", "wqbt", 8, 8), ("wk1t", "wkbt", 8, 8),
                               ("wv1t", "wvbt", 8, 8), ("wo1t", "wobt", 8, 8),
                               ("w1Tt", "w1bt", 32, 8), ("w2Tt", "w2bt", 32, 8),
                               ("h1Tt", "h1bt", 32, 32), ("h2Tt", "h2bt", 32, 32),
                               ("w3Tt", "w3bt", 8, 32)):
        dstt = getattr(c, dsn)
        for oc in range(noc):
            for a0 in range(0, na, 8):
                c.pc_jobs.append((srcn, dstt, oc, a0, a0 + 8))
    c.pc_state = {"li": 0, "si": 0, "loadv": {}, "storev": {}}
    c.pc_done = {}


def pc_step(c):
    b, dr = c.b, c.dr
    st = c.pc_state
    nb = len(c.pcst)
    i = st["li"]
    if i < len(c.pc_jobs):
        bi = i % nb
        srcn, dstt, oc, a0, a1 = c.pc_jobs[i]
        if i >= nb:
            b.wait("gpsimd", c.dPS[bi], st["storev"][i - nb])
        b.dma("gpsimd", c.pcst[bi].ap().rearrange("q (a m) -> q a m", a=8),
              dr[srcn][oc, a0:a1, :, :].rearrange("a p m -> p a m"), c.dPL[bi])
        st["loadv"][i] = c.dPL[bi].n
        st["li"] = i + 1
    j = st["si"]
    if j < len(c.pc_jobs) and j <= st["li"] - 2 or (st["li"] == len(c.pc_jobs)
                                                    and j < st["li"]):
        bj = j % nb
        srcn, dstt, oc, a0, a1 = c.pc_jobs[j]
        b.wait("gpsimd", c.dPL[bj], st["loadv"][j])
        b.dma("gpsimd", dstt[oc, a0:a1, :, :].rearrange("a p m -> p a m"),
              c.pcst[bj].ap().rearrange("q (a m) -> q a m", a=8), c.dPS[bj])
        st["storev"][j] = c.dPS[bj].n
        st["si"] = j + 1
        if j + 1 == len(c.pc_jobs) or c.pc_jobs[j + 1][0] != srcn:
            c.pc_done[srcn] = [c.dPS[k].n for k in range(4)]


def pc_drain(c):
    while c.pc_state["si"] < len(c.pc_jobs):
        pc_step(c)


def emit_phase2(c):
    nc, b, dr = c.nc, c.b, c.dr
    dS, dG, dT, TS, VS, SS, GS = c.dS, c.dG, c.dT, c.TS, c.VS, c.SS, c.GS
    pb = c.pb
    x_in = c.x_hbm[0]
    x_out = c.x_hbm[1]

    b.wait("gpsimd", c.TS, c.TS.n)
    b.wait("gpsimd", c.VS, c.VS.n)
    b.wait("gpsimd", c.SS, c.SS.n)
    b.wait("sync", c.dT, c.dT.n)

    with contextlib.ExitStack() as ph:
        sbf = lambda n, sh, dt: ph.enter_context(nc.sbuf_tensor(n, list(sh), dt))
        c.pcst = [sbf(f"pcst{i}", (P, 1024), BF16) for i in range(4)]
        xq = sbf("p2xq", (P, NCH * T), BF16)
        sm2 = sbf("p2sm", (1, 16), F32)

        with contextlib.ExitStack() as sA:
            sbl = lambda n, sh, dt: sA.enter_context(nc.sbuf_tensor(n, list(sh), dt))
            wfA = [sbl(f"p2wfA{i}", (P, 4096), F32) for i in range(2)]
            acol = sbl("p2acol", (P, 4), F32)
            rcol = sbl("p2rcol", (P, 1), F32)
            b.call("vector", "memset", acol[:, :], 0.0)
            b.barrier()
            specs = [("wgT", 8, 4096, 0), ("wuT", 8, 4096, 1), ("wdT", 32, 1024, 2)]
            prev_red = {0: 0, 1: 0}
            dsems = {0: dS, 1: c.dS2}
            for name, nsl, wid, idx in specs:
                for s in range(nsl):
                    for _ in range(3):
                        pc_step(c)
                    bi = s % 2
                    b.wait("sync", VS, prev_red[bi])
                    b.dma("sync", wfA[bi][:, 0:wid],
                          dr[name][s * P:(s + 1) * P, :], dsems[bi])
                    b.wait("vector", dsems[bi], dsems[bi].n)
                    b.call("vector", "reduce_sum", rcol[:, :], wfA[bi][:, 0:wid],
                           AXX, apply_absolute_value=True)
                    vv = b.call("vector", "tensor_add", acol[:, idx:idx + 1],
                                acol[:, idx:idx + 1], rcol[:, :])
                    prev_red[bi] = vv
            b.wait("tensor", VS, VS.n)
            for idx, (scol, rcol_i, nelem) in enumerate(
                    ((c.C_WGS, c.C_WGR, 4096 * D), (c.C_WUS, c.C_WUR, 4096 * D),
                     (c.C_WDS, c.C_WDR, 4096 * D))):
                vt = b.call("tensor", "matmul", pb[0][0:1, 0:1],
                            acol[:, idx:idx + 1], c.ones_col[:, :],
                            start=True, stop=True, inc=TS)
                b.wait("vector", TS, vt)
                b.call("vector", "tensor_scalar", sm2[0:1, 0:1], pb[0][0:1, 0:1],
                       1.0 / nelem, 1e-5, ALU.mult, ALU.max)
                vv = b.call("vector", "reciprocal", sm2[0:1, 1:2], sm2[0:1, 0:1])
                b.wait("tensor", VS, vv)
                b.call("tensor", "matmul", pb[1][:, 0:1], c.ones_row[:, :],
                       sm2[0:1, 0:1], start=True, stop=True, inc=TS)
                vt = b.call("tensor", "matmul", pb[2][:, 0:1], c.ones_row[:, :],
                            sm2[0:1, 1:2], start=True, stop=True, inc=TS)
                b.wait("vector", TS, vt)
                b.call("vector", "tensor_copy", c.smalls[:, rcol_i:rcol_i + 1],
                       pb[1][:, 0:1])
                vv = b.call("vector", "tensor_copy", c.smalls[:, scol:scol + 1],
                            pb[2][:, 0:1])
                b.wait("tensor", VS, vv)
            b.barrier()

        with contextlib.ExitStack() as sL:
            sbl = lambda n, sh, dt: sL.enter_context(nc.sbuf_tensor(n, list(sh), dt))
            xln = sbl("p2xln", (P, NCH * T), F32)
            stat = sbl("p2stat", (1, 3 * TT), F32)
            sq = sbl("p2sq", (P, TT), F32)
            xt = sbl("p2xt", (P, NCH * TT), F32)
            mfull = sbl("p2mfull", (P, T), F32)
            tfull = sbl("p2tfull", (P, T), F32)
            rfull = sbl("p2rfull", (P, T), F32)
            q1col = sbl("p2q1col", (P, 1), F32)
            q1row = sbl("p2q1row", (1, T), F32)
            emit_ln_to(c, x_in, xln, stat, sq, xt, 0)
            for ck in range(NCH):
                vs = b.call("scalar", "activation", tfull[:, :],
                            xln[:, ck * T:(ck + 1) * T], AF.Abs, inc=SS)
                b.wait("vector", SS, vs)
                if ck == 0:
                    vv = b.call("vector", "tensor_copy", mfull[:, :], tfull[:, :])
                else:
                    vv = b.call("vector", "tensor_max", mfull[:, :], mfull[:, :],
                                tfull[:, :])
                b.wait("scalar", VS, vv)
            emit_token_max_bcast(c, mfull, q1col, q1row, tfull)
            b.call("vector", "tensor_scalar_max", mfull[:, :], tfull[:, :], 1e-5)
            b.call("vector", "reciprocal", rfull[:, :], mfull[:, :])
            b.call("vector", "tensor_scalar_mul", mfull[:, :], mfull[:, :],
                   1.0 / 127.0)
            for ck in range(NCH):
                b.call("vector", "tensor_mul", tfull[:, :],
                       xln[:, ck * T:(ck + 1) * T], rfull[:, :])
                b.call("vector", "tensor_scalar", tfull[:, :], tfull[:, :],
                       127.0, RND_C, ALU.mult, ALU.add)
                b.call("vector", "tensor_scalar", tfull[:, :], tfull[:, :],
                       RND_C, None, ALU.subtract, ALU.bypass)
                b.call("vector", "tensor_mul", xq[:, ck * T:(ck + 1) * T],
                       tfull[:, :], mfull[:, :])
            b.barrier()

        gu = ph.enter_context(nc.sbuf_tensor("p2gu", [P, 32 * T], BF16))

        with contextlib.ExitStack() as sG:
            sbl = lambda n, sh, dt: sG.enter_context(nc.sbuf_tensor(n, list(sh), dt))
            wf = sbl("p2wf", (P, NCH * 512), F32)
            wgq = sbl("p2wgq", (P, NCH * 512), BF16)
            wuq = sbl("p2wuq", (P, NCH * 512), BF16)
            gsb = sbl("p2gsb", (P, 2 * TT), BF16)
            g_s = {0: 0, 1: 0}
            g_v = {0: 0, 1: 0}
            for og in range(8):
                co = og * 512
                for wname, wq_sb, scol in (("wgT", wgq, c.C_WGS),
                                           ("wuT", wuq, c.C_WUS)):
                    for ck in range(NCH):
                        b.dma("sync", wf[:, ck * 512:(ck + 1) * 512],
                              dr[wname][ck * P:(ck + 1) * P, co:co + 512], dS)
                    b.wait("vector", dS, dS.n)
                    b.call("vector", "tensor_scalar", wf[:, :], wf[:, :],
                           c.smalls[:, scol:scol + 1], RND_C, ALU.mult, ALU.add)
                    b.call("vector", "tensor_scalar", wf[:, :], wf[:, :],
                           RND_C, 1.0, ALU.subtract, ALU.min)
                    vv = b.call("vector", "tensor_scalar", wq_sb[:, :], wf[:, :],
                                -1.0, None, ALU.max, ALU.bypass)
                    b.wait("sync", VS, vv)
                    b.wait("tensor", VS, vv)
                for oc4 in range(4):
                    ic = og * 4 + oc4
                    for tt in range(NTT):
                        gi = ic * NTT + tt
                        bp = 2 * (gi % 2)
                        b.wait("tensor", SS, g_s[gi % 2])
                        b.wait("tensor", VS, g_v[gi % 2])
                        for ck in range(NCH):
                            b.call("tensor", "matmul", pb[bp][:, 0:TT],
                                   wgq[:, ck * 512 + oc4 * P:ck * 512 + (oc4 + 1) * P],
                                   xq[:, ck * T + tt * TT:ck * T + (tt + 1) * TT],
                                   start=(ck == 0), stop=(ck == NCH - 1), inc=TS)
                        mmA = TS.n
                        for ck in range(NCH):
                            b.call("tensor", "matmul", pb[bp + 1][:, 0:TT],
                                   wuq[:, ck * 512 + oc4 * P:ck * 512 + (oc4 + 1) * P],
                                   xq[:, ck * T + tt * TT:ck * T + (tt + 1) * TT],
                                   start=(ck == 0), stop=(ck == NCH - 1), inc=TS)
                        mmB = TS.n
                        b.wait("scalar", TS, mmA)
                        b.wait("scalar", VS, g_v[gi % 2])
                        vs = b.call("scalar", "activation",
                                    gsb[:, (gi % 2) * TT:(gi % 2 + 1) * TT],
                                    pb[bp][:, 0:TT], AF.Silu,
                                    scale=c.smalls[:, c.C_WGR:c.C_WGR + 1],
                                    inc=SS)
                        g_s[gi % 2] = vs
                        b.wait("vector", SS, vs)
                        b.wait("vector", TS, mmB)
                        vv = b.call("vector", "scalar_tensor_tensor",
                                    gu[:, ic * T + tt * TT:ic * T + (tt + 1) * TT],
                                    pb[bp + 1][:, 0:TT],
                                    c.smalls[:, c.C_WUR:c.C_WUR + 1],
                                    gsb[:, (gi % 2) * TT:(gi % 2 + 1) * TT],
                                    ALU.mult, ALU.mult)
                        g_v[gi % 2] = vv
                for _ in range(36):
                    pc_step(c)
            b.barrier()

        with contextlib.ExitStack() as sQ:
            sbl = lambda n, sh, dt: sQ.enter_context(nc.sbuf_tensor(n, list(sh), dt))
            mfull = sbl("p2mful2", (P, T), F32)
            tfull = sbl("p2tful2", (P, T), F32)
            rfull = sbl("p2rful2", (P, T), F32)
            q1col = sbl("p2q2col", (P, 1), F32)
            q1row = rfull[0:1, :]
            for ic in range(32):
                vs = b.call("scalar", "activation", tfull[:, :],
                            gu[:, ic * T:(ic + 1) * T], AF.Abs, inc=SS)
                b.wait("vector", SS, vs)
                if ic == 0:
                    vv = b.call("vector", "tensor_copy", mfull[:, :], tfull[:, :])
                else:
                    vv = b.call("vector", "tensor_max", mfull[:, :], mfull[:, :],
                                tfull[:, :])
                b.wait("scalar", VS, vv)
            emit_token_max_bcast(c, mfull, q1col, q1row, tfull)
            b.call("vector", "tensor_scalar_max", mfull[:, :], tfull[:, :], 1e-5)
            b.call("vector", "reciprocal", rfull[:, :], mfull[:, :])
            b.call("vector", "tensor_scalar_mul", rfull[:, :], rfull[:, :], 127.0)
            b.call("vector", "tensor_scalar_mul", mfull[:, :], mfull[:, :],
                   1.0 / 127.0)
            for ic in range(32):
                b.call("vector", "tensor_mul", tfull[:, :],
                       gu[:, ic * T:(ic + 1) * T], rfull[:, :])
                b.call("vector", "tensor_scalar", tfull[:, :], tfull[:, :],
                       RND_C, RND_C, ALU.add, ALU.subtract)
                b.call("vector", "tensor_mul", gu[:, ic * T:(ic + 1) * T],
                       tfull[:, :], mfull[:, :])
            b.barrier()

        with contextlib.ExitStack() as sD:
            sbl = lambda n, sh, dt: sD.enter_context(nc.sbuf_tensor(n, list(sh), dt))
            wdf = sbl("p2wdf", (P, 8 * P), F32)
            wdq = [sbl(f"p2wdq{i}", (P, 32 * P), BF16) for i in range(2)]
            xrel = sbl("p2xrel", (P, TT), F32)
            yst = sbl("p2yst", (P, TT), F32)
            wq_done = {0: 0, 1: 0}
            mm_done = {0: 0, 1: 0}

            def prefD2(oc2):
                bw = oc2 % 2
                b.wait("sync", TS, mm_done[bw])
                for hf in range(4):
                    b.wait("sync", VS, wq_done[1 - bw] if hf == 0 else VS.n)
                    for ck2 in range(8):
                        cka = hf * 8 + ck2
                        b.dma("sync", wdf[:, ck2 * P:(ck2 + 1) * P],
                              dr["wdT"][cka * P:(cka + 1) * P,
                                        oc2 * P:(oc2 + 1) * P], dS)
                    b.wait("vector", dS, dS.n)
                    b.call("vector", "tensor_scalar", wdf[:, :], wdf[:, :],
                           c.smalls[:, c.C_WDS:c.C_WDS + 1], RND_C,
                           ALU.mult, ALU.add)
                    b.call("vector", "tensor_scalar", wdf[:, :], wdf[:, :],
                           RND_C, 1.0, ALU.subtract, ALU.min)
                    wq_done[bw] = b.call("vector", "tensor_scalar",
                                         wdq[bw][:, hf * 8 * P:
                                                 (hf + 1) * 8 * P],
                                         wdf[:, :],
                                         -1.0, None, ALU.max, ALU.bypass)

            prefD2(0)
            prev_ev = {0: 0, 1: 0}
            for oc2 in range(NCH):
                bw = oc2 % 2
                b.wait("tensor", VS, wq_done[bw])
                for tt in range(NTT):
                    bk = tt % 2
                    b.wait("tensor", VS, prev_ev[bk])
                    for ck2 in range(32):
                        b.call("tensor", "matmul", pb[bk][:, 0:TT],
                               wdq[bw][:, ck2 * P:(ck2 + 1) * P],
                               gu[:, ck2 * T + tt * TT:ck2 * T + (tt + 1) * TT],
                               start=(ck2 == 0), stop=(ck2 == 31), inc=TS)
                    mmT = TS.n
                    if tt == 0 and oc2 + 1 < NCH:
                        prefD2(oc2 + 1)
                    b.wait("sync", VS, prev_ev[bk])
                    b.dma("sync", xrel[:, :],
                          x_in[oc2 * P:(oc2 + 1) * P, tt * TT:(tt + 1) * TT],
                          c.dS2)
                    b.wait("vector", c.dS2, c.dS2.n)
                    b.wait("vector", TS, mmT)
                    vv = b.call("vector", "scalar_tensor_tensor", yst[:, :],
                                pb[bk][:, 0:TT],
                                c.smalls[:, c.C_WDR:c.C_WDR + 1], xrel[:, :],
                                ALU.mult, ALU.add)
                    prev_ev[bk] = vv
                    b.wait("sync", VS, vv)
                    b.dma("sync",
                          x_out[oc2 * P:(oc2 + 1) * P, tt * TT:(tt + 1) * TT],
                          yst[:, :], c.dT)
                    b.wait("vector", c.dT, c.dT.n)
                mm_done[bw] = TS.n
            b.barrier()

        pc_drain(c)
        for k in range(4):
            for eng in ("vector", "sync", "scalar", "tensor"):
                b.wait(eng, c.dPS[k], c.pc_done["w3Tt"][k])

    b.barrier()
    c.x_cur = 1


def emit_phase3(c):
    nc, b, dr = c.nc, c.b, c.dr
    dS, dG, dT, TS, VS, SS, CC = c.dS, c.dG, c.dT, c.TS, c.VS, c.SS, c.CC
    pb = c.pb
    x_in = c.x_hbm[1]
    x_out = c.x_hbm[0]
    b.wait("sync", dT, dT.n)

    with contextlib.ExitStack() as ph:
        sbf = lambda n, sh, dt: ph.enter_context(nc.sbuf_tensor(n, list(sh), dt))
        h_loc = sbf("p3hloc", (P, NCH * T), BF16)
        ctx = sbf("p3ctx", (P, NCH * T), BF16)

        with contextlib.ExitStack() as sL:
            sbl = lambda n, sh, dt: sL.enter_context(nc.sbuf_tensor(n, list(sh), dt))
            stat = sbl("p3stat", (1, 3 * TT), F32)
            sq = sbl("p3sq", (P, TT), F32)
            xt = sbl("p3xt", (P, NCH * TT), F32)
            pet = sbl("p3pet", (P, TT), F32)
            hst = sbl("p3hst", (P, TT), BF16)
            for tt in range(NTT):
                for ck in range(NCH):
                    b.dma("sync", xt[:, ck * TT:(ck + 1) * TT],
                          x_in[ck * P:(ck + 1) * P, tt * TT:(tt + 1) * TT], dS)
                b.wait("tensor", dS, dS.n)
                b.wait("scalar", dS, dS.n)
                srcs = [xt[:, ck * TT:(ck + 1) * TT] for ck in range(NCH)]
                vt = ln_stats_bcast(c, srcs, D, pb[2], pb[3], stat, sq)
                b.wait("vector", TS, vt)
                for ck in range(NCH):
                    b.dma("sync", pet[:, :],
                          dr["pe_in"][ck * P:(ck + 1) * P, tt * TT:(tt + 1) * TT],
                          c.dS2)
                    b.wait("vector", c.dS2, c.dS2.n)
                    b.call("vector", "tensor_sub", xt[:, ck * TT:(ck + 1) * TT],
                           xt[:, ck * TT:(ck + 1) * TT], pb[2][:, 0:TT])
                    b.call("vector", "tensor_mul", xt[:, ck * TT:(ck + 1) * TT],
                           xt[:, ck * TT:(ck + 1) * TT], pb[3][:, 0:TT])
                    vv = b.call("vector", "tensor_add", hst[:, :],
                                xt[:, ck * TT:(ck + 1) * TT], pet[:, :])
                    b.call("vector", "tensor_copy",
                           h_loc[:, ck * T + tt * TT:ck * T + (tt + 1) * TT],
                           hst[:, :])
                    b.wait("sync", VS, VS.n)
                    b.dma("sync",
                          c.cc_h_in[ck * P:(ck + 1) * P, tt * TT:(tt + 1) * TT],
                          hst[:, :], c.dT)
                    b.wait("vector", c.dT, c.dT.n)
                b.barrier()

        vg = b.op("gpsimd", lambda e: e.collective_compute(
            "AllGather", ALU.bypass,
            replica_groups=[[0, 1], [2, 3], [4, 5], [6, 7]],
            ins=[c.cc_h_in.ap()], outs=[c.cc_h_out.ap()]), inc=CC)
        b.wait("gpsimd", CC, vg)

        with contextlib.ExitStack() as s2:
            sbl = lambda n, sh, dt: s2.enter_context(nc.sbuf_tensor(n, list(sh), dt))
            h_full = sbl("p3hfull", (P, NCH * 2 * T), BF16)
            for ck in range(NCH):
                for hf in range(2):
                    b.dma("gpsimd",
                          h_full[:, ck * 2 * T + hf * T:ck * 2 * T + (hf + 1) * T],
                          c.cc_h_out[hf * D + ck * P:hf * D + (ck + 1) * P, :], dG)
            hload_dG = dG.n

            with contextlib.ExitStack() as s3:
                sbm = lambda n, sh, dt: s3.enter_context(nc.sbuf_tensor(n, list(sh), dt))
                wq2 = sbm("p3wq", (P, 2 * NCH * P), BF16)
                wk = sbm("p3wk", (P, NCH * P), BF16)
                wv = sbm("p3wv", (P, NCH * P), BF16)
                biases = sbm("p3bias", (P, 4 * NCH), F32)
                qp_all = sbm("p3qp", (P, NCH * T), BF16)
                kp = sbm("p3kp", (P, 2 * T), BF16)
                vp = sbm("p3vp", (P, 2 * T), BF16)
                vtm = sbm("p3vtm", (P, 32 * 130), BF16)
                pt = sbm("p3pt", (P, 4 * TT), BF16)
                rdrow = sbm("p3rd", (P, TT), F32)
                ctmp0 = sbm("p3ctmp0", (64, TT), BF16)
                ctmp1 = sbm("p3ctmp1", (64, TT), BF16)
                ctmp2 = sbm("p3ctmp2", (64, TT), BF16)

                b.dma("sync", biases[:, 0:NCH], dr["bq1"].ap(), dS)
                b.dma("sync", biases[:, NCH:2 * NCH], dr["bk1"].ap(), dS)
                b.dma("sync", biases[:, 2 * NCH:3 * NCH], dr["bv1"].ap(), dS)
                b.dma("sync", biases[:, 3 * NCH:4 * NCH], dr["bo1"].ap(), dS)
                b.wait("tensor", dS, dS.n)
                b.wait("scalar", dS, dS.n)
                b.wait("vector", dS, dS.n)

                b.call("vector", "memset",
                       vtm.ap().rearrange("p (a c) -> p a c", c=65)[:, :, 64:65], 1.0)

                for k in range(4):
                    b.wait("sync", c.dPS[k], c.pc_done["wq1t"][k])
                q_guard = {0: 0, 1: 0}
                dwv = {0: 0, 1: 0}
                prev_q = {0: 0, 1: 0}
                for p in range(NCH):
                    b2 = p % 2
                    b.wait("sync", TS, q_guard[b2])
                    b.dma("sync",
                          wq2[:, b2 * NCH * P:(b2 + 1) * NCH * P]
                          .rearrange("q (a m) -> q a m", a=NCH),
                          c.wqbt[p, :, :, :].rearrange("a p m -> p a m"),
                          c.dW[b2])
                    dwv[b2] = c.dW[b2].n
                    b.wait("tensor", c.dW[b2], dwv[b2])
                    for tt in range(NTT):
                        bk = tt % 2
                        b.wait("tensor", VS, prev_q[bk])
                        for ck in range(NCH):
                            b.call("tensor", "matmul", pb[bk][:, 0:TT],
                                   wq2[:, (b2 * NCH + ck) * P:
                                       (b2 * NCH + ck + 1) * P],
                                   h_loc[:, ck * T + tt * TT:ck * T + (tt + 1) * TT],
                                   start=(ck == 0), stop=(ck == NCH - 1), inc=TS)
                        b.wait("vector", TS, TS.n)
                        vs = b.call("vector", "tensor_scalar",
                                    qp_all[:, p * T + tt * TT:p * T + (tt + 1) * TT],
                                    pb[bk][:, 0:TT],
                                    biases[:, p:p + 1], None, ALU.add, ALU.bypass)
                        prev_q[bk] = vs
                    q_guard[b2] = TS.n

                b.wait("tensor", CC, vg)
                b.wait("tensor", dG, hload_dG)
                for k in range(4):
                    b.wait("sync", c.dPS[k], c.pc_done["wv1t"][k])
                b.barrier()

                prev_kv = {0: 0, 1: 0}
                kv_guard = 0
                c.pv_guard = 0
                c.dcx_prev = 0
                for p in range(NCH):
                    b.wait("sync", TS, kv_guard)
                    b.dma("sync", wk.ap().rearrange("q (a m) -> q a m", a=NCH),
                          c.wkbt[p, :, :, :].rearrange("a p m -> p a m"), c.dW[2])
                    b.dma("sync", wv.ap().rearrange("q (a m) -> q a m", a=NCH),
                          c.wvbt[p, :, :, :].rearrange("a p m -> p a m"), c.dW[3])
                    b.wait("tensor", c.dW[2], c.dW[2].n)
                    b.wait("tensor", c.dW[3], c.dW[3].n)
                    for tg in range(2 * NTT):
                        for j, (dst, wmat, bcol) in enumerate(
                                ((kp, wk, NCH + p), (vp, wv, 2 * NCH + p))):
                            bi = 2 + (tg * 2 + j) % 2
                            b.wait("tensor", VS, prev_kv[bi - 2])
                            for ck in range(NCH):
                                b.call("tensor", "matmul", pb[bi][:, 0:TT],
                                       wmat[:, ck * P:(ck + 1) * P],
                                       h_full[:, ck * 2 * T + tg * TT:
                                              ck * 2 * T + (tg + 1) * TT],
                                       start=(ck == 0), stop=(ck == NCH - 1), inc=TS)
                            b.wait("vector", TS, TS.n)
                            vs = b.call("vector", "tensor_scalar",
                                        dst[:, tg * TT:(tg + 1) * TT],
                                        pb[bi][:, 0:TT],
                                        biases[:, bcol:bcol + 1], None,
                                        ALU.add, ALU.bypass)
                            prev_kv[bi - 2] = vs
                    kv_guard = TS.n
                    b.wait("tensor", VS, VS.n)
                    for kc in range(32):
                        vt = b.call("tensor", "transpose", c.pbt[:, 0:P],
                                    vp[:, kc * P:(kc + 1) * P], c.identbf[:, :],
                                    inc=TS)
                        b.wait("vector", TS, vt)
                        b.call("vector", "tensor_copy",
                               vtm[:, kc * 130:kc * 130 + 64], c.pbt[:, 0:64])
                        vv = b.call("vector", "tensor_copy",
                                    vtm[:, kc * 130 + 65:kc * 130 + 129],
                                    c.pbt[:, 64:128])
                        b.wait("tensor", VS, vv)

                    for qt in range(NTT):
                        qk_T = {}
                        pv_T = {}
                        exp_v = {}
                        for w in range(33):
                            bi = w % 2
                            if w < 32:
                                b.wait("tensor", SS, exp_v.get(w - 2, 0))
                                if w == 0:
                                    b.wait("tensor", VS, c.pv_guard)
                                b.call("tensor", "matmul", pb[2 * bi][:, 0:TT],
                                       kp[0:64, w * P:(w + 1) * P],
                                       qp_all[0:64, p * T + qt * TT:
                                              p * T + (qt + 1) * TT],
                                       start=True, stop=True,
                                       tile_position=(0, 0), inc=TS)
                                b.call("tensor", "matmul", pb[2 * bi + 1][:, 0:TT],
                                       kp[64:128, w * P:(w + 1) * P],
                                       qp_all[64:128, p * T + qt * TT:
                                              p * T + (qt + 1) * TT],
                                       start=True, stop=True,
                                       tile_position=(64, 0), inc=TS)
                                qk_T[w] = TS.n
                            if w > 0:
                                wp = w - 1
                                bip = wp % 2
                                b.wait("tensor", SS, exp_v[wp])
                                b.call("tensor", "matmul", pb[4][0:65, 0:TT],
                                       vtm[:, wp * 130:wp * 130 + 65],
                                       pt[:, 2 * bip * TT:(2 * bip + 1) * TT],
                                       start=(wp == 0), stop=(wp == 31), inc=TS)
                                b.call("tensor", "matmul", pb[5][0:65, 0:TT],
                                       vtm[:, wp * 130 + 65:wp * 130 + 130],
                                       pt[:, (2 * bip + 1) * TT:(2 * bip + 2) * TT],
                                       start=(wp == 0), stop=(wp == 31), inc=TS)
                                pv_T[wp] = TS.n
                            if w < 32:
                                b.wait("scalar", TS, qk_T[w])
                                b.wait("scalar", TS, pv_T.get(w - 2, 0))
                                vs = b.call("scalar", "activation",
                                            pt[:, 2 * bi * TT:(2 * bi + 2) * TT],
                                            c.parena[:, 2 * bi * TT:
                                                     (2 * bi + 2) * TT],
                                            AF.Exp, scale=0.125,
                                            bias=c.smalls[:, c.C_NEG8:
                                                          c.C_NEG8 + 1],
                                            inc=SS)
                                exp_v[w] = vs
                        b.wait("vector", TS, pv_T[31])
                        r0 = b.call("vector", "reciprocal",
                                    rdrow[64:65, 0:TT], pb[4][64:65, 0:TT])
                        b.call("vector", "tensor_copy", ctmp0[0:64, :],
                               pb[4][0:64, 0:TT])
                        cpv = b.call("vector", "tensor_copy", ctmp1[0:64, :],
                                     pb[5][0:64, 0:TT])
                        c.pv_guard = cpv
                        b.wait("tensor", VS, r0)
                        bc0 = b.call("tensor", "matmul", pb[6][0:64, 0:TT],
                                     c.ones_mat[64:65, 0:64],
                                     rdrow[64:65, 0:TT],
                                     start=True, stop=True, inc=TS)
                        b.wait("vector", TS, bc0)
                        m0 = b.call("vector", "tensor_mul",
                                    ctx[0:64, p * T + qt * TT:
                                        p * T + (qt + 1) * TT],
                                    ctmp0[0:64, :], pb[6][0:64, 0:TT])
                        r1 = b.call("vector", "reciprocal",
                                    rdrow[64:65, 0:TT], pb[5][64:65, 0:TT])
                        b.wait("tensor", VS, r1)
                        bc1 = b.call("tensor", "matmul", pb[6][0:64, 0:TT],
                                     c.ones_mat[64:65, 0:64],
                                     rdrow[64:65, 0:TT],
                                     start=True, stop=True, inc=TS)
                        b.wait("vector", TS, bc1)
                        b.wait("vector", c.dCX, c.dcx_prev)
                        m1 = b.call("vector", "tensor_mul", ctmp2[0:64, :],
                                    ctmp1[0:64, :], pb[6][0:64, 0:TT])
                        b.wait("sync", VS, m1)
                        b.dma("sync",
                              ctx[64:128, p * T + qt * TT:p * T + (qt + 1) * TT],
                              ctmp2[0:64, :], c.dCX)
                        c.dcx_prev = c.dCX.n
                    b.barrier()

        with contextlib.ExitStack() as s4:
            sbl = lambda n, sh, dt: s4.enter_context(nc.sbuf_tensor(n, list(sh), dt))
            wo = sbl("p3wo", (P, NCH * D), BF16)
            yf = sbl("p3y", (P, NCH * T), F32)
            bias4 = sbl("p3bo", (P, 3 * NCH), F32)
            stat = sbl("p3stat2", (1, 3 * TT), F32)
            sq = sbl("p3sq2", (P, TT), F32)
            xrel = sbl("p3xrel", (P, TT), F32)
            yst = sbl("p3yst", (P, TT), F32)
            for k in range(4):
                b.wait("sync", c.dPS[k], c.pc_done["wo1t"][k])
            for pp in range(NCH):
                b.dma("sync", wo[:, pp * D:(pp + 1) * D]
                      .rearrange("q (a m) -> q a m", a=NCH),
                      c.wobt[pp, :, :, :].rearrange("a p m -> p a m"), dS)
            b.wait("tensor", c.dCX, c.dcx_prev)
            b.dma("sync", bias4[:, 0:NCH], dr["bo1"].ap(), dS)
            b.dma("sync", bias4[:, NCH:2 * NCH], dr["gam1"].ap(), dS)
            b.dma("sync", bias4[:, 2 * NCH:3 * NCH], dr["bet1"].ap(), dS)
            b.wait("tensor", dS, dS.n)
            b.wait("scalar", dS, dS.n)
            b.wait("vector", dS, dS.n)
            prev_read = {0: 0, 1: 0}
            for oc in range(NCH):
                for tt in range(NTT):
                    bk = (oc * NTT + tt) % 2
                    b.wait("tensor", VS, prev_read[bk])
                    for pp in range(NCH):
                        b.call("tensor", "matmul", pb[bk][:, 0:TT],
                               wo[:, pp * D + oc * P:pp * D + (oc + 1) * P],
                               ctx[:, pp * T + tt * TT:pp * T + (tt + 1) * TT],
                               start=(pp == 0), stop=(pp == NCH - 1), inc=TS)
                    mmT = TS.n
                    b.wait("vector", TS, mmT)
                    vv = b.call("vector", "scalar_tensor_tensor",
                                yf[:, oc * T + tt * TT:oc * T + (tt + 1) * TT],
                                pb[bk][:, 0:TT], bias4[:, oc:oc + 1],
                                h_loc[:, oc * T + tt * TT:oc * T + (tt + 1) * TT],
                                ALU.add, ALU.add)
                    prev_read[bk] = vv
            b.barrier()
            for tt in range(NTT):
                srcs = [yf[:, ck * T + tt * TT:ck * T + (tt + 1) * TT]
                        for ck in range(NCH)]
                vt = ln_stats_bcast(c, srcs, D, pb[2], pb[3], stat, sq)
                b.wait("vector", TS, vt)
                for ck in range(NCH):
                    b.dma("sync", xrel[:, :],
                          x_in[ck * P:(ck + 1) * P, tt * TT:(tt + 1) * TT], c.dS2)
                    b.call("vector", "tensor_sub", sq[:, :],
                           yf[:, ck * T + tt * TT:ck * T + (tt + 1) * TT],
                           pb[2][:, 0:TT])
                    b.call("vector", "tensor_mul", sq[:, :], sq[:, :],
                           pb[3][:, 0:TT])
                    b.call("vector", "tensor_scalar", sq[:, :], sq[:, :],
                           bias4[:, NCH + ck:NCH + ck + 1],
                           bias4[:, 2 * NCH + ck:2 * NCH + ck + 1],
                           ALU.mult, ALU.add)
                    b.wait("vector", c.dS2, c.dS2.n)
                    vv = b.call("vector", "tensor_add", yst[:, :], sq[:, :],
                                xrel[:, :])
                    b.wait("sync", VS, vv)
                    b.dma("sync",
                          x_out[ck * P:(ck + 1) * P, tt * TT:(tt + 1) * TT],
                          yst[:, :], c.dT)
                    b.wait("vector", c.dT, c.dT.n)
                b.barrier()

    b.barrier()
    c.x_cur = 0


def emit_phase4(c):
    nc, b, dr = c.nc, c.b, c.dr
    dS, dT, TS, VS, SS = c.dS, c.dT, c.TS, c.VS, c.SS
    pb = c.pb
    x_in = c.x_hbm[0]
    b.wait("sync", dT, dT.n)
    RW = 32 * TT

    with contextlib.ExitStack() as ph:
        sbf = lambda n, sh, dt: ph.enter_context(nc.sbuf_tensor(n, list(sh), dt))
        xln = sbf("p4xln", (P, NCH * T), BF16)
        ha = sbf("p4ha", (P, 4 * RW), BF16)
        scal = sbf("p4scal", (P, 3 * 32 + NCH), F32)

        b.dma("sync", scal[:, 0:32], dr["s1c"].ap(), dS)
        b.dma("sync", scal[:, 32:64], dr["s2c"].ap(), dS)
        b.dma("sync", scal[:, 64:96], dr["sh1c"].ap(), dS)
        b.dma("sync", scal[:, 96:96 + NCH], dr["s3c"].ap(), dS)
        b.wait("sync", dS, dS.n)
        b.wait("scalar", dS, dS.n)
        b.wait("vector", dS, dS.n)

        with contextlib.ExitStack() as sLn:
            sbn = lambda n, sh, dt: sLn.enter_context(nc.sbuf_tensor(n, list(sh), dt))
            stat = sbn("p4stat", (1, 3 * TT), F32)
            sq = sbn("p4sq", (P, TT), F32)
            xt = sbn("p4xt", (P, NCH * TT), F32)
            emit_ln_to(c, x_in, xln, stat, sq, xt, 0)

        hsl = [sbf(f"p4hsl{i}", (P, 32 * P), BF16) for i in range(4)]

        regions = [(xln, 0), (ha, 0), (ha, RW), (ha, 2 * RW), (ha, 3 * RW)]

        assert c.pc_state["si"] == len(c.pc_jobs), "pre-cast incomplete"
        pc_done = c.pc_done

        g_mm = {i: 0 for i in range(4)}
        g_ps_s = {0: 0, 1: 0}
        g_ps_v = {0: 0, 1: 0}
        dg_val = {i: 0 for i in range(4)}

        def prefA(ic):
            bi = ic % 4
            sem = c.dW[bi]
            b.wait("sync", TS, g_mm[bi])
            b.dma("sync", hsl[bi][:, 0:NCH * P].rearrange("q (a m) -> q a m", a=NCH),
                  c.w1bt[ic, :, :, :].rearrange("a p m -> p a m"), sem)
            b.dma("sync", hsl[bi][:, NCH * P:2 * NCH * P]
                  .rearrange("q (a m) -> q a m", a=NCH),
                  c.w2bt[ic, :, :, :].rearrange("a p m -> p a m"), sem)
            dg_val[bi] = sem.n

        with contextlib.ExitStack() as sA:
            t1 = sA.enter_context(nc.sbuf_tensor("p4t1", [P, 2 * TT], F32))
            for k in range(4):
                b.wait("sync", c.dPS[k], pc_done["w2Tt"][k])
            for ic in range(4):
                prefA(ic)
            for ic in range(32):
                bi = ic % 4
                b.wait("tensor", c.dW[bi], dg_val[bi])
                for tt in range(NTT):
                    gi = ic * NTT + tt
                    b2 = gi % 2
                    pa, pbk = pb[2 * b2], pb[2 * b2 + 1]
                    rten, rbase = regions[1 + tt]
                    b.wait("tensor", SS, g_ps_s[b2])
                    b.wait("tensor", VS, g_ps_v[b2])
                    for ck in range(NCH):
                        b.call("tensor", "matmul", pa[:, 0:TT],
                               hsl[bi][:, ck * P:(ck + 1) * P],
                               xln[:, ck * T + tt * TT:ck * T + (tt + 1) * TT],
                               start=(ck == 0), stop=(ck == NCH - 1), inc=TS)
                    mmA = TS.n
                    for ck in range(NCH):
                        b.call("tensor", "matmul", pbk[:, 0:TT],
                               hsl[bi][:, (NCH + ck) * P:(NCH + ck + 1) * P],
                               xln[:, ck * T + tt * TT:ck * T + (tt + 1) * TT],
                               start=(ck == 0), stop=(ck == NCH - 1), inc=TS)
                    mmB = TS.n
                    b.wait("scalar", TS, mmA)
                    b.wait("scalar", VS, g_ps_v[b2])
                    vs = b.call("scalar", "activation",
                                t1[:, b2 * TT:(b2 + 1) * TT], pa[:, 0:TT],
                                AF.Silu, scale=scal[:, ic:ic + 1], inc=SS)
                    g_ps_s[b2] = vs
                    b.wait("vector", TS, mmB)
                    b.wait("vector", SS, vs)
                    vv = b.call("vector", "scalar_tensor_tensor",
                                rten[:, rbase + ic * TT:rbase + (ic + 1) * TT],
                                pbk[:, 0:TT], scal[:, 32 + ic:32 + ic + 1],
                                t1[:, b2 * TT:(b2 + 1) * TT], ALU.mult, ALU.mult)
                    g_ps_v[b2] = vv
                g_mm[bi] = TS.n
                if ic + 4 < 32:
                    prefA(ic + 4)
        b.barrier()

        g_mm = {i: 0 for i in range(4)}
        g_ps_s = {i: 0 for i in range(4)}
        g_ps_v = {i: 0 for i in range(4)}
        dw_val = {i: 0 for i in range(4)}

        def mk_pref(wsrc, n_oc):
            def pref(i):
                oc = i % n_oc
                bi = i % 4
                b.wait("sync", TS, g_mm[bi])
                b.dma("sync", hsl[bi].ap().rearrange("q (a m) -> q a m", a=32),
                      wsrc[oc, :, :, :].rearrange("a p m -> p a m"), c.dW[bi])
                dw_val[bi] = c.dW[bi].n
            return pref

        prev = regions[1:5]
        for stage, wsrc, pcn, scol in (("B", c.h1bt, "h1Tt", None),
                                       ("C", c.h2bt, "h2Tt", "sh2c")):
            if scol is not None:
                b.wait("sync", SS, SS.n)
                b.dma("sync", scal[:, 64:96], dr[scol].ap(), dS)
                b.wait("sync", dS, dS.n)
                b.wait("scalar", dS, dS.n)
            cur = ([regions[0], regions[1], regions[2], regions[3]] if stage == "B"
                   else [regions[4], regions[0], regions[1], regions[2]])
            for k in range(4):
                b.wait("sync", c.dPS[k], pc_done[pcn][k])
            pref = mk_pref(wsrc, 32)
            for i in range(4):
                pref(i)
            for tt in range(NTT):
                sten, sbase = prev[tt]
                rten, rbase = cur[tt]
                for oc in range(32):
                    i = tt * 32 + oc
                    bi = i % 4
                    b.wait("tensor", c.dW[bi], dw_val[bi])
                    b.wait("tensor", SS, g_ps_s[bi])
                    b.wait("tensor", VS, g_ps_v[bi])
                    for ick in range(32):
                        b.call("tensor", "matmul", pb[bi][:, 0:TT],
                               hsl[bi][:, ick * P:(ick + 1) * P],
                               sten[:, sbase + ick * TT:sbase + (ick + 1) * TT],
                               start=(ick == 0), stop=(ick == 31), inc=TS)
                    g_mm[bi] = TS.n
                    if i + 4 < 128:
                        pref(i + 4)
                    b.wait("scalar", TS, g_mm[bi])
                    vs = b.call("scalar", "activation",
                                rten[:, rbase + oc * TT:rbase + (oc + 1) * TT],
                                pb[bi][:, 0:TT], AF.Gelu,
                                scale=scal[:, 64 + oc:64 + oc + 1], inc=SS)
                    g_ps_s[bi] = vs
            prev = cur

        with contextlib.ExitStack() as sD:
            xrel = sD.enter_context(nc.sbuf_tensor("p4xrel", [P, TT], F32))
            yst = sD.enter_context(nc.sbuf_tensor("p4yst", [P, TT], F32))
            for k in range(4):
                b.wait("sync", c.dPS[k], pc_done["w3Tt"][k])
            pref = mk_pref(c.w3bt, NCH)
            prev_stt = 0
            for i in range(4):
                pref(i)
            for tt in range(NTT):
                sten, sbase = prev[tt]
                for oc2 in range(NCH):
                    i = tt * NCH + oc2
                    bi = i % 4
                    b.wait("tensor", c.dW[bi], dw_val[bi])
                    b.wait("tensor", SS, g_ps_s[bi])
                    b.wait("tensor", VS, g_ps_v[bi])
                    for ick in range(32):
                        b.call("tensor", "matmul", pb[bi][:, 0:TT],
                               hsl[bi][:, ick * P:(ick + 1) * P],
                               sten[:, sbase + ick * TT:sbase + (ick + 1) * TT],
                               start=(ick == 0), stop=(ick == 31), inc=TS)
                    g_mm[bi] = TS.n
                    if i + 4 < 32:
                        pref(i + 4)
                    b.wait("sync", VS, prev_stt)
                    b.dma("sync", xrel[:, :],
                          x_in[oc2 * P:(oc2 + 1) * P, tt * TT:(tt + 1) * TT],
                          c.dS2)
                    b.wait("vector", c.dS2, c.dS2.n)
                    b.wait("vector", TS, g_mm[bi])
                    vv = b.call("vector", "scalar_tensor_tensor", yst[:, :],
                                pb[bi][:, 0:TT],
                                scal[:, 96 + oc2:96 + oc2 + 1], xrel[:, :],
                                ALU.mult, ALU.add)
                    g_ps_v[bi] = vv
                    prev_stt = vv
                    b.wait("sync", VS, vv)
                    b.dma("sync",
                          c.y_out[oc2 * P:(oc2 + 1) * P, tt * TT:(tt + 1) * TT],
                          yst[:, :], dT)
                    b.wait("vector", dT, dT.n)
            b.barrier()

    b.barrier()
    c.x_cur = -1


def emit_finish(c):
    if PHASES >= 4:
        return
    b = c.b
    src = c.x_hbm[c.x_cur] if hasattr(c, "x_cur") else c.dr["x_in"]
    with contextlib.ExitStack() as ph:
        t = ph.enter_context(c.nc.sbuf_tensor("ycopy", [P, TT], F32))
        for ck in range(NCH):
            for q4 in range(NTT):
                b.dma("sync", t[:, :],
                      src[ck * P:(ck + 1) * P, q4 * TT:(q4 + 1) * TT], c.dS)
                b.wait("sync", c.dS, c.dS.n)
                b.dma("sync", c.y_out[ck * P:(ck + 1) * P, q4 * TT:(q4 + 1) * TT],
                      t[:, :], c.dT)
                b.wait("sync", c.dT, c.dT.n)
        b.barrier()


_NC_CACHE = [None]


def kernel(**inputs):
    maps = prep_inputs(inputs)
    if _NC_CACHE[0] is None:
        _NC_CACHE[0] = build_nc()
    nc = _NC_CACHE[0]
    res = run_bass_kernel_spmd(nc, maps, core_ids=list(range(8)))
    out = np.empty((4, SEQ, D), np.float32)
    for cc in range(8):
        bi, half = divmod(cc, 2)
        y = np.asarray(res.results[cc]["y"])
        out[bi, half * T:(half + 1) * T, :] = y.T
    return out


# version 17 (final)
# speedup vs baseline: 1.0749x; 1.0080x over previous
import contextlib
import numpy as np

import concourse.bass as bass
import concourse.mybir as mybir
import concourse.library_config as LC
from concourse.bass_isa import ReduceOp
from concourse.bass_utils import run_bass_kernel_spmd

F32 = mybir.dt.float32
BF16 = mybir.dt.bfloat16
AF = mybir.ActivationFunctionType
ALU = mybir.AluOpType
AXX = mybir.AxisListType.X

P = 128
T = 2048
D = 1024
NCH = 8
NTC = 16
TT = 512
NTT = 4
RND_C = 12582912.0
EPS = 1e-5

NH = 4
DK = 128
DV = 256
CH = 32
NCK = 64

HEADS = 16
DH = 64
SEQ = 4096

PHASES = 4
PHASE_BARR = []
P4STOP = None


class Sem:
    def __init__(self, h):
        self.h = h
        self.n = 0

    def bump(self, k=1):
        self.n += k
        return self.n


class Builder:
    ENGS = ("tensor", "vector", "scalar", "gpsimd", "sync")

    def __init__(self, nc, es):
        self.nc = nc
        self.es = es
        self.q = {e: [] for e in self.ENGS}
        self.last_wait = {}
        self.sems = {}

    def sem(self, name):
        if name not in self.sems:
            self.sems[name] = Sem(self.es.enter_context(self.nc.semaphore(name)))
        return self.sems[name]

    def wait(self, eng, sem, val):
        if val is None or val <= 0:
            return
        key = (eng, sem.h.name)
        if self.last_wait.get(key, -1) >= val:
            return
        self.last_wait[key] = val
        self.q[eng].append(lambda e, s=sem, v=val: e.wait_ge(s.h, v))

    def op(self, eng, fn, inc=None, k=1, serial=True):
        ser = getattr(self, "serial", {}).get(eng)
        if serial and ser is not None:
            self.wait(eng, ser, ser.n)
        if inc is None and ser is not None:
            inc = ser
        if inc is not None:
            v = inc.bump(k)
            def run(e, fn=fn, s=inc, kk=k):
                fn(e).then_inc(s.h, kk)
            self.q[eng].append(run)
            return v
        self.q[eng].append(lambda e, fn=fn: fn(e))
        return None

    def call(self, eng, meth, *args, inc=None, k=1, serial=True, **kwargs):
        def fn(e, meth=meth, args=args, kwargs=kwargs):
            return getattr(e, meth)(*args, **kwargs)
        return self.op(eng, fn, inc=inc, k=k, serial=serial)

    def dma(self, eng, dst, src, sem):
        return self.op(eng, lambda e, d=dst, s=src: e.dma_start(out=d, in_=s),
                       inc=sem, k=16, serial=False)

    def barrier(self):
        bs = self.sem("BARR")
        v = bs.n + len(self.ENGS)
        for e in self.ENGS:
            ser = getattr(self, "barrier_serial", {}).get(e)
            if ser is not None:
                self.wait(e, ser, ser.n)
            self.q[e].append(lambda eng, s=bs: eng.nop().then_inc(s.h, 1))
            self.q[e].append(lambda eng, s=bs, vv=v: eng.wait_ge(s.h, vv))
            self.last_wait[(e, bs.h.name)] = v
        bs.n = v

    def replay(self, block):
        for name in self.ENGS:
            ops = self.q[name]
            getattr(block, name)(lambda eng, ops=ops: [f(eng) for f in ops])


def _ct(a):
    return np.ascontiguousarray(np.asarray(a))


def _pe_const():
    pos = np.arange(SEQ, dtype=np.float64)[:, None]
    div = np.exp(np.arange(0, D, 2, dtype=np.float64) * (-np.log(10000.0) / D))
    pe = np.zeros((SEQ, D), np.float32)
    pe[:, 0::2] = np.sin(pos * div).astype(np.float32)
    pe[:, 1::2] = np.cos(pos * div).astype(np.float32)
    return pe


def prep_inputs(inp):
    x = np.asarray(inp["L0_attn_hidden_states"])

    def fm(mat):
        return _ct(mat.T)

    wx2 = np.asarray(inp["L0_attn_Wx2"]).reshape(D, 5, 32)
    xb = np.asarray(inp["L0_attn_x_bias"])
    wx2T = np.stack([np.concatenate([_ct(wx2[:, g, :].T), xb[g][None, :]], axis=0)
                     for g in range(5)], axis=0)

    common = {
        "wx0T": _ct(np.asarray(inp["L0_attn_Wx0"]).T),
        "wx2T": _ct(wx2T),
        "rWT": _ct(np.asarray(inp["L0_attn_rW"]).T),
        "wAT": _ct(np.asarray(inp["L0_attn_wA"]).T),
        "wBT": _ct(np.asarray(inp["L0_attn_wB"]).T),
        "kWT": _ct(np.asarray(inp["L0_attn_kW"]).T),
        "vWT": _ct(np.asarray(inp["L0_attn_vW"]).T),
        "gAT": _ct(np.asarray(inp["L0_attn_gA"]).T),
        "gBT": _ct(np.asarray(inp["L0_attn_gB"]).T),
        "oWT": _ct(np.asarray(inp["L0_attn_oW"]).T),
        "mu0c": _ct(np.asarray(inp["L0_attn_mu0"]).reshape(NCH, P).T),
        "wbc": _ct(np.asarray(inp["L0_attn_wb"]).reshape(NH, P).T),
        "gbr": _ct(np.asarray(inp["L0_attn_gb"])[None, :]),
        "uc": _ct(np.asarray(inp["L0_attn_bonus"]).T),
        "lngr": _ct(np.asarray(inp["L0_attn_ln_g"])[None, :]),
        "lnbr": _ct(np.asarray(inp["L0_attn_ln_b"])[None, :]),
        "wgT": _ct(np.asarray(inp["L0_ffn_w_gate"]).T),
        "wuT": _ct(np.asarray(inp["L0_ffn_w_up"]).T),
        "wdT": _ct(np.asarray(inp["L0_ffn_w_down"]).T),
        "wq1t": _ct(np.asarray(inp["L1_attn_Wq"]).reshape(8, 128, 8, 128)
                     .transpose(2, 0, 1, 3)),
        "wk1t": _ct(np.asarray(inp["L1_attn_Wk"]).reshape(8, 128, 8, 128)
                     .transpose(2, 0, 1, 3)),
        "wv1t": _ct(np.asarray(inp["L1_attn_Wv"]).reshape(8, 128, 8, 128)
                     .transpose(2, 0, 1, 3)),
        "wo1t": _ct(np.asarray(inp["L1_attn_Wo"]).reshape(8, 128, 8, 128)
                     .transpose(0, 2, 1, 3)),
        "bq1": _ct(np.asarray(inp["L1_attn_bq"]).reshape(NCH, P).T),
        "bk1": _ct(np.asarray(inp["L1_attn_bk"]).reshape(NCH, P).T),
        "bv1": _ct(np.asarray(inp["L1_attn_bv"]).reshape(NCH, P).T),
        "bo1": _ct(np.asarray(inp["L1_attn_bo"]).reshape(NCH, P).T),
        "gam1": _ct(np.asarray(inp["L1_attn_gamma"]).reshape(NCH, P).T),
        "bet1": _ct(np.asarray(inp["L1_attn_beta"]).reshape(NCH, P).T),
        "w1Tt": _ct(np.asarray(inp["L1_ffn_w1_w"]).reshape(32, 128, 8, 128)
                    .transpose(0, 2, 3, 1)),
        "w2Tt": _ct(np.asarray(inp["L1_ffn_w2_w"]).reshape(32, 128, 8, 128)
                    .transpose(0, 2, 3, 1)),
        "h1Tt": _ct(np.asarray(inp["L1_ffn_h1_w"]).reshape(32, 128, 32, 128)
                    .transpose(0, 2, 3, 1)),
        "h2Tt": _ct(np.asarray(inp["L1_ffn_h2_w"]).reshape(32, 128, 32, 128)
                    .transpose(0, 2, 3, 1)),
        "w3Tt": _ct(np.asarray(inp["L1_ffn_w3_w"]).reshape(8, 128, 32, 128)
                    .transpose(0, 2, 3, 1)),
        "s1c": _ct(np.asarray(inp["L1_ffn_w1_s"]).reshape(32, P).T),
        "s2c": _ct(np.asarray(inp["L1_ffn_w2_s"]).reshape(32, P).T),
        "sh1c": _ct(np.asarray(inp["L1_ffn_h1_s"]).reshape(32, P).T),
        "sh2c": _ct(np.asarray(inp["L1_ffn_h2_s"]).reshape(32, P).T),
        "s3c": _ct(np.asarray(inp["L1_ffn_w3_s"]).reshape(NCH, P).T),
    }

    pe = _pe_const()
    maps = []
    for c in range(8):
        b, half = divmod(c, 2)
        t0 = half * T
        m = dict(common)
        m["x_in"] = fm(x[b, t0:t0 + T, :])
        if half == 0:
            m["xprev_in"] = np.zeros((P, NCH), np.float32)
            m["flag"] = np.zeros((P, 1), np.float32)
        else:
            m["xprev_in"] = _ct(x[b, t0 - 1, :].reshape(NCH, P).T)
            m["flag"] = np.ones((P, 1), np.float32)
        m["pe_in"] = fm(pe[t0:t0 + T, :])
        maps.append(m)
    return maps


class Ctx:
    pass


def _declare_io(nc, c):
    dr = {}
    def din(name, shape, dt=F32):
        dr[name] = nc.dram_tensor(name, list(shape), dt, kind="ExternalInput")
    c.dr = dr
    din("x_in", (D, T)); din("xprev_in", (P, NCH)); din("flag", (P, 1))
    din("pe_in", (D, T))
    din("wx0T", (D, 160)); din("wx2T", (5, 33, D))
    din("rWT", (D, 512)); din("wAT", (D, 64)); din("wBT", (64, 512))
    din("kWT", (D, 512)); din("vWT", (D, D)); din("gAT", (D, 64))
    din("gBT", (64, D)); din("oWT", (D, D))
    din("mu0c", (P, NCH)); din("wbc", (P, NH)); din("gbr", (1, D))
    din("uc", (P, NH)); din("lngr", (1, D)); din("lnbr", (1, D))
    din("wgT", (D, 4096)); din("wuT", (D, 4096)); din("wdT", (4096, D))
    for n in ("wq1t", "wk1t", "wv1t", "wo1t"):
        din(n, (NCH, NCH, P, P))
    for n in ("bq1", "bk1", "bv1", "bo1", "gam1", "bet1"):
        din(n, (P, NCH))
    din("w1Tt", (32, 8, P, P)); din("w2Tt", (32, 8, P, P))
    din("h1Tt", (32, 32, P, P)); din("h2Tt", (32, 32, P, P))
    din("w3Tt", (8, 32, P, P))
    for n in ("s1c", "s2c", "sh1c", "sh2c"):
        din(n, (P, 32))
    din("s3c", (P, NCH))

    c.y_out = nc.dram_tensor("y", [D, T], F32, kind="ExternalOutput")
    c.x_hbm = [nc.dram_tensor(f"x_hbm{i}", [D, T], F32, kind="Internal")
               for i in range(2)]
    c.cc_s_in = nc.dram_tensor("cc_s_in", [DK, NH * DV], F32, kind="Internal")
    c.cc_s_out = nc.dram_tensor("cc_s_out", [2 * DK, NH * DV], F32, kind="Internal")
    c.cc_h_in = nc.dram_tensor("cc_h_in", [D, T], BF16, kind="Internal")
    c.h1bt = nc.dram_tensor("h1bt", [32, 32, P, P], BF16, kind="Internal")
    c.w1bt = nc.dram_tensor("w1bt", [32, NCH, P, P], BF16, kind="Internal")
    c.w2bt = nc.dram_tensor("w2bt", [32, NCH, P, P], BF16, kind="Internal")
    c.wqbt = nc.dram_tensor("wqbt", [NCH, NCH, P, P], BF16, kind="Internal")
    c.wkbt = nc.dram_tensor("wkbt", [NCH, NCH, P, P], BF16, kind="Internal")
    c.wvbt = nc.dram_tensor("wvbt", [NCH, NCH, P, P], BF16, kind="Internal")
    c.wobt = nc.dram_tensor("wobt", [NCH, NCH, P, P], BF16, kind="Internal")
    c.h2bt = nc.dram_tensor("h2bt", [32, 32, P, P], BF16, kind="Internal")
    c.w3bt = nc.dram_tensor("w3bt", [8, 32, P, P], BF16, kind="Internal")
    c.cc_h_out = nc.dram_tensor("cc_h_out", [2 * D, T], BF16, kind="Internal")

    jj, ii = np.meshgrid(np.arange(P), np.arange(P), indexing="ij")
    rmask = ((ii // CH == jj // CH) & (ii > jj)).astype(np.float32)
    c.rmask_t = nc.inline_tensor(rmask, name="rwkv_mask")
    c.ident_t = nc.inline_tensor(np.eye(P, dtype=np.float32), name="ident_f")
    c.ones_col_t = nc.inline_tensor(np.ones((P, 1), np.float32), name="ones_colc")
    c.ones_row_t = nc.inline_tensor(np.ones((1, P), np.float32), name="ones_rowc")


def build_nc():
    nc = bass.Bass("TRN2", target_bir_lowering=False)
    es = contextlib.ExitStack()
    c = Ctx()
    c.nc = nc
    b = Builder(nc, es)
    c.b = b
    _declare_io(nc, c)

    c.dS = b.sem("dS"); c.dC = b.sem("dC"); c.dG = b.sem("dG")
    c.dT = b.sem("dT")
    c.dS2 = b.sem("dS2")
    c.dG2 = b.sem("dG2")
    c.dGs = [c.dG, c.dG2, b.sem("dG3"), b.sem("dG4")]
    c.dCX = b.sem("dCX")
    c.dW = [b.sem(f"dW{i}") for i in range(4)]
    c.dPC = b.sem("dPC")
    c.dPL = [b.sem(f"dPL{i}") for i in range(4)]
    c.dPS = [b.sem(f"dPS{i}") for i in range(4)]
    c.TS = b.sem("TS"); c.VS = b.sem("VS"); c.SS = b.sem("SS"); c.GS = b.sem("GS")
    c.CC = b.sem("CC")
    b.serial = {"vector": c.VS, "scalar": c.SS, "gpsimd": c.GS}
    b.barrier_serial = {"vector": c.VS, "scalar": c.SS, "gpsimd": c.GS,
                        "tensor": c.TS}

    with es:
        def sb(name, shape, dt):
            return es.enter_context(nc.sbuf_tensor(name, list(shape), dt))
        c.sb = sb

        c.identsb = sb("identsb", (P, P), F32)
        c.identbf = sb("identbf", (P, P), BF16)
        c.rmask_sb = sb("rmask_sb", (P, P), BF16)
        c.ones_col = sb("ones_col", (P, 1), F32)
        c.ones_row = sb("ones_row", (1, P), F32)
        c.smalls = sb("smalls", (P, 64), F32)
        c.zrow = sb("zrow", (1, 256), BF16)
        c.ones_mat = sb("ones_mat", (P, 64), F32)
        c.C_MU0, c.C_WB, c.C_U, c.C_FLAG, c.C_NEG8 = 0, 8, 12, 16, 17
        c.C_WGS, c.C_WGR, c.C_WUS, c.C_WUR, c.C_WDS, c.C_WDR = 25, 26, 27, 28, 29, 30
        c.C_BQ, c.C_BK, c.C_BV, c.C_BO, c.C_GAM, c.C_BET, c.C_S3 = \
            24, 32, 40, 48, 56, 60, 52

        c.parena = es.enter_context(nc.psum_tensor("parena", [P, 7 * TT], F32))
        c.pb = [c.parena[:, i * TT:(i + 1) * TT] for i in range(7)]
        c.pbt = es.enter_context(nc.psum_tensor("pbt", [P, 2 * P], BF16))
        pc_init(c)

        PHASE_BARR.clear()
        emit_preload(c)
        PHASE_BARR.append(b.sem("BARR").n)
        emit_phase1(c)
        PHASE_BARR.append(b.sem("BARR").n)
        if PHASES >= 2:
            emit_phase2(c)
            PHASE_BARR.append(b.sem("BARR").n)
        if PHASES >= 3:
            emit_phase3(c)
            PHASE_BARR.append(b.sem("BARR").n)
        if PHASES >= 4:
            emit_phase4(c)
            PHASE_BARR.append(b.sem("BARR").n)
        emit_finish(c)

        block = es.enter_context(nc.Block())
        b.replay(block)
    return nc


def ln_stats_bcast(c, src_tiles, n_d, pb_m, pb_r, stat, sq):
    b = c.b
    W = src_tiles[0].shape[-1]
    TSm, VSm, SSm = c.TS, c.VS, c.SS
    pbs, pbq = c.pb[0], c.pb[1]
    for ck in range(NCH):
        b.call("tensor", "matmul", pbs[0:1, 0:W], c.ones_col[:, :], src_tiles[ck],
               start=(ck == 0), stop=(ck == NCH - 1), inc=TSm)
    for ck in range(NCH):
        vs = b.call("scalar", "activation", sq[:, 0:W], src_tiles[ck], AF.Square,
                    inc=SSm)
        b.wait("tensor", SSm, vs)
        vt = b.call("tensor", "matmul", pbq[0:1, 0:W], c.ones_col[:, :], sq[:, 0:W],
                    start=(ck == 0), stop=(ck == NCH - 1), inc=TSm)
        b.wait("scalar", TSm, vt)
    b.wait("vector", TSm, vt)
    b.call("vector", "tensor_scalar_mul", stat[0:1, 0:W], pbs[0:1, 0:W], 1.0 / n_d)
    b.call("vector", "tensor_scalar_mul", stat[0:1, W:2 * W], pbq[0:1, 0:W], 1.0 / n_d)
    b.call("vector", "tensor_mul", stat[0:1, 2 * W:3 * W], stat[0:1, 0:W],
           stat[0:1, 0:W])
    vv = b.call("vector", "scalar_tensor_tensor", stat[0:1, W:2 * W],
                stat[0:1, W:2 * W], EPS, stat[0:1, 2 * W:3 * W],
                ALU.add, ALU.subtract)
    b.wait("scalar", c.VS, vv)
    vs = b.call("scalar", "activation", stat[0:1, 2 * W:3 * W], stat[0:1, W:2 * W],
                AF.Sqrt, inc=SSm)
    b.wait("vector", SSm, vs)
    vv = b.call("vector", "reciprocal", stat[0:1, W:2 * W], stat[0:1, 2 * W:3 * W])
    b.wait("tensor", c.VS, vv)
    b.call("tensor", "matmul", pb_m[:, 0:W], c.ones_row[:, :], stat[0:1, 0:W],
           start=True, stop=True, inc=TSm)
    vt = b.call("tensor", "matmul", pb_r[:, 0:W], c.ones_row[:, :], stat[0:1, W:2 * W],
                start=True, stop=True, inc=TSm)
    return vt


def emit_preload(c):
    b, dr = c.b, c.dr
    b.dma("sync", c.identsb[:, :], c.ident_t.ap(), c.dS)
    b.dma("gpsimd", c.identbf[:, :], c.ident_t.ap(), c.dG)
    b.dma("gpsimd", c.rmask_sb[:, :], c.rmask_t.ap(), c.dG)
    b.dma("sync", c.ones_col[:, :], c.ones_col_t.ap(), c.dS)
    b.dma("sync", c.ones_row[:, :], c.ones_row_t.ap(), c.dS)
    b.dma("sync", c.smalls[:, c.C_MU0:c.C_MU0 + NCH], c.dr["mu0c"].ap(), c.dS)
    b.dma("sync", c.smalls[:, c.C_WB:c.C_WB + NH], dr["wbc"].ap(), c.dS)
    b.dma("sync", c.smalls[:, c.C_U:c.C_U + NH], dr["uc"].ap(), c.dS)
    b.dma("sync", c.smalls[:, c.C_FLAG:c.C_FLAG + 1], dr["flag"].ap(), c.dS)
    b.call("vector", "memset", c.zrow[:, :], 0.0)
    b.call("vector", "memset", c.ones_mat[:, :], 1.0)
    b.call("vector", "memset", c.smalls[:, c.C_NEG8:c.C_NEG8 + 1], -8.0)
    b.barrier()


def emit_ln_to(c, src_dram, xln, stat, sq, xt, shift_cols):
    b = c.b
    pb = c.pb
    Ts = T + shift_cols
    for tt in range(NTT):
        for ck in range(NCH):
            b.dma("sync", xt[:, ck * TT:(ck + 1) * TT],
                  src_dram[ck * P:(ck + 1) * P, tt * TT:(tt + 1) * TT], c.dS)
        b.wait("tensor", c.dS, c.dS.n)
        b.wait("scalar", c.dS, c.dS.n)
        srcs = [xt[:, ck * TT:(ck + 1) * TT] for ck in range(NCH)]
        vt = ln_stats_bcast(c, srcs, D, pb[2], pb[3], stat, sq)
        b.wait("vector", c.TS, vt)
        for ck in range(NCH):
            b.call("vector", "tensor_sub", xt[:, ck * TT:(ck + 1) * TT],
                   xt[:, ck * TT:(ck + 1) * TT], pb[2][:, 0:TT])
            b.call("vector", "tensor_mul",
                   xln[:, ck * Ts + shift_cols + tt * TT:
                       ck * Ts + shift_cols + (tt + 1) * TT],
                   xt[:, ck * TT:(ck + 1) * TT], pb[3][:, 0:TT])
        b.barrier()


def emit_phase1(c):
    nc, b, dr = c.nc, c.b, c.dr
    dS, dG, TS, VS, SS, CC = c.dS, c.dG, c.TS, c.VS, c.SS, c.CC
    pb = c.pb

    with contextlib.ExitStack() as ph:
        sbf = lambda n, sh, dt: ph.enter_context(nc.sbuf_tensor(n, list(sh), dt))
        r_ = sbf("r_", (P, NH * T), BF16)
        k_ = sbf("k_", (P, NH * T), BF16)
        vtm = sbf("vtm", (P, NTC * D), BF16)
        tw = sbf("tw", (P, T), BF16)
        tg = sbf("tg", (P, T), BF16)

        with contextlib.ExitStack() as s1:
            sbl = lambda n, sh, dt: s1.enter_context(nc.sbuf_tensor(n, list(sh), dt))
            xln = sbl("xln", (P, NCH * (T + 1)), BF16)
            stat = sbl("stat", (1, 3 * TT), F32)
            sq = sbl("sq", (P, TT), F32)
            xt = sbl("xt", (P, NCH * TT), F32)
            mscr = sbl("mscr", (P, TT), F32)
            mixb = sbl("mixb", (P, NCH * TT), BF16)
            xx = sbl("xx", (33, 5 * TT), BF16)
            wx0 = sbl("wx0", (P, NCH * 160), BF16)
            wx2 = sbl("wx2", (33, 5 * D), BF16)
            rwS = sbl("rwS", (P, NCH * 512), BF16)
            kwS = sbl("kwS", (P, NCH * 512), BF16)
            vwS = sbl("vwS", (P, NCH * D), BF16)
            waS = sbl("waS", (P, NCH * 64), BF16)
            gaS = sbl("gaS", (P, NCH * 64), BF16)
            xprev = sbl("xprev", (P, NCH), F32)

            for ck in range(NCH):
                b.dma("gpsimd", wx0[:, ck * 160:(ck + 1) * 160],
                      dr["wx0T"][ck * P:(ck + 1) * P, :], dG)
                b.dma("gpsimd", rwS[:, ck * 512:(ck + 1) * 512],
                      dr["rWT"][ck * P:(ck + 1) * P, :], dG)
                b.dma("gpsimd", kwS[:, ck * 512:(ck + 1) * 512],
                      dr["kWT"][ck * P:(ck + 1) * P, :], dG)
                b.dma("gpsimd", vwS[:, ck * D:(ck + 1) * D],
                      dr["vWT"][ck * P:(ck + 1) * P, :], dG)
                b.dma("gpsimd", waS[:, ck * 64:(ck + 1) * 64],
                      dr["wAT"][ck * P:(ck + 1) * P, :], dG)
                b.dma("gpsimd", gaS[:, ck * 64:(ck + 1) * 64],
                      dr["gAT"][ck * P:(ck + 1) * P, :], dG)
            for g in range(5):
                b.dma("gpsimd", wx2[:, g * D:(g + 1) * D], dr["wx2T"][g, :, :], dG)
            wload_dG = dG.n

            b.dma("sync", xprev[:, :], dr["xprev_in"].ap(), dS)
            b.wait("tensor", dS, dS.n)
            b.wait("scalar", dS, dS.n)
            b.call("tensor", "matmul", pb[0][0:1, 0:NCH], c.ones_col[:, :],
                   xprev[:, :], start=True, stop=True, inc=TS)
            vs = b.call("scalar", "activation", sq[:, 0:NCH], xprev[:, :], AF.Square,
                        inc=SS)
            b.wait("tensor", SS, vs)
            vt = b.call("tensor", "matmul", pb[1][0:1, 0:NCH], c.ones_col[:, :],
                        sq[:, 0:NCH], start=True, stop=True, inc=TS)
            b.wait("vector", TS, vt)
            b.call("vector", "reduce_sum", stat[0:1, 0:1], pb[0][0:1, 0:NCH], AXX)
            b.call("vector", "reduce_sum", stat[0:1, 1:2], pb[1][0:1, 0:NCH], AXX)
            b.call("vector", "tensor_scalar_mul", stat[0:1, 0:2], stat[0:1, 0:2],
                   1.0 / D)
            b.call("vector", "tensor_mul", stat[0:1, 2:3], stat[0:1, 0:1],
                   stat[0:1, 0:1])
            vv = b.call("vector", "scalar_tensor_tensor", stat[0:1, 1:2],
                        stat[0:1, 1:2], EPS, stat[0:1, 2:3], ALU.add, ALU.subtract)
            b.wait("scalar", VS, vv)
            vs = b.call("scalar", "activation", stat[0:1, 2:3], stat[0:1, 1:2],
                        AF.Sqrt, inc=SS)
            b.wait("vector", SS, vs)
            vv = b.call("vector", "reciprocal", stat[0:1, 1:2], stat[0:1, 2:3])
            b.wait("tensor", VS, vv)
            b.call("tensor", "matmul", pb[2][:, 0:1], c.ones_row[:, :],
                   stat[0:1, 0:1], start=True, stop=True, inc=TS)
            vt = b.call("tensor", "matmul", pb[3][:, 0:1], c.ones_row[:, :],
                        stat[0:1, 1:2], start=True, stop=True, inc=TS)
            b.wait("vector", TS, vt)
            for ck in range(NCH):
                b.call("vector", "tensor_scalar",
                       xln[:, ck * (T + 1):ck * (T + 1) + 1], xprev[:, ck:ck + 1],
                       pb[2][:, 0:1], pb[3][:, 0:1], ALU.subtract, ALU.mult)
            b.barrier()

            emit_ln_to(c, dr["x_in"], xln, stat, sq, xt, 1)

            b.call("vector", "memset", xx[32:33, :], 1.0)
            b.barrier()
            b.wait("tensor", dG, wload_dG)
            b.wait("vector", dG, wload_dG)

            for tt in range(NTT):
                base = tt * TT
                for ck in range(NCH):
                    xo = ck * (T + 1) + 1 + base
                    b.call("vector", "tensor_sub", xt[:, ck * TT:(ck + 1) * TT],
                           xln[:, xo - 1:xo - 1 + TT], xln[:, xo:xo + TT])
                for ck in range(NCH):
                    xo = ck * (T + 1) + 1 + base
                    b.call("vector", "scalar_tensor_tensor",
                           mixb[:, ck * TT:(ck + 1) * TT],
                           xt[:, ck * TT:(ck + 1) * TT],
                           c.smalls[:, c.C_MU0 + ck:c.C_MU0 + ck + 1],
                           xln[:, xo:xo + TT], ALU.mult, ALU.add)
                b.wait("tensor", VS, VS.n)
                for g in range(5):
                    for ck in range(NCH):
                        b.call("tensor", "matmul", pb[g][0:32, 0:TT],
                               wx0[:, ck * 160 + g * 32:ck * 160 + (g + 1) * 32],
                               mixb[:, ck * TT:(ck + 1) * TT],
                               start=(ck == 0), stop=(ck == NCH - 1), inc=TS)
                b.wait("scalar", TS, TS.n)
                for g in range(5):
                    b.call("scalar", "activation", xx[0:32, g * TT:(g + 1) * TT],
                           pb[g][0:32, 0:TT], AF.Tanh, inc=SS)
                xx_S = SS.n

                for gname, gidx in (("r", 0), ("w", 1), ("k", 2), ("v", 3), ("g", 4)):
                    b.wait("tensor", SS, xx_S)
                    for mck in range(NCH):
                        xo = mck * (T + 1) + 1 + base
                        vt = b.call("tensor", "matmul", pb[5][:, 0:TT],
                                    wx2[0:33, gidx * D + mck * P:gidx * D + (mck + 1) * P],
                                    xx[0:33, gidx * TT:(gidx + 1) * TT],
                                    start=True, stop=True, inc=TS)
                        b.wait("vector", TS, vt)
                        b.call("vector", "tensor_mul", mscr[:, 0:TT], pb[5][:, 0:TT],
                               xt[:, mck * TT:(mck + 1) * TT])
                        vv = b.call("vector", "tensor_add",
                                    mixb[:, mck * TT:(mck + 1) * TT], mscr[:, 0:TT],
                                    xln[:, xo:xo + TT])
                        b.wait("tensor", VS, vv)
                    if gname in ("r", "k"):
                        wsb = rwS if gname == "r" else kwS
                        dst = r_ if gname == "r" else k_
                        for m in range(4):
                            for ck in range(NCH):
                                b.call("tensor", "matmul", pb[m][:, 0:TT],
                                       wsb[:, ck * 512 + m * P:ck * 512 + (m + 1) * P],
                                       mixb[:, ck * TT:(ck + 1) * TT],
                                       start=(ck == 0), stop=(ck == NCH - 1), inc=TS)
                        b.wait("vector", TS, TS.n)
                        for m in range(4):
                            b.call("vector", "tensor_copy",
                                   dst[:, m * T + base:m * T + base + TT],
                                   pb[m][:, 0:TT])
                        b.wait("tensor", VS, VS.n)
                    elif gname == "v":
                        for tc4 in range(4):
                            tc = tt * 4 + tc4
                            for fh in range(2):
                                bk = (tc4 * 2 + fh) % 2
                                for ck in range(NCH):
                                    b.call("tensor", "matmul", pb[bk][:, 0:TT],
                                           mixb[:, ck * TT + tc4 * P:
                                                ck * TT + (tc4 + 1) * P],
                                           vwS[:, ck * D + fh * 512:
                                               ck * D + (fh + 1) * 512],
                                           start=(ck == 0), stop=(ck == NCH - 1),
                                           inc=TS)
                                b.wait("vector", TS, TS.n)
                                vv = b.call("vector", "tensor_copy",
                                            vtm[:, tc * D + fh * 512:
                                                tc * D + (fh + 1) * 512],
                                            pb[bk][:, 0:TT])
                                b.wait("tensor", VS, vv)
                    else:
                        wsb = waS if gname == "w" else gaS
                        dst = tw if gname == "w" else tg
                        for ck in range(NCH):
                            b.call("tensor", "matmul", pb[6][0:64, 0:TT],
                                   wsb[:, ck * 64:(ck + 1) * 64],
                                   mixb[:, ck * TT:(ck + 1) * TT],
                                   start=(ck == 0), stop=(ck == NCH - 1), inc=TS)
                        b.wait("scalar", TS, TS.n)
                        vs = b.call("scalar", "activation",
                                    dst[0:64, base:base + TT], pb[6][0:64, 0:TT],
                                    AF.Tanh, inc=SS)
                        b.wait("tensor", SS, vs)
                b.barrier()

        with contextlib.ExitStack() as s2:
            sbl = lambda n, sh, dt: s2.enter_context(nc.sbuf_tensor(n, list(sh), dt))
            gbT = sbl("gbT", (64, D), BF16)
            oWb = sbl("oWb", (P, NCH * D), BF16)
            o_tm = sbl("o_tm", (P, NTC * D), BF16)

            with contextlib.ExitStack() as s3:
                sbm = lambda n, sh, dt: s3.enter_context(nc.sbuf_tensor(n, list(sh), dt))
                wbT = sbm("wbT", (64, 512), BF16)
                qe = sbm("qe", (P, NH * T), BF16)
                ke = sbm("ke", (P, T), BF16)
                kwtm = sbm("kwtm", (P, NTC * P), BF16)
                wtil = sbm("wtil", (P, T), F32)
                wcc = sbm("wcc", (P, T), F32)
                e1 = sbm("e1", (P, T), F32)
                amask = sbm("amask", (P, P), BF16)
                dexps = sbm("dexps", (P, 5 * NCK), F32)
                bnd = sbm("bnd", (P, NCK), F32)
                s_row = sbm("s_row", (1, T), F32)
                s_tm = sbm("s_tm", (P, NH * NTC), F32)
                S_all = sbm("S_all", (P, NH * DV), BF16)
                S_in = sbm("S_in", (P, NH * DV), BF16)

                b.dma("gpsimd", wbT[:, :], dr["wBT"].ap(), dG)
                b.dma("gpsimd", gbT[:, :], dr["gBT"].ap(), dG)
                for ck in range(NCH):
                    b.dma("gpsimd", oWb[:, ck * D:(ck + 1) * D],
                          dr["oWT"][ck * P:(ck + 1) * P, :], dG)
                b.wait("tensor", dG, dG.n)

                for h in range(NH):
                    vv = b.call("vector", "tensor_mul", e1[:, :],
                                r_[:, h * T:(h + 1) * T], k_[:, h * T:(h + 1) * T])
                    b.wait("tensor", VS, vv)
                    cp = {0: 0, 1: 0}
                    for q4 in range(NTT):
                        bk5 = 5 + q4 % 2
                        b.wait("tensor", VS, cp[q4 % 2])
                        vt = b.call("tensor", "matmul", pb[bk5][0:1, 0:TT],
                                    c.smalls[:, c.C_U + h:c.C_U + h + 1],
                                    e1[:, q4 * TT:(q4 + 1) * TT],
                                    start=True, stop=True, inc=TS)
                        b.wait("vector", TS, vt)
                        cp[q4 % 2] = b.call("vector", "tensor_copy",
                                            s_row[0:1, q4 * TT:(q4 + 1) * TT],
                                            pb[bk5][0:1, 0:TT])
                    for tc in range(NTC):
                        bk5 = 5 + tc % 2
                        b.wait("tensor", VS, cp[tc % 2])
                        vt = b.call("tensor", "transpose", pb[bk5][:, 0:1],
                                    s_row[0:1, tc * P:(tc + 1) * P],
                                    c.identsb[0:1, 0:1], inc=TS)
                        b.wait("vector", TS, vt)
                        cp[tc % 2] = b.call("vector", "tensor_copy",
                                            s_tm[:, h * NTC + tc:h * NTC + tc + 1],
                                            pb[bk5][:, 0:1])
                    b.wait("tensor", VS, max(cp.values()))
                b.call("vector", "memset", S_all[:, :], 0.0)
                b.barrier()

                for h in range(NH):
                    hT = h * T
                    ep = {0: 0, 1: 0}
                    for q4 in range(NTT):
                        bk5 = 5 + q4 % 2
                        b.wait("tensor", SS, ep[q4 % 2])
                        vt = b.call("tensor", "matmul", pb[bk5][:, 0:TT],
                                    wbT[:, h * P:(h + 1) * P],
                                    tw[0:64, q4 * TT:(q4 + 1) * TT],
                                    start=True, stop=True, inc=TS)
                        b.wait("scalar", TS, vt)
                        ep[q4 % 2] = b.call("scalar", "activation",
                                            wtil[:, q4 * TT:(q4 + 1) * TT],
                                            pb[bk5][:, 0:TT], AF.Exp,
                                            bias=c.smalls[:, c.C_WB + h:
                                                          c.C_WB + h + 1],
                                            inc=SS)
                    b.wait("vector", SS, SS.n)
                    b.call("vector", "tensor_tensor_scan", wcc[:, :],
                           wtil[:, :], wtil[:, :], 0.0, ALU.add, ALU.bypass)
                    b.call("vector", "memset", bnd[:, 0:1], 0.0)
                    b.call("vector", "tensor_copy", bnd[:, 1:NCK],
                           wcc[:, CH - 1:T - 1:CH])
                    b.call("vector", "tensor_tensor",
                           wcc.ap().rearrange("p (a b) -> p a b", a=NCK),
                           wcc.ap().rearrange("p (a b) -> p a b", a=NCK),
                           bnd[:, :, None].broadcast_to([P, NCK, CH]),
                           ALU.subtract)
                    b.wait("scalar", VS, VS.n)
                    b.call("scalar", "activation", dexps[:, 0:NCK],
                           wcc[:, CH - 1:T:CH], AF.Exp, scale=-1.0, inc=SS)
                    vs = b.call("scalar", "activation", dexps[:, NCK:2 * NCK],
                                wcc[:, CH - 1:T:CH], AF.Copy, scale=-1.0, inc=SS)
                    b.wait("vector", SS, vs)
                    b.call("vector", "tensor_tensor_scan", dexps[:, 2 * NCK:3 * NCK],
                           dexps[:, NCK:2 * NCK], dexps[:, NCK:2 * NCK], 0.0,
                           ALU.add, ALU.bypass)
                    vv = b.call("vector", "tensor_sub", dexps[:, 3 * NCK:4 * NCK],
                                dexps[:, 2 * NCK:3 * NCK], dexps[:, NCK:2 * NCK])
                    b.wait("scalar", VS, vv)
                    dexc_s = b.call("scalar", "activation", dexps[:, 4 * NCK:5 * NCK],
                                    dexps[:, 3 * NCK:4 * NCK], AF.Exp, inc=SS)

                    vv = b.call("vector", "tensor_sub", e1[:, :], wtil[:, :],
                                wcc[:, :])
                    b.wait("scalar", VS, vv)
                    vs = b.call("scalar", "activation", e1[:, :], e1[:, :], AF.Exp,
                                inc=SS)
                    b.wait("vector", SS, vs)
                    vv = b.call("vector", "tensor_mul", qe[:, hT:hT + T],
                                r_[:, hT:hT + T], e1[:, :])
                    b.wait("scalar", VS, vv)
                    vs = b.call("scalar", "activation", e1[:, :], wcc[:, :], AF.Exp,
                                inc=SS)
                    b.wait("vector", SS, vs)
                    vv = b.call("vector", "tensor_mul", ke[:, :], k_[:, hT:hT + T],
                                e1[:, :])
                    b.wait("scalar", VS, vv)
                    b.call("vector", "tensor_copy", bnd[:, 0:NCK],
                           wcc[:, CH - 1:T:CH])
                    b.call("vector", "tensor_tensor",
                           e1.ap().rearrange("p (a b) -> p a b", a=NCK),
                           wcc.ap().rearrange("p (a b) -> p a b", a=NCK),
                           bnd[:, :, None].broadcast_to([P, NCK, CH]),
                           ALU.subtract)
                    b.wait("scalar", VS, VS.n)
                    vs = b.call("scalar", "activation", e1[:, :], e1[:, :], AF.Exp,
                                inc=SS)
                    b.wait("vector", SS, vs)
                    vv = b.call("vector", "tensor_mul", e1[:, :], k_[:, hT:hT + T],
                                e1[:, :])
                    b.wait("tensor", VS, vv)
                    cp = {0: 0, 1: 0}
                    for tc in range(NTC):
                        bk5 = 5 + tc % 2
                        b.wait("tensor", VS, cp[tc % 2])
                        vt = b.call("tensor", "transpose", pb[bk5][:, 0:P],
                                    e1[:, tc * P:(tc + 1) * P], c.identsb[:, :],
                                    inc=TS)
                        b.wait("vector", TS, vt)
                        cp[tc % 2] = b.call("vector", "tensor_copy",
                                            kwtm[:, tc * P:(tc + 1) * P],
                                            pb[bk5][:, 0:P])
                    b.wait("tensor", VS, max(cp.values()))

                    Sh = S_all[:, h * DV:(h + 1) * DV]
                    b.wait("vector", SS, dexc_s)
                    for tc in range(NTC):
                        vt = b.call("tensor", "matmul", pb[0][:, 0:P],
                                    ke[:, tc * P:(tc + 1) * P],
                                    qe[:, hT + tc * P:hT + (tc + 1) * P],
                                    start=True, stop=True, inc=TS)
                        b.wait("vector", TS, vt)
                        am_v = b.call("vector", "tensor_mul", amask[:, :],
                                      pb[0][:, 0:P], c.rmask_sb[:, :])
                        for j4 in range(4):
                            b.call("tensor", "matmul", pb[2 + j4][:, 0:DV],
                                   kwtm[j4 * CH:(j4 + 1) * CH, tc * P:(tc + 1) * P],
                                   vtm[j4 * CH:(j4 + 1) * CH,
                                       tc * D + h * DV:tc * D + (h + 1) * DV],
                                   start=True, stop=True,
                                   tile_position=(j4 * CH, 0), inc=TS)
                        b.wait("tensor", VS, am_v)
                        b.call("tensor", "matmul", pb[1][:, 0:DV], amask[:, :],
                               vtm[:, tc * D + h * DV:tc * D + (h + 1) * DV],
                               start=True, stop=False, inc=TS)
                        for j4 in range(4):
                            vt = b.call("tensor", "matmul",
                                        pb[1][j4 * CH:(j4 + 1) * CH, 0:DV],
                                        qe[:, hT + tc * P + j4 * CH:
                                           hT + tc * P + (j4 + 1) * CH],
                                        Sh, start=False, stop=False,
                                        tile_position=(0, j4 * CH), inc=TS)
                            b.wait("vector", TS, vt)
                            vv = b.call("vector", "scalar_tensor_tensor", Sh, Sh,
                                        dexps[:, tc * 4 + j4:tc * 4 + j4 + 1],
                                        pb[2 + j4][:, 0:DV], ALU.mult, ALU.add)
                            b.wait("tensor", VS, vv)
                        b.call("tensor", "matmul", pb[1][:, 0:DV],
                               c.zrow[0:1, 0:P], c.zrow[0:1, 0:DV],
                               start=False, stop=True, inc=TS)
                        b.wait("vector", TS, TS.n)
                        vv = b.call("vector", "scalar_tensor_tensor",
                                    o_tm[:, tc * D + h * DV:tc * D + (h + 1) * DV],
                                    vtm[:, tc * D + h * DV:tc * D + (h + 1) * DV],
                                    s_tm[:, h * NTC + tc:h * NTC + tc + 1],
                                    pb[1][:, 0:DV], ALU.mult, ALU.add)
                        b.wait("tensor", VS, vv)
                    for tc in range(NTC):
                        qs = qe[:, hT + tc * P:hT + (tc + 1) * P]
                        b.call("vector", "tensor_tensor",
                               qs.rearrange("p (a b) -> p a b", a=4),
                               qs.rearrange("p (a b) -> p a b", a=4),
                               dexps[:, 4 * NCK + tc * 4:4 * NCK + (tc + 1) * 4]
                               [:, :, None].broadcast_to([P, 4, CH]),
                               ALU.mult)
                    b.barrier()

                b.dma("gpsimd", c.cc_s_in.ap(), S_all[:, :], dG)
                b.wait("gpsimd", dG, dG.n)
                vg = b.op("gpsimd", lambda e: e.collective_compute(
                    "AllGather", ALU.bypass,
                    replica_groups=[[0, 1], [2, 3], [4, 5], [6, 7]],
                    ins=[c.cc_s_in.ap()], outs=[c.cc_s_out.ap()]), inc=CC)
                b.wait("gpsimd", CC, vg)
                b.dma("gpsimd", S_in[:, :], c.cc_s_out[0:DK, :], dG)
                b.wait("vector", dG, dG.n)
                vv = b.call("vector", "tensor_scalar_mul", S_in[:, :], S_in[:, :],
                            c.smalls[:, c.C_FLAG:c.C_FLAG + 1])
                b.wait("tensor", VS, vv)
                for h in range(NH):
                    for tc in range(NTC):
                        vt = b.call("tensor", "matmul", pb[0][:, 0:DV],
                                    qe[:, h * T + tc * P:h * T + (tc + 1) * P],
                                    S_in[:, h * DV:(h + 1) * DV],
                                    start=True, stop=True, inc=TS)
                        b.wait("vector", TS, vt)
                        vv = b.call("vector", "tensor_add",
                                    o_tm[:, tc * D + h * DV:tc * D + (h + 1) * DV],
                                    o_tm[:, tc * D + h * DV:tc * D + (h + 1) * DV],
                                    pb[0][:, 0:DV])
                        b.wait("tensor", VS, vv)
                b.barrier()

            with contextlib.ExitStack() as s4:
                sbl = lambda n, sh, dt: s4.enter_context(nc.sbuf_tensor(n, list(sh), dt))
                gbb = sbl("gbb", (P, 3 * D), F32)
                ogfm = sbl("ogfm", (P, NCH * T), BF16)
                lnv = sbl("lnv", (1, 3 * D), F32)
                colstat = sbl("colstat", (P, 8), F32)
                yst = sbl("yst", (P, TT), F32)
                xrel = sbl("xrel", (P, TT), F32)
                sc = sbl("sc", (P, 2 * D), F32)

                b.dma("sync", lnv[0:1, 0:D], dr["lngr"].ap(), dS)
                b.dma("sync", lnv[0:1, D:2 * D], dr["lnbr"].ap(), dS)
                b.dma("sync", lnv[0:1, 2 * D:3 * D], dr["gbr"].ap(), dS)
                b.wait("tensor", dS, dS.n)
                for i in range(3):
                    for fh in range(2):
                        vt = b.call("tensor", "matmul", pb[6][:, 0:TT],
                                    c.ones_row[:, :],
                                    lnv[0:1, i * D + fh * 512:i * D + (fh + 1) * 512],
                                    start=True, stop=True, inc=TS)
                        b.wait("vector", TS, vt)
                        vv = b.call("vector", "tensor_copy",
                                    gbb[:, i * D + fh * 512:i * D + (fh + 1) * 512],
                                    pb[6][:, 0:TT])
                        b.wait("tensor", VS, vv)
                b.barrier()

                for tc in range(NTC):
                    b.call("scalar", "activation", sc[:, 0:D],
                           o_tm[:, tc * D:(tc + 1) * D], AF.Copy,
                           accum_out=colstat[:, 0:1], inc=SS)
                    vs = b.call("scalar", "activation", sc[:, 0:D],
                                o_tm[:, tc * D:(tc + 1) * D], AF.Square,
                                accum_out=colstat[:, 1:2], inc=SS)
                    b.wait("vector", SS, vs)
                    b.call("vector", "tensor_scalar_mul", colstat[:, 0:2],
                           colstat[:, 0:2], 1.0 / D)
                    b.call("vector", "tensor_mul", colstat[:, 2:3], colstat[:, 0:1],
                           colstat[:, 0:1])
                    vv = b.call("vector", "scalar_tensor_tensor", colstat[:, 1:2],
                                colstat[:, 1:2], EPS, colstat[:, 2:3],
                                ALU.add, ALU.subtract)
                    b.wait("scalar", VS, vv)
                    vs = b.call("scalar", "activation", colstat[:, 2:3],
                                colstat[:, 1:2], AF.Sqrt, inc=SS)
                    b.wait("vector", SS, vs)
                    b.call("vector", "reciprocal", colstat[:, 1:2], colstat[:, 2:3])
                    for fh in range(2):
                        f0 = fh * 512
                        vt = b.call("tensor", "matmul", pb[0][:, 0:TT],
                                    tg[0:64, tc * P:(tc + 1) * P],
                                    gbT[:, f0:f0 + 512],
                                    start=True, stop=True, inc=TS)
                        b.wait("vector", TS, vt)
                        vv = b.call("vector", "tensor_add", sc[:, 0:TT],
                                    pb[0][:, 0:TT], gbb[:, 2 * D + f0:2 * D + f0 + 512])
                        b.wait("scalar", VS, vv)
                        b.wait("tensor", VS, vv)
                        gs = b.call("scalar", "activation", sc[:, 0:TT], sc[:, 0:TT],
                                    AF.Silu, inc=SS)
                        b.call("vector", "tensor_scalar", sc[:, D:D + TT],
                               o_tm[:, tc * D + f0:tc * D + f0 + 512],
                               colstat[:, 0:1], colstat[:, 1:2],
                               ALU.subtract, ALU.mult)
                        b.call("vector", "tensor_mul", sc[:, D:D + TT],
                               sc[:, D:D + TT], gbb[:, f0:f0 + 512])
                        b.call("vector", "tensor_add", sc[:, D:D + TT],
                               sc[:, D:D + TT], gbb[:, D + f0:D + f0 + 512])
                        b.wait("vector", SS, gs)
                        vv = b.call("vector", "tensor_mul",
                                    o_tm[:, tc * D + f0:tc * D + f0 + 512],
                                    sc[:, D:D + TT], sc[:, 0:TT])
                        b.wait("scalar", VS, vv)
                b.barrier()

                for tc in range(NTC):
                    for ck in range(NCH):
                        vt = b.call("tensor", "transpose", c.pbt[:, 0:P],
                                    o_tm[:, tc * D + ck * P:tc * D + (ck + 1) * P],
                                    c.identbf[:, :], inc=TS)
                        b.wait("vector", TS, vt)
                        vv = b.call("vector", "tensor_copy",
                                    ogfm[:, ck * T + tc * P:ck * T + (tc + 1) * P],
                                    c.pbt[:, 0:P])
                        b.wait("tensor", VS, vv)
                b.barrier()

                prev_read = {0: 0, 1: 0}
                for oc in range(NCH):
                    for q4 in range(NTT):
                        bk = (oc * NTT + q4) % 2
                        b.wait("tensor", VS, prev_read[bk])
                        for ck in range(NCH):
                            b.call("tensor", "matmul", pb[bk][:, 0:TT],
                                   oWb[:, ck * D + oc * P:ck * D + (oc + 1) * P],
                                   ogfm[:, ck * T + q4 * TT:ck * T + (q4 + 1) * TT],
                                   start=(ck == 0), stop=(ck == NCH - 1), inc=TS)
                        mm_T = TS.n
                        b.dma("sync", xrel[:, :],
                              dr["x_in"][oc * P:(oc + 1) * P, q4 * TT:(q4 + 1) * TT],
                              dS)
                        b.wait("vector", dS, dS.n)
                        b.wait("vector", TS, mm_T)
                        vv = b.call("vector", "tensor_add", yst[:, :],
                                    pb[bk][:, 0:TT], xrel[:, :])
                        prev_read[bk] = vv
                        b.wait("sync", VS, vv)
                        b.dma("sync",
                              c.x_hbm[0][oc * P:(oc + 1) * P, q4 * TT:(q4 + 1) * TT],
                              yst[:, :], c.dT)
                        b.wait("tensor", c.dT, c.dT.n)
                        b.wait("vector", c.dT, c.dT.n)
                b.barrier()

    b.barrier()
    c.x_cur = 0


def emit_token_max_bcast(c, macc, colbuf, rowbuf, mfull):
    b = c.b
    pb = c.pb
    TS, VS = c.TS, c.VS
    b.wait("tensor", VS, VS.n)
    for tc in range(NTC):
        vt = b.call("tensor", "transpose", pb[4][:, 0:P],
                    macc[:, tc * P:(tc + 1) * P], c.identsb[:, :], inc=TS)
        b.wait("vector", TS, vt)
        vv = b.call("vector", "reduce_max", colbuf[:, :], pb[4][:, 0:P], AXX)
        b.wait("tensor", VS, vv)
        vt = b.call("tensor", "transpose", pb[5][0:1, 0:P], colbuf[:, :],
                    c.identsb[:, :], inc=TS)
        b.wait("vector", TS, vt)
        vv = b.call("vector", "tensor_copy", rowbuf[0:1, tc * P:(tc + 1) * P],
                    pb[5][0:1, 0:P])
        b.wait("tensor", VS, vv)
    for q4 in range(NTT):
        vt = b.call("tensor", "matmul", pb[4][:, 0:TT], c.ones_row[:, :],
                    rowbuf[0:1, q4 * TT:(q4 + 1) * TT], start=True, stop=True,
                    inc=TS)
        b.wait("vector", TS, vt)
        vv = b.call("vector", "tensor_copy", mfull[:, q4 * TT:(q4 + 1) * TT],
                    pb[4][:, 0:TT])
        b.wait("tensor", VS, vv)
    b.wait("vector", c.TS, c.TS.n)



def pc_init(c):
    c.pc_jobs = []
    for srcn, dsn, noc, na in (("wq1t", "wqbt", 8, 8), ("wk1t", "wkbt", 8, 8),
                               ("wv1t", "wvbt", 8, 8), ("wo1t", "wobt", 8, 8),
                               ("w1Tt", "w1bt", 32, 8), ("w2Tt", "w2bt", 32, 8),
                               ("h1Tt", "h1bt", 32, 32), ("h2Tt", "h2bt", 32, 32),
                               ("w3Tt", "w3bt", 8, 32)):
        dstt = getattr(c, dsn)
        for oc in range(noc):
            for a0 in range(0, na, 8):
                c.pc_jobs.append((srcn, dstt, oc, a0, a0 + 8))
    c.pc_state = {"li": 0, "si": 0, "loadv": {}, "storev": {}}
    c.pc_done = {}


def pc_step(c):
    b, dr = c.b, c.dr
    st = c.pc_state
    i = st["li"]
    if i < len(c.pc_jobs):
        bi = i % 4
        srcn, dstt, oc, a0, a1 = c.pc_jobs[i]
        if i >= 4:
            b.wait("gpsimd", c.dPS[bi], st["storev"][i - 4])
        b.dma("gpsimd", c.pcst[bi].ap().rearrange("q (a m) -> q a m", a=8),
              dr[srcn][oc, a0:a1, :, :].rearrange("a p m -> p a m"), c.dPL[bi])
        st["loadv"][i] = c.dPL[bi].n
        st["li"] = i + 1
    j = st["si"]
    if j < len(c.pc_jobs) and j <= st["li"] - 2 or (st["li"] == len(c.pc_jobs)
                                                    and j < st["li"]):
        bj = j % 4
        srcn, dstt, oc, a0, a1 = c.pc_jobs[j]
        b.wait("gpsimd", c.dPL[bj], st["loadv"][j])
        b.dma("gpsimd", dstt[oc, a0:a1, :, :].rearrange("a p m -> p a m"),
              c.pcst[bj].ap().rearrange("q (a m) -> q a m", a=8), c.dPS[bj])
        st["storev"][j] = c.dPS[bj].n
        st["si"] = j + 1
        if j + 1 == len(c.pc_jobs) or c.pc_jobs[j + 1][0] != srcn:
            c.pc_done[srcn] = [c.dPS[k].n for k in range(4)]


def pc_drain(c):
    while c.pc_state["si"] < len(c.pc_jobs):
        pc_step(c)


def emit_phase2(c):
    nc, b, dr = c.nc, c.b, c.dr
    dS, dG, dT, TS, VS, SS, GS = c.dS, c.dG, c.dT, c.TS, c.VS, c.SS, c.GS
    pb = c.pb
    x_in = c.x_hbm[0]
    x_out = c.x_hbm[1]

    with contextlib.ExitStack() as ph:
        sbf = lambda n, sh, dt: ph.enter_context(nc.sbuf_tensor(n, list(sh), dt))
        xq = sbf("p2xq", (P, NCH * T), BF16)
        sm2 = sbf("p2sm", (1, 16), F32)

        with contextlib.ExitStack() as sA:
            sbl = lambda n, sh, dt: sA.enter_context(nc.sbuf_tensor(n, list(sh), dt))
            wfA = [sbl(f"p2wfA{i}", (P, 4096), F32) for i in range(2)]
            acol = sbl("p2acol", (P, 4), F32)
            rcol = sbl("p2rcol", (P, 1), F32)
            b.call("vector", "memset", acol[:, :], 0.0)
            b.barrier()
            specs = [("wgT", 8, 4096, 0), ("wuT", 8, 4096, 1), ("wdT", 32, 1024, 2)]
            prev_red = {0: 0, 1: 0}
            dsems = {0: dS, 1: c.dS2}
            for name, nsl, wid, idx in specs:
                for s in range(nsl):
                    bi = s % 2
                    b.wait("sync", VS, prev_red[bi])
                    b.dma("sync", wfA[bi][:, 0:wid],
                          dr[name][s * P:(s + 1) * P, :], dsems[bi])
                    b.wait("vector", dsems[bi], dsems[bi].n)
                    b.call("vector", "reduce_sum", rcol[:, :], wfA[bi][:, 0:wid],
                           AXX, apply_absolute_value=True)
                    vv = b.call("vector", "tensor_add", acol[:, idx:idx + 1],
                                acol[:, idx:idx + 1], rcol[:, :])
                    prev_red[bi] = vv
            b.wait("tensor", VS, VS.n)
            for idx, (scol, rcol_i, nelem) in enumerate(
                    ((c.C_WGS, c.C_WGR, 4096 * D), (c.C_WUS, c.C_WUR, 4096 * D),
                     (c.C_WDS, c.C_WDR, 4096 * D))):
                vt = b.call("tensor", "matmul", pb[0][0:1, 0:1],
                            acol[:, idx:idx + 1], c.ones_col[:, :],
                            start=True, stop=True, inc=TS)
                b.wait("vector", TS, vt)
                b.call("vector", "tensor_scalar", sm2[0:1, 0:1], pb[0][0:1, 0:1],
                       1.0 / nelem, 1e-5, ALU.mult, ALU.max)
                vv = b.call("vector", "reciprocal", sm2[0:1, 1:2], sm2[0:1, 0:1])
                b.wait("tensor", VS, vv)
                b.call("tensor", "matmul", pb[1][:, 0:1], c.ones_row[:, :],
                       sm2[0:1, 0:1], start=True, stop=True, inc=TS)
                vt = b.call("tensor", "matmul", pb[2][:, 0:1], c.ones_row[:, :],
                            sm2[0:1, 1:2], start=True, stop=True, inc=TS)
                b.wait("vector", TS, vt)
                b.call("vector", "tensor_copy", c.smalls[:, rcol_i:rcol_i + 1],
                       pb[1][:, 0:1])
                vv = b.call("vector", "tensor_copy", c.smalls[:, scol:scol + 1],
                            pb[2][:, 0:1])
                b.wait("tensor", VS, vv)
            b.barrier()

        with contextlib.ExitStack() as sL:
            sbl = lambda n, sh, dt: sL.enter_context(nc.sbuf_tensor(n, list(sh), dt))
            xln = sbl("p2xln", (P, NCH * T), F32)
            stat = sbl("p2stat", (1, 3 * TT), F32)
            sq = sbl("p2sq", (P, TT), F32)
            xt = sbl("p2xt", (P, NCH * TT), F32)
            mfull = sbl("p2mfull", (P, T), F32)
            tfull = sbl("p2tfull", (P, T), F32)
            rfull = sbl("p2rfull", (P, T), F32)
            q1col = sbl("p2q1col", (P, 1), F32)
            q1row = sbl("p2q1row", (1, T), F32)
            emit_ln_to(c, x_in, xln, stat, sq, xt, 0)
            for ck in range(NCH):
                vs = b.call("scalar", "activation", tfull[:, :],
                            xln[:, ck * T:(ck + 1) * T], AF.Abs, inc=SS)
                b.wait("vector", SS, vs)
                if ck == 0:
                    vv = b.call("vector", "tensor_copy", mfull[:, :], tfull[:, :])
                else:
                    vv = b.call("vector", "tensor_max", mfull[:, :], mfull[:, :],
                                tfull[:, :])
                b.wait("scalar", VS, vv)
            emit_token_max_bcast(c, mfull, q1col, q1row, tfull)
            b.call("vector", "tensor_scalar_max", mfull[:, :], tfull[:, :], 1e-5)
            b.call("vector", "reciprocal", rfull[:, :], mfull[:, :])
            b.call("vector", "tensor_scalar_mul", mfull[:, :], mfull[:, :],
                   1.0 / 127.0)
            for ck in range(NCH):
                b.call("vector", "tensor_mul", tfull[:, :],
                       xln[:, ck * T:(ck + 1) * T], rfull[:, :])
                b.call("vector", "tensor_scalar", tfull[:, :], tfull[:, :],
                       127.0, RND_C, ALU.mult, ALU.add)
                b.call("vector", "tensor_scalar", tfull[:, :], tfull[:, :],
                       RND_C, None, ALU.subtract, ALU.bypass)
                b.call("vector", "tensor_mul", xq[:, ck * T:(ck + 1) * T],
                       tfull[:, :], mfull[:, :])
            b.barrier()

        gu = ph.enter_context(nc.sbuf_tensor("p2gu", [P, 32 * T], BF16))
        c.pcst = [ph.enter_context(nc.sbuf_tensor(f"pcst{i}", [P, 1024], BF16))
                  for i in range(4)]

        with contextlib.ExitStack() as sG:
            sbl = lambda n, sh, dt: sG.enter_context(nc.sbuf_tensor(n, list(sh), dt))
            wf = sbl("p2wf", (P, NCH * 512), F32)
            wgq = sbl("p2wgq", (P, NCH * 512), BF16)
            wuq = sbl("p2wuq", (P, NCH * 512), BF16)
            gsb = sbl("p2gsb", (P, TT), BF16)
            usb = sbl("p2usb", (P, TT), BF16)
            for og in range(8):
                co = og * 512
                for wname, wq_sb, scol in (("wgT", wgq, c.C_WGS),
                                           ("wuT", wuq, c.C_WUS)):
                    for ck in range(NCH):
                        b.dma("sync", wf[:, ck * 512:(ck + 1) * 512],
                              dr[wname][ck * P:(ck + 1) * P, co:co + 512], dS)
                    b.wait("vector", dS, dS.n)
                    b.call("vector", "tensor_scalar", wf[:, :], wf[:, :],
                           c.smalls[:, scol:scol + 1], RND_C, ALU.mult, ALU.add)
                    b.call("vector", "tensor_scalar", wf[:, :], wf[:, :],
                           RND_C, 1.0, ALU.subtract, ALU.min)
                    vv = b.call("vector", "tensor_scalar", wq_sb[:, :], wf[:, :],
                                -1.0, None, ALU.max, ALU.bypass)
                    b.wait("sync", VS, vv)
                    b.wait("tensor", VS, vv)
                for oc4 in range(4):
                    ic = og * 4 + oc4
                    for tt in range(NTT):
                        for ck in range(NCH):
                            b.call("tensor", "matmul", pb[0][:, 0:TT],
                                   wgq[:, ck * 512 + oc4 * P:ck * 512 + (oc4 + 1) * P],
                                   xq[:, ck * T + tt * TT:ck * T + (tt + 1) * TT],
                                   start=(ck == 0), stop=(ck == NCH - 1), inc=TS)
                        for ck in range(NCH):
                            b.call("tensor", "matmul", pb[1][:, 0:TT],
                                   wuq[:, ck * 512 + oc4 * P:ck * 512 + (oc4 + 1) * P],
                                   xq[:, ck * T + tt * TT:ck * T + (tt + 1) * TT],
                                   start=(ck == 0), stop=(ck == NCH - 1), inc=TS)
                        mmT = TS.n
                        b.wait("scalar", TS, mmT)
                        b.call("scalar", "activation", gsb[:, :], pb[0][:, 0:TT],
                               AF.Silu, scale=c.smalls[:, c.C_WGR:c.C_WGR + 1],
                               inc=SS)
                        vs = b.call("scalar", "activation", usb[:, :],
                                    pb[1][:, 0:TT], AF.Copy,
                                    scale=c.smalls[:, c.C_WUR:c.C_WUR + 1], inc=SS)
                        b.wait("vector", SS, vs)
                        vv = b.call("vector", "tensor_mul",
                                    gu[:, ic * T + tt * TT:ic * T + (tt + 1) * TT],
                                    gsb[:, :], usb[:, :])
                        b.wait("tensor", VS, vv)
                        b.wait("scalar", VS, vv)
                for _ in range(36):
                    pc_step(c)
            b.barrier()

        with contextlib.ExitStack() as sQ:
            sbl = lambda n, sh, dt: sQ.enter_context(nc.sbuf_tensor(n, list(sh), dt))
            mfull = sbl("p2mful2", (P, T), F32)
            tfull = sbl("p2tful2", (P, T), F32)
            rfull = sbl("p2rful2", (P, T), F32)
            q1col = sbl("p2q2col", (P, 1), F32)
            q1row = sbl("p2q2row", (1, T), F32)
            for ic in range(32):
                vs = b.call("scalar", "activation", tfull[:, :],
                            gu[:, ic * T:(ic + 1) * T], AF.Abs, inc=SS)
                b.wait("vector", SS, vs)
                if ic == 0:
                    vv = b.call("vector", "tensor_copy", mfull[:, :], tfull[:, :])
                else:
                    vv = b.call("vector", "tensor_max", mfull[:, :], mfull[:, :],
                                tfull[:, :])
                b.wait("scalar", VS, vv)
            emit_token_max_bcast(c, mfull, q1col, q1row, tfull)
            b.call("vector", "tensor_scalar_max", mfull[:, :], tfull[:, :], 1e-5)
            b.call("vector", "reciprocal", rfull[:, :], mfull[:, :])
            b.call("vector", "tensor_scalar_mul", mfull[:, :], mfull[:, :],
                   1.0 / 127.0)
            for ic in range(32):
                b.call("vector", "tensor_mul", tfull[:, :],
                       gu[:, ic * T:(ic + 1) * T], rfull[:, :])
                b.call("vector", "tensor_scalar", tfull[:, :], tfull[:, :],
                       127.0, RND_C, ALU.mult, ALU.add)
                b.call("vector", "tensor_scalar", tfull[:, :], tfull[:, :],
                       RND_C, None, ALU.subtract, ALU.bypass)
                b.call("vector", "tensor_mul", gu[:, ic * T:(ic + 1) * T],
                       tfull[:, :], mfull[:, :])
            b.barrier()

        with contextlib.ExitStack() as sD:
            sbl = lambda n, sh, dt: sD.enter_context(nc.sbuf_tensor(n, list(sh), dt))
            wdf = sbl("p2wdf", (P, 32 * P), F32)
            wdq = sbl("p2wdq", (P, 32 * P), BF16)
            xrel = sbl("p2xrel", (P, TT), F32)
            yst = sbl("p2yst", (P, TT), F32)
            for oc2 in range(NCH):
                for ck2 in range(32):
                    b.dma("sync", wdf[:, ck2 * P:(ck2 + 1) * P],
                          dr["wdT"][ck2 * P:(ck2 + 1) * P,
                                    oc2 * P:(oc2 + 1) * P], dS)
                b.wait("vector", dS, dS.n)
                for _ in range(12):
                    pc_step(c)
                b.call("vector", "tensor_scalar", wdf[:, :], wdf[:, :],
                       c.smalls[:, c.C_WDS:c.C_WDS + 1], RND_C, ALU.mult, ALU.add)
                b.call("vector", "tensor_scalar", wdf[:, :], wdf[:, :],
                       RND_C, 1.0, ALU.subtract, ALU.min)
                vv = b.call("vector", "tensor_scalar", wdq[:, :], wdf[:, :],
                            -1.0, None, ALU.max, ALU.bypass)
                b.wait("tensor", VS, vv)
                b.wait("sync", VS, vv)
                for tt in range(NTT):
                    bk = tt % 2
                    for ck2 in range(32):
                        b.call("tensor", "matmul", pb[bk][:, 0:TT],
                               wdq[:, ck2 * P:(ck2 + 1) * P],
                               gu[:, ck2 * T + tt * TT:ck2 * T + (tt + 1) * TT],
                               start=(ck2 == 0), stop=(ck2 == 31), inc=TS)
                    mmT = TS.n
                    b.dma("sync", xrel[:, :],
                          x_in[oc2 * P:(oc2 + 1) * P, tt * TT:(tt + 1) * TT], dS)
                    b.wait("vector", dS, dS.n)
                    b.wait("vector", TS, mmT)
                    vv = b.call("vector", "scalar_tensor_tensor", yst[:, :],
                                pb[bk][:, 0:TT],
                                c.smalls[:, c.C_WDR:c.C_WDR + 1], xrel[:, :],
                                ALU.mult, ALU.add)
                    b.wait("sync", VS, vv)
                    b.dma("sync",
                          x_out[oc2 * P:(oc2 + 1) * P, tt * TT:(tt + 1) * TT],
                          yst[:, :], c.dT)
                    b.wait("tensor", c.dT, c.dT.n)
                    b.wait("vector", c.dT, c.dT.n)
            b.barrier()

        pc_drain(c)
        for k in range(4):
            b.wait("vector", c.dPS[k], c.pc_done["w3Tt"][k])

    b.barrier()
    c.x_cur = 1


def emit_phase3(c):
    nc, b, dr = c.nc, c.b, c.dr
    dS, dG, dT, TS, VS, SS, CC = c.dS, c.dG, c.dT, c.TS, c.VS, c.SS, c.CC
    pb = c.pb
    x_in = c.x_hbm[1]
    x_out = c.x_hbm[0]

    with contextlib.ExitStack() as ph:
        sbf = lambda n, sh, dt: ph.enter_context(nc.sbuf_tensor(n, list(sh), dt))
        h_loc = sbf("p3hloc", (P, NCH * T), BF16)
        ctx = sbf("p3ctx", (P, NCH * T), BF16)

        with contextlib.ExitStack() as sL:
            sbl = lambda n, sh, dt: sL.enter_context(nc.sbuf_tensor(n, list(sh), dt))
            stat = sbl("p3stat", (1, 3 * TT), F32)
            sq = sbl("p3sq", (P, TT), F32)
            xt = sbl("p3xt", (P, NCH * TT), F32)
            pet = sbl("p3pet", (P, TT), F32)
            hst = sbl("p3hst", (P, TT), BF16)
            for tt in range(NTT):
                for ck in range(NCH):
                    b.dma("sync", xt[:, ck * TT:(ck + 1) * TT],
                          x_in[ck * P:(ck + 1) * P, tt * TT:(tt + 1) * TT], dS)
                b.wait("tensor", dS, dS.n)
                b.wait("scalar", dS, dS.n)
                srcs = [xt[:, ck * TT:(ck + 1) * TT] for ck in range(NCH)]
                vt = ln_stats_bcast(c, srcs, D, pb[2], pb[3], stat, sq)
                b.wait("vector", TS, vt)
                for ck in range(NCH):
                    b.dma("sync", pet[:, :],
                          dr["pe_in"][ck * P:(ck + 1) * P, tt * TT:(tt + 1) * TT],
                          c.dS2)
                    b.wait("vector", c.dS2, c.dS2.n)
                    b.call("vector", "tensor_sub", xt[:, ck * TT:(ck + 1) * TT],
                           xt[:, ck * TT:(ck + 1) * TT], pb[2][:, 0:TT])
                    b.call("vector", "tensor_mul", xt[:, ck * TT:(ck + 1) * TT],
                           xt[:, ck * TT:(ck + 1) * TT], pb[3][:, 0:TT])
                    vv = b.call("vector", "tensor_add", hst[:, :],
                                xt[:, ck * TT:(ck + 1) * TT], pet[:, :])
                    b.call("vector", "tensor_copy",
                           h_loc[:, ck * T + tt * TT:ck * T + (tt + 1) * TT],
                           hst[:, :])
                    b.wait("sync", VS, VS.n)
                    b.dma("sync",
                          c.cc_h_in[ck * P:(ck + 1) * P, tt * TT:(tt + 1) * TT],
                          hst[:, :], c.dT)
                    b.wait("vector", c.dT, c.dT.n)
                b.barrier()

        vg = b.op("gpsimd", lambda e: e.collective_compute(
            "AllGather", ALU.bypass,
            replica_groups=[[0, 1], [2, 3], [4, 5], [6, 7]],
            ins=[c.cc_h_in.ap()], outs=[c.cc_h_out.ap()]), inc=CC)
        b.wait("gpsimd", CC, vg)

        with contextlib.ExitStack() as s2:
            sbl = lambda n, sh, dt: s2.enter_context(nc.sbuf_tensor(n, list(sh), dt))
            h_full = sbl("p3hfull", (P, NCH * 2 * T), BF16)
            for ck in range(NCH):
                for hf in range(2):
                    b.dma("gpsimd",
                          h_full[:, ck * 2 * T + hf * T:ck * 2 * T + (hf + 1) * T],
                          c.cc_h_out[hf * D + ck * P:hf * D + (ck + 1) * P, :], dG)
            hload_dG = dG.n

            with contextlib.ExitStack() as s3:
                sbm = lambda n, sh, dt: s3.enter_context(nc.sbuf_tensor(n, list(sh), dt))
                wq2 = sbm("p3wq", (P, 2 * NCH * P), BF16)
                wk = sbm("p3wk", (P, NCH * P), BF16)
                wv = sbm("p3wv", (P, NCH * P), BF16)
                biases = sbm("p3bias", (P, 4 * NCH), F32)
                qp_all = sbm("p3qp", (P, NCH * T), BF16)
                kp = sbm("p3kp", (P, 2 * T), BF16)
                vp = sbm("p3vp", (P, 2 * T), BF16)
                vtm = sbm("p3vtm", (P, 32 * 130), BF16)
                pt = sbm("p3pt", (P, 4 * TT), BF16)
                rdrow = sbm("p3rd", (P, TT), F32)
                ctmp0 = sbm("p3ctmp0", (64, TT), BF16)
                ctmp1 = sbm("p3ctmp1", (64, TT), BF16)
                ctmp2 = sbm("p3ctmp2", (64, TT), BF16)

                b.dma("sync", biases[:, 0:NCH], dr["bq1"].ap(), dS)
                b.dma("sync", biases[:, NCH:2 * NCH], dr["bk1"].ap(), dS)
                b.dma("sync", biases[:, 2 * NCH:3 * NCH], dr["bv1"].ap(), dS)
                b.dma("sync", biases[:, 3 * NCH:4 * NCH], dr["bo1"].ap(), dS)
                b.wait("tensor", dS, dS.n)
                b.wait("scalar", dS, dS.n)
                b.wait("vector", dS, dS.n)

                b.call("vector", "memset",
                       vtm.ap().rearrange("p (a c) -> p a c", c=65)[:, :, 64:65], 1.0)

                for k in range(4):
                    b.wait("sync", c.dPS[k], c.pc_done["wq1t"][k])
                q_guard = {0: 0, 1: 0}
                dwv = {0: 0, 1: 0}
                prev_q = {0: 0, 1: 0}
                for p in range(NCH):
                    b2 = p % 2
                    b.wait("sync", TS, q_guard[b2])
                    b.dma("sync",
                          wq2[:, b2 * NCH * P:(b2 + 1) * NCH * P]
                          .rearrange("q (a m) -> q a m", a=NCH),
                          c.wqbt[p, :, :, :].rearrange("a p m -> p a m"),
                          c.dW[b2])
                    dwv[b2] = c.dW[b2].n
                    b.wait("tensor", c.dW[b2], dwv[b2])
                    for tt in range(NTT):
                        bk = tt % 2
                        b.wait("tensor", VS, prev_q[bk])
                        for ck in range(NCH):
                            b.call("tensor", "matmul", pb[bk][:, 0:TT],
                                   wq2[:, (b2 * NCH + ck) * P:
                                       (b2 * NCH + ck + 1) * P],
                                   h_loc[:, ck * T + tt * TT:ck * T + (tt + 1) * TT],
                                   start=(ck == 0), stop=(ck == NCH - 1), inc=TS)
                        b.wait("vector", TS, TS.n)
                        vs = b.call("vector", "tensor_scalar",
                                    qp_all[:, p * T + tt * TT:p * T + (tt + 1) * TT],
                                    pb[bk][:, 0:TT],
                                    biases[:, p:p + 1], None, ALU.add, ALU.bypass)
                        prev_q[bk] = vs
                    q_guard[b2] = TS.n

                b.wait("tensor", CC, vg)
                b.wait("tensor", dG, hload_dG)
                for k in range(4):
                    b.wait("sync", c.dPS[k], c.pc_done["wv1t"][k])
                b.barrier()

                prev_kv = {0: 0, 1: 0}
                kv_guard = 0
                c.pv_guard = 0
                c.dcx_prev = 0
                for p in range(NCH):
                    b.wait("sync", TS, kv_guard)
                    b.dma("sync", wk.ap().rearrange("q (a m) -> q a m", a=NCH),
                          c.wkbt[p, :, :, :].rearrange("a p m -> p a m"), c.dW[2])
                    b.dma("sync", wv.ap().rearrange("q (a m) -> q a m", a=NCH),
                          c.wvbt[p, :, :, :].rearrange("a p m -> p a m"), c.dW[3])
                    b.wait("tensor", c.dW[2], c.dW[2].n)
                    b.wait("tensor", c.dW[3], c.dW[3].n)
                    for tg in range(2 * NTT):
                        for j, (dst, wmat, bcol) in enumerate(
                                ((kp, wk, NCH + p), (vp, wv, 2 * NCH + p))):
                            bi = 2 + (tg * 2 + j) % 2
                            b.wait("tensor", VS, prev_kv[bi - 2])
                            for ck in range(NCH):
                                b.call("tensor", "matmul", pb[bi][:, 0:TT],
                                       wmat[:, ck * P:(ck + 1) * P],
                                       h_full[:, ck * 2 * T + tg * TT:
                                              ck * 2 * T + (tg + 1) * TT],
                                       start=(ck == 0), stop=(ck == NCH - 1), inc=TS)
                            b.wait("vector", TS, TS.n)
                            vs = b.call("vector", "tensor_scalar",
                                        dst[:, tg * TT:(tg + 1) * TT],
                                        pb[bi][:, 0:TT],
                                        biases[:, bcol:bcol + 1], None,
                                        ALU.add, ALU.bypass)
                            prev_kv[bi - 2] = vs
                    kv_guard = TS.n
                    b.wait("tensor", VS, VS.n)
                    for kc in range(32):
                        vt = b.call("tensor", "transpose", c.pbt[:, 0:P],
                                    vp[:, kc * P:(kc + 1) * P], c.identbf[:, :],
                                    inc=TS)
                        b.wait("vector", TS, vt)
                        b.call("vector", "tensor_copy",
                               vtm[:, kc * 130:kc * 130 + 64], c.pbt[:, 0:64])
                        vv = b.call("vector", "tensor_copy",
                                    vtm[:, kc * 130 + 65:kc * 130 + 129],
                                    c.pbt[:, 64:128])
                        b.wait("tensor", VS, vv)

                    for qt in range(NTT):
                        qk_T = {}
                        pv_T = {}
                        exp_v = {}
                        for w in range(33):
                            bi = w % 2
                            if w < 32:
                                b.wait("tensor", SS, exp_v.get(w - 2, 0))
                                if w == 0:
                                    b.wait("tensor", VS, c.pv_guard)
                                b.call("tensor", "matmul", pb[2 * bi][:, 0:TT],
                                       kp[0:64, w * P:(w + 1) * P],
                                       qp_all[0:64, p * T + qt * TT:
                                              p * T + (qt + 1) * TT],
                                       start=True, stop=True,
                                       tile_position=(0, 0), inc=TS)
                                b.call("tensor", "matmul", pb[2 * bi + 1][:, 0:TT],
                                       kp[64:128, w * P:(w + 1) * P],
                                       qp_all[64:128, p * T + qt * TT:
                                              p * T + (qt + 1) * TT],
                                       start=True, stop=True,
                                       tile_position=(64, 0), inc=TS)
                                qk_T[w] = TS.n
                            if w > 0:
                                wp = w - 1
                                bip = wp % 2
                                b.wait("tensor", SS, exp_v[wp])
                                b.call("tensor", "matmul", pb[4][0:65, 0:TT],
                                       vtm[:, wp * 130:wp * 130 + 65],
                                       pt[:, 2 * bip * TT:(2 * bip + 1) * TT],
                                       start=(wp == 0), stop=(wp == 31), inc=TS)
                                b.call("tensor", "matmul", pb[5][0:65, 0:TT],
                                       vtm[:, wp * 130 + 65:wp * 130 + 130],
                                       pt[:, (2 * bip + 1) * TT:(2 * bip + 2) * TT],
                                       start=(wp == 0), stop=(wp == 31), inc=TS)
                                pv_T[wp] = TS.n
                            if w < 32:
                                b.wait("scalar", TS, qk_T[w])
                                b.wait("scalar", TS, pv_T.get(w - 2, 0))
                                vs = b.call("scalar", "activation",
                                            pt[:, 2 * bi * TT:(2 * bi + 2) * TT],
                                            c.parena[:, 2 * bi * TT:
                                                     (2 * bi + 2) * TT],
                                            AF.Exp, scale=0.125,
                                            bias=c.smalls[:, c.C_NEG8:
                                                          c.C_NEG8 + 1],
                                            inc=SS)
                                exp_v[w] = vs
                        b.wait("vector", TS, pv_T[31])
                        r0 = b.call("vector", "reciprocal",
                                    rdrow[64:65, 0:TT], pb[4][64:65, 0:TT])
                        b.call("vector", "tensor_copy", ctmp0[0:64, :],
                               pb[4][0:64, 0:TT])
                        cpv = b.call("vector", "tensor_copy", ctmp1[0:64, :],
                                     pb[5][0:64, 0:TT])
                        c.pv_guard = cpv
                        b.wait("tensor", VS, r0)
                        bc0 = b.call("tensor", "matmul", pb[6][0:64, 0:TT],
                                     c.ones_mat[64:65, 0:64],
                                     rdrow[64:65, 0:TT],
                                     start=True, stop=True, inc=TS)
                        b.wait("vector", TS, bc0)
                        m0 = b.call("vector", "tensor_mul",
                                    ctx[0:64, p * T + qt * TT:
                                        p * T + (qt + 1) * TT],
                                    ctmp0[0:64, :], pb[6][0:64, 0:TT])
                        r1 = b.call("vector", "reciprocal",
                                    rdrow[64:65, 0:TT], pb[5][64:65, 0:TT])
                        b.wait("tensor", VS, r1)
                        bc1 = b.call("tensor", "matmul", pb[6][0:64, 0:TT],
                                     c.ones_mat[64:65, 0:64],
                                     rdrow[64:65, 0:TT],
                                     start=True, stop=True, inc=TS)
                        b.wait("vector", TS, bc1)
                        b.wait("vector", c.dCX, c.dcx_prev)
                        m1 = b.call("vector", "tensor_mul", ctmp2[0:64, :],
                                    ctmp1[0:64, :], pb[6][0:64, 0:TT])
                        b.wait("sync", VS, m1)
                        b.dma("sync",
                              ctx[64:128, p * T + qt * TT:p * T + (qt + 1) * TT],
                              ctmp2[0:64, :], c.dCX)
                        c.dcx_prev = c.dCX.n
                    b.barrier()

        with contextlib.ExitStack() as s4:
            sbl = lambda n, sh, dt: s4.enter_context(nc.sbuf_tensor(n, list(sh), dt))
            wo = sbl("p3wo", (P, NCH * D), BF16)
            yf = sbl("p3y", (P, NCH * T), F32)
            bias4 = sbl("p3bo", (P, 3 * NCH), F32)
            stat = sbl("p3stat2", (1, 3 * TT), F32)
            sq = sbl("p3sq2", (P, TT), F32)
            xrel = sbl("p3xrel", (P, TT), F32)
            yst = sbl("p3yst", (P, TT), F32)
            for k in range(4):
                b.wait("sync", c.dPS[k], c.pc_done["wo1t"][k])
            for pp in range(NCH):
                b.dma("sync", wo[:, pp * D:(pp + 1) * D]
                      .rearrange("q (a m) -> q a m", a=NCH),
                      c.wobt[pp, :, :, :].rearrange("a p m -> p a m"), dS)
            b.wait("tensor", c.dCX, c.dcx_prev)
            b.dma("sync", bias4[:, 0:NCH], dr["bo1"].ap(), dS)
            b.dma("sync", bias4[:, NCH:2 * NCH], dr["gam1"].ap(), dS)
            b.dma("sync", bias4[:, 2 * NCH:3 * NCH], dr["bet1"].ap(), dS)
            b.wait("tensor", dS, dS.n)
            b.wait("scalar", dS, dS.n)
            b.wait("vector", dS, dS.n)
            prev_read = {0: 0, 1: 0}
            for oc in range(NCH):
                for tt in range(NTT):
                    bk = (oc * NTT + tt) % 2
                    b.wait("tensor", VS, prev_read[bk])
                    for pp in range(NCH):
                        b.call("tensor", "matmul", pb[bk][:, 0:TT],
                               wo[:, pp * D + oc * P:pp * D + (oc + 1) * P],
                               ctx[:, pp * T + tt * TT:pp * T + (tt + 1) * TT],
                               start=(pp == 0), stop=(pp == NCH - 1), inc=TS)
                    mmT = TS.n
                    b.wait("vector", TS, mmT)
                    vv = b.call("vector", "scalar_tensor_tensor",
                                yf[:, oc * T + tt * TT:oc * T + (tt + 1) * TT],
                                pb[bk][:, 0:TT], bias4[:, oc:oc + 1],
                                h_loc[:, oc * T + tt * TT:oc * T + (tt + 1) * TT],
                                ALU.add, ALU.add)
                    prev_read[bk] = vv
            b.barrier()
            sq2 = sbl("p3sq3", (P, TT), F32)
            trans_end = {0: 0, 1: 0}
            helper_vs = {0: 0, 1: 0}
            for tt in range(NTT):
                par = tt % 2
                pbm, pbr = (pb[2], pb[3]) if par == 0 else (pb[4], pb[5])
                srcs = [yf[:, ck * T + tt * TT:ck * T + (tt + 1) * TT]
                        for ck in range(NCH)]
                b.wait("tensor", VS, helper_vs[1 - par])
                b.wait("tensor", VS, trans_end[par])
                vt = ln_stats_bcast(c, srcs, D, pbm, pbr, stat, sq2)
                helper_vs[par] = VS.n
                b.wait("vector", TS, vt)
                for ck in range(NCH):
                    b.dma("sync", xrel[:, :],
                          x_in[ck * P:(ck + 1) * P, tt * TT:(tt + 1) * TT], c.dS2)
                    b.call("vector", "tensor_sub", sq[:, :],
                           yf[:, ck * T + tt * TT:ck * T + (tt + 1) * TT],
                           pbm[:, 0:TT])
                    b.call("vector", "tensor_mul", sq[:, :], sq[:, :],
                           pbr[:, 0:TT])
                    b.call("vector", "tensor_scalar", sq[:, :], sq[:, :],
                           bias4[:, NCH + ck:NCH + ck + 1],
                           bias4[:, 2 * NCH + ck:2 * NCH + ck + 1],
                           ALU.mult, ALU.add)
                    b.wait("vector", c.dS2, c.dS2.n)
                    vv = b.call("vector", "tensor_add", yst[:, :], sq[:, :],
                                xrel[:, :])
                    b.wait("sync", VS, vv)
                    b.dma("sync",
                          x_out[ck * P:(ck + 1) * P, tt * TT:(tt + 1) * TT],
                          yst[:, :], c.dT)
                    b.wait("vector", c.dT, c.dT.n)
                    trans_end[par] = vv
            b.barrier()

    b.barrier()
    c.x_cur = 0


def emit_phase4(c):
    nc, b, dr = c.nc, c.b, c.dr
    dS, dT, TS, VS, SS = c.dS, c.dT, c.TS, c.VS, c.SS
    pb = c.pb
    x_in = c.x_hbm[0]
    RW = 32 * TT

    with contextlib.ExitStack() as ph:
        sbf = lambda n, sh, dt: ph.enter_context(nc.sbuf_tensor(n, list(sh), dt))
        xln = sbf("p4xln", (P, NCH * T), BF16)
        ha = sbf("p4ha", (P, 4 * RW), BF16)
        scal = sbf("p4scal", (P, 3 * 32 + NCH), F32)

        b.dma("sync", scal[:, 0:32], dr["s1c"].ap(), dS)
        b.dma("sync", scal[:, 32:64], dr["s2c"].ap(), dS)
        b.dma("sync", scal[:, 64:96], dr["sh1c"].ap(), dS)
        b.dma("sync", scal[:, 96:96 + NCH], dr["s3c"].ap(), dS)
        b.wait("sync", dS, dS.n)
        b.wait("scalar", dS, dS.n)
        b.wait("vector", dS, dS.n)

        with contextlib.ExitStack() as sLn:
            sbn = lambda n, sh, dt: sLn.enter_context(nc.sbuf_tensor(n, list(sh), dt))
            stat = sbn("p4stat", (1, 3 * TT), F32)
            sq = sbn("p4sq", (P, TT), F32)
            xt = sbn("p4xt", (P, NCH * TT), F32)
            emit_ln_to(c, x_in, xln, stat, sq, xt, 0)

        hsl = [sbf(f"p4hsl{i}", (P, 32 * P), BF16) for i in range(4)]

        regions = [(xln, 0), (ha, 0), (ha, RW), (ha, 2 * RW), (ha, 3 * RW)]

        assert c.pc_state["si"] == len(c.pc_jobs), "pre-cast incomplete"
        pc_done = c.pc_done

        g_mm = {i: 0 for i in range(4)}
        g_ps_s = {0: 0, 1: 0}
        g_ps_v = {0: 0, 1: 0}
        dg_val = {i: 0 for i in range(4)}

        def prefA(ic):
            bi = ic % 4
            sem = c.dW[bi]
            b.wait("sync", TS, g_mm[bi])
            b.dma("sync", hsl[bi][:, 0:NCH * P].rearrange("q (a m) -> q a m", a=NCH),
                  c.w1bt[ic, :, :, :].rearrange("a p m -> p a m"), sem)
            b.dma("sync", hsl[bi][:, NCH * P:2 * NCH * P]
                  .rearrange("q (a m) -> q a m", a=NCH),
                  c.w2bt[ic, :, :, :].rearrange("a p m -> p a m"), sem)
            dg_val[bi] = sem.n

        with contextlib.ExitStack() as sA:
            t1 = sA.enter_context(nc.sbuf_tensor("p4t1", [P, 2 * TT], F32))
            for k in range(4):
                b.wait("sync", c.dPS[k], pc_done["w2Tt"][k])
            for ic in range(4):
                prefA(ic)
            for ic in range(32):
                bi = ic % 4
                b.wait("tensor", c.dW[bi], dg_val[bi])
                for tt in range(NTT):
                    gi = ic * NTT + tt
                    b2 = gi % 2
                    pa, pbk = pb[2 * b2], pb[2 * b2 + 1]
                    rten, rbase = regions[1 + tt]
                    b.wait("tensor", SS, g_ps_s[b2])
                    b.wait("tensor", VS, g_ps_v[b2])
                    for ck in range(NCH):
                        b.call("tensor", "matmul", pa[:, 0:TT],
                               hsl[bi][:, ck * P:(ck + 1) * P],
                               xln[:, ck * T + tt * TT:ck * T + (tt + 1) * TT],
                               start=(ck == 0), stop=(ck == NCH - 1), inc=TS)
                    mmA = TS.n
                    for ck in range(NCH):
                        b.call("tensor", "matmul", pbk[:, 0:TT],
                               hsl[bi][:, (NCH + ck) * P:(NCH + ck + 1) * P],
                               xln[:, ck * T + tt * TT:ck * T + (tt + 1) * TT],
                               start=(ck == 0), stop=(ck == NCH - 1), inc=TS)
                    mmB = TS.n
                    b.wait("scalar", TS, mmA)
                    b.wait("scalar", VS, g_ps_v[b2])
                    vs = b.call("scalar", "activation",
                                t1[:, b2 * TT:(b2 + 1) * TT], pa[:, 0:TT],
                                AF.Silu, scale=scal[:, ic:ic + 1], inc=SS)
                    g_ps_s[b2] = vs
                    b.wait("vector", TS, mmB)
                    b.wait("vector", SS, vs)
                    vv = b.call("vector", "scalar_tensor_tensor",
                                rten[:, rbase + ic * TT:rbase + (ic + 1) * TT],
                                pbk[:, 0:TT], scal[:, 32 + ic:32 + ic + 1],
                                t1[:, b2 * TT:(b2 + 1) * TT], ALU.mult, ALU.mult)
                    g_ps_v[b2] = vv
                g_mm[bi] = TS.n
                if ic + 4 < 32:
                    prefA(ic + 4)
        b.barrier()

        g_mm = {i: 0 for i in range(4)}
        g_ps_s = {i: 0 for i in range(4)}
        g_ps_v = {i: 0 for i in range(4)}
        dw_val = {i: 0 for i in range(4)}

        def mk_pref(wsrc, n_oc):
            def pref(i):
                oc = i % n_oc
                bi = i % 4
                b.wait("sync", TS, g_mm[bi])
                b.dma("sync", hsl[bi].ap().rearrange("q (a m) -> q a m", a=32),
                      wsrc[oc, :, :, :].rearrange("a p m -> p a m"), c.dW[bi])
                dw_val[bi] = c.dW[bi].n
            return pref

        prev = regions[1:5]
        for stage, wsrc, pcn, scol in (("B", c.h1bt, "h1Tt", None),
                                       ("C", c.h2bt, "h2Tt", "sh2c")):
            if scol is not None:
                b.wait("sync", SS, SS.n)
                b.dma("sync", scal[:, 64:96], dr[scol].ap(), dS)
                b.wait("sync", dS, dS.n)
                b.wait("scalar", dS, dS.n)
            cur = ([regions[0], regions[1], regions[2], regions[3]] if stage == "B"
                   else [regions[4], regions[0], regions[1], regions[2]])
            for k in range(4):
                b.wait("sync", c.dPS[k], pc_done[pcn][k])
            pref = mk_pref(wsrc, 32)
            for i in range(4):
                pref(i)
            for tt in range(NTT):
                sten, sbase = prev[tt]
                rten, rbase = cur[tt]
                for oc in range(32):
                    i = tt * 32 + oc
                    bi = i % 4
                    b.wait("tensor", c.dW[bi], dw_val[bi])
                    b.wait("tensor", SS, g_ps_s[bi])
                    b.wait("tensor", VS, g_ps_v[bi])
                    for ick in range(32):
                        b.call("tensor", "matmul", pb[bi][:, 0:TT],
                               hsl[bi][:, ick * P:(ick + 1) * P],
                               sten[:, sbase + ick * TT:sbase + (ick + 1) * TT],
                               start=(ick == 0), stop=(ick == 31), inc=TS)
                    g_mm[bi] = TS.n
                    if i + 4 < 128:
                        pref(i + 4)
                    b.wait("scalar", TS, g_mm[bi])
                    vs = b.call("scalar", "activation",
                                rten[:, rbase + oc * TT:rbase + (oc + 1) * TT],
                                pb[bi][:, 0:TT], AF.Gelu,
                                scale=scal[:, 64 + oc:64 + oc + 1], inc=SS)
                    g_ps_s[bi] = vs
            prev = cur

        with contextlib.ExitStack() as sD:
            xrel = sD.enter_context(nc.sbuf_tensor("p4xrel", [P, TT], F32))
            yst = sD.enter_context(nc.sbuf_tensor("p4yst", [P, TT], F32))
            for k in range(4):
                b.wait("sync", c.dPS[k], pc_done["w3Tt"][k])
            pref = mk_pref(c.w3bt, NCH)
            prev_stt = 0
            for i in range(4):
                pref(i)
            for tt in range(NTT):
                sten, sbase = prev[tt]
                for oc2 in range(NCH):
                    i = tt * NCH + oc2
                    bi = i % 4
                    b.wait("tensor", c.dW[bi], dw_val[bi])
                    b.wait("tensor", SS, g_ps_s[bi])
                    b.wait("tensor", VS, g_ps_v[bi])
                    for ick in range(32):
                        b.call("tensor", "matmul", pb[bi][:, 0:TT],
                               hsl[bi][:, ick * P:(ick + 1) * P],
                               sten[:, sbase + ick * TT:sbase + (ick + 1) * TT],
                               start=(ick == 0), stop=(ick == 31), inc=TS)
                    g_mm[bi] = TS.n
                    if i + 4 < 32:
                        pref(i + 4)
                    b.wait("sync", VS, prev_stt)
                    b.dma("sync", xrel[:, :],
                          x_in[oc2 * P:(oc2 + 1) * P, tt * TT:(tt + 1) * TT],
                          c.dS2)
                    b.wait("vector", c.dS2, c.dS2.n)
                    b.wait("vector", TS, g_mm[bi])
                    vv = b.call("vector", "scalar_tensor_tensor", yst[:, :],
                                pb[bi][:, 0:TT],
                                scal[:, 96 + oc2:96 + oc2 + 1], xrel[:, :],
                                ALU.mult, ALU.add)
                    g_ps_v[bi] = vv
                    prev_stt = vv
                    b.wait("sync", VS, vv)
                    b.dma("sync",
                          c.y_out[oc2 * P:(oc2 + 1) * P, tt * TT:(tt + 1) * TT],
                          yst[:, :], dT)
                    b.wait("vector", dT, dT.n)
            b.barrier()

    b.barrier()
    c.x_cur = -1


def emit_finish(c):
    if PHASES >= 4:
        return
    b = c.b
    src = c.x_hbm[c.x_cur] if hasattr(c, "x_cur") else c.dr["x_in"]
    with contextlib.ExitStack() as ph:
        t = ph.enter_context(c.nc.sbuf_tensor("ycopy", [P, TT], F32))
        for ck in range(NCH):
            for q4 in range(NTT):
                b.dma("sync", t[:, :],
                      src[ck * P:(ck + 1) * P, q4 * TT:(q4 + 1) * TT], c.dS)
                b.wait("sync", c.dS, c.dS.n)
                b.dma("sync", c.y_out[ck * P:(ck + 1) * P, q4 * TT:(q4 + 1) * TT],
                      t[:, :], c.dT)
                b.wait("sync", c.dT, c.dT.n)
        b.barrier()


_NC_CACHE = [None]


def kernel(**inputs):
    maps = prep_inputs(inputs)
    if _NC_CACHE[0] is None:
        _NC_CACHE[0] = build_nc()
    nc = _NC_CACHE[0]
    res = run_bass_kernel_spmd(nc, maps, core_ids=list(range(8)))
    out = np.empty((4, SEQ, D), np.float32)
    for cc in range(8):
        bi, half = divmod(cc, 2)
        y = np.asarray(res.results[cc]["y"])
        out[bi, half * T:(half + 1) * T, :] = y.T
    return out
